# version 14
# baseline (speedup 1.0000x reference)
import contextlib

import numpy as np

import concourse.bass as bass
import concourse.mybir as mybir
from concourse import library_config
from concourse import library_overlay
from concourse.bass_utils import run_bass_kernel_spmd

F32 = mybir.dt.float32
BF16 = mybir.dt.bfloat16
I16 = mybir.dt.int16
I32 = mybir.dt.int32
U32 = mybir.dt.uint32
AF = mybir.ActivationFunctionType
OP = mybir.AluOpType
AX = mybir.AxisListType

D = 1024
NS = 8
H = 4
DK = DV = 256
CH = 32
SEG = 256
NCHS = SEG // CH



def pack_mm(W, mo_major=False):
    out_d, in_d = W.shape
    nki, nmo = in_d // 128, out_d // 128
    g = np.ascontiguousarray(np.asarray(W).T).reshape(nki, 128, nmo, 128)
    g = g.transpose(1, 2, 0, 3) if mo_major else g.transpose(1, 0, 2, 3)
    return np.ascontiguousarray(g.reshape(128, nki * nmo * 128))


def pack_thin(W):
    out_d, in_d = W.shape
    nki = in_d // 128
    g = np.ascontiguousarray(np.asarray(W).T).reshape(nki, 128, out_d)
    return np.ascontiguousarray(g.transpose(1, 0, 2).reshape(128, nki * out_d))


def pack_chan(w):
    w = np.asarray(w)
    Kk = w.shape[-1]
    return np.ascontiguousarray(
        w.reshape(NS, 128, Kk).transpose(1, 0, 2).reshape(128, NS * Kk))


def host_consts():
    i = np.arange(128)
    ident = (i[:, None] == i[None, :]).astype(np.float32)
    within = (i[:, None] // 32) == (i[None, :] // 32)
    sl = (((i[:, None] % 32) > (i[None, :] % 32)) & within).astype(np.float32)
    tu = (((i[:, None] % 32) <= (i[None, :] % 32)) & within).astype(np.float32)
    z = np.zeros((128, 128), np.float32)
    return np.concatenate([ident, -sl, tu, z], axis=1)


def host_rowsel(nrows):
    m = np.zeros((128, nrows * nrows), np.float32)
    for r in range(nrows):
        m[:, r * nrows + r] = 1.0
    return m


def host_shv16():
    m = np.zeros((128, 16), np.float32)
    for blk in range(4):
        for bi in range(4):
            m[32 * blk + 4 + bi, 8 + bi] = 1.0
    return m


def _l1_gcb(cb):
    z = np.zeros((128, 1), np.float32)
    for h in range(4):
        z[32 * h + 3, 0] = cb[h]
    return z


def _l1_gw1s(W1):
    s = np.ascontiguousarray(np.asarray(W1)[:, 1024:1040].T)
    perm = np.zeros(16, np.int64)
    for bi in range(4):
        perm[bi] = bi * 4 + 3
        perm[4 + bi] = bi * 4
        perm[8 + bi] = bi * 4 + 1
        perm[12 + bi] = bi * 4 + 2
    sp = s[perm]
    z = np.zeros((128, 1024), np.float32)
    for b in range(4):
        z[32 * b:32 * b + 16] = sp
    return z


def pack_rhs(W):
    W = np.asarray(W)
    out_d = W.shape[1]
    return np.ascontiguousarray(
        W.reshape(8, 128, out_d).transpose(1, 0, 2).reshape(128, 8 * out_d))


def host_moe_consts():
    i = np.arange(128)
    su16 = (((i[:, None] // 16) == (i[None, :] // 16))
            & ((i[:, None] % 16) < (i[None, :] % 16))).astype(np.float32)
    esum8 = np.zeros((128, 8), np.float32)
    for e in range(8):
        esum8[e * 16:(e + 1) * 16, e] = 1.0
    coll = np.zeros((128, 16 * 4 * 32), np.float32)
    for a in range(16):
        for g in range(4):
            for e in range(32):
                if e // 8 == g:
                    coll[(e % 8) * 16 + a, (a * 4 + g) * 32 + e] = 1.0
    return np.concatenate([su16, esum8, coll], axis=1)


def host_moe_exp8():
    m = np.zeros((8, 128), np.float32)
    for r in range(8):
        m[r, r * 16:(r + 1) * 16] = 1.0
    return m


def host_rowrep(nrows):
    m = np.zeros((nrows, nrows * 128), np.float32)
    for r in range(nrows):
        m[r, r * 128:(r + 1) * 128] = 1.0
    return m



class Ctr:
    def __init__(self, stack, nc, name):
        self.sem = stack.enter_context(nc.semaphore(name))
        self.v = 0

    def inc(self, inst, n=1):
        inst.then_inc(self.sem, n)
        self.v += n
        return inst


class AutoEng:

    NO_SELF_WAIT = False
    _PASS = {"wait_ge", "dma_start", "indirect_dma_start", "load_library",
             "dma_start_transpose"}

    def __init__(self, eng, ctr, k=None):
        self._eng = eng
        self._ctr = ctr
        self._k = k

    def __getattr__(self, name):
        f = getattr(self._eng, name)
        if name.startswith("_") or name in self._PASS or not callable(f):
            return f

        def g(*a, **kw):
            if self._k is not None:
                self._k.hb_tick()
            if not AutoEng.NO_SELF_WAIT:
                self._eng.wait_ge(self._ctr.sem, self._ctr.v)
            return f(*a, **kw)
        return g


class K:
    def __init__(self, nc, stack, T, NC):
        self.nc = nc
        self.stack = stack
        self.T = T
        self.NC = NC
        self.pe = Ctr(stack, nc, "c_pe")
        self.dve = Ctr(stack, nc, "c_dve")
        self.act = Ctr(stack, nc, "c_act")
        self.pool = Ctr(stack, nc, "c_pool")
        self.dma = Ctr(stack, nc, "d_sp")
        self.dmag = Ctr(stack, nc, "d_pl")
        self.str_e = Ctr(stack, nc, "d_se")
        self.str_o = Ctr(stack, nc, "d_so")
        self.gat = [Ctr(stack, nc, "d_g%d" % i) for i in range(4)]
        self.eo_e = Ctr(stack, nc, "d_ee")
        self.eo_o = Ctr(stack, nc, "d_eo")
        self.wcast = Ctr(stack, nc, "d_wc")
        self.we_e = Ctr(stack, nc, "d_we")
        self.we_o = Ctr(stack, nc, "d_wo")
        self.cc = Ctr(stack, nc, "c_cc")
        self._consts = {}
        self.V = AutoEng(nc.vector, self.dve, self)
        self.A = AutoEng(nc.scalar, self.act, self)
        self.G = AutoEng(nc.gpsimd, self.pool)
        self.hb_cfg = None
        self._hbn = 0

    def hb_tick(self):
        if self.hb_cfg is None:
            return
        self._hbn += 1
        if self._hbn % 3:
            return
        self.nc.tensor.ldweights(self.identb[:, 0:4])

    def sb(self, name, shape, dtype, scope=None):
        st = scope if scope is not None else self.stack
        return st.enter_context(self.nc.sbuf_tensor(name, shape, dtype))

    def cst(self, value):
        if value not in self._consts:
            j = len(self._consts)
            assert j < self._consts_tensor.shape[1]
            self.dve.inc(self.V.memset(
                self._consts_tensor[:, j:j + 1], float(value)))
            self._consts[value] = j
        j = self._consts[value]
        return self._consts_tensor[:, j:j + 1]

    def w(self, eng, *ctrs):
        for c in ctrs:
            eng.wait_ge(c.sem, c.v)

    def wall(self, *ctrs):
        nc = self.nc
        for eng in (nc.tensor, nc.vector, nc.scalar, nc.gpsimd, nc.sync):
            for c in ctrs:
                eng.wait_ge(c.sem, c.v)

    def evac(self, out_ap, ps_ap, eng="dve"):
        nc = self.nc
        if eng == "act":
            self.w(nc.scalar, self.pe)
            self.act.inc(self.A.copy(out_ap, ps_ap))
        else:
            self.w(nc.vector, self.pe)
            self.dve.inc(self.V.tensor_copy(out_ap, ps_ap))

    def rsqrt(self, ap, eps):
        nc = self.nc
        self.w(nc.scalar, self.dve)
        self.act.inc(self.A.activation(ap, ap, AF.Ln,
                                       bias=self.cst(eps)[0:ap.shape[0]]))
        self.w(nc.scalar, self.act)
        self.act.inc(self.A.activation(ap, ap, AF.Exp, scale=-0.5))

    def hload(self, dst_sb, src_ap):
        self.dmag.inc(self.nc.gpsimd.dma_start(out=dst_sb, in_=src_ap), 16)


def xr_slab_ap(xr_hbm, t0, w):
    return xr_hbm.rearrange("(s p) t -> p s t", p=128)[:, :, t0:t0 + w]



def emit_ln(k, src_slabs, W, out_sb=None, out_off=0, out_w=None, eps=1e-5,
            res_slabs=None, src3=None, out3=None, res3=None):
    nc = k.nc
    if out_w is None:
        out_w = W
    if src3 is not None:
        sq3 = k.sq[:].rearrange("p (s t) -> p s t", s=NS)[:, :, 0:W]
        k.w(nc.scalar, k.pe, k.dve)
        k.act.inc(k.A.activation(sq3, src3, AF.Square))
        k.w(nc.tensor, k.act, k.dve)
        for s in range(NS):
            ones = k.ones128f if src3.dtype == F32 else k.ones128
            nc.tensor.matmul(k.w1[:, 0:W], ones[:], src3[:, s, :],
                             start=(s == 0), stop=(s == NS - 1))
            mm = nc.tensor.matmul(k.w2[:, 0:W], k.ones128[:], sq3[:, s, :],
                                  start=(s == 0), stop=(s == NS - 1))
        k.pe.inc(mm)
        r = k.scr[:, 0:W]
        m = k.scr2[:, 0:W]
        k.w(nc.vector, k.pe)
        k.dve.inc(k.V.tensor_scalar(m, k.w1[:, 0:W], 1.0 / D, None,
                                    OP.mult))
        k.dve.inc(k.V.tensor_scalar(r, k.w2[:, 0:W], 1.0 / D, None,
                                    OP.mult))
        k.w(nc.vector, k.dve)
        k.dve.inc(k.V.tensor_tensor(k.scr3[:, 0:W], m, m, OP.mult))
        k.w(nc.vector, k.dve)
        k.dve.inc(k.V.tensor_tensor(r, r, k.scr3[:, 0:W], OP.subtract))
        k.rsqrt(r, eps)
        k.w(nc.vector, k.act)
        k.dve.inc(k.V.scalar_tensor_tensor(m, m, -1.0, r, OP.mult,
                                           OP.mult))
        rb = r.unsqueeze(1).broadcast_to([128, NS, W])
        mb = m.unsqueeze(1).broadcast_to([128, NS, W])
        w3 = k.wide[:].rearrange("p (s t) -> p s t", s=NS)[:, :, 0:W]
        k.w(nc.vector, k.dve)
        k.dve.inc(k.V.tensor_tensor(w3, src3, rb, OP.mult))
        k.w(nc.vector, k.dve)
        if res3 is None:
            k.dve.inc(k.V.tensor_tensor(out3, w3, mb, OP.add))
        else:
            k.dve.inc(k.V.tensor_tensor(w3, w3, mb, OP.add))
            k.w(nc.vector, k.dve)
            k.dve.inc(k.V.tensor_tensor(res3, res3, w3, OP.add))
            if out3 is not None:
                k.w(nc.scalar, k.dve)
                k.act.inc(k.A.copy(out3, w3))
        k.w(nc.vector, k.dve)
        k.w(nc.scalar, k.dve)
        k.w(nc.tensor, k.dve)
        return
    k.w(nc.scalar, k.pe, k.dve)
    for s in range(NS):
        k.act.inc(k.A.activation(k.sq[:, s * W:(s + 1) * W],
                                       src_slabs[s], AF.Square))
    k.w(nc.tensor, k.act, k.dve)
    for s in range(NS):
        ones = k.ones128f if src_slabs[s].dtype == F32 else k.ones128
        nc.tensor.matmul(k.w1[:, 0:W], ones[:], src_slabs[s],
                         start=(s == 0), stop=(s == NS - 1))
        mm = nc.tensor.matmul(k.w2[:, 0:W], k.ones128[:],
                              k.sq[:, s * W:(s + 1) * W],
                              start=(s == 0), stop=(s == NS - 1))
    k.pe.inc(mm)
    r = k.scr[:, 0:W]
    m = k.scr2[:, 0:W]
    k.w(nc.vector, k.pe)
    k.dve.inc(k.V.tensor_scalar(m, k.w1[:, 0:W], 1.0 / D, None, OP.mult))
    k.dve.inc(k.V.tensor_scalar(r, k.w2[:, 0:W], 1.0 / D, None, OP.mult))
    k.w(nc.vector, k.dve)
    k.dve.inc(k.V.tensor_tensor(k.scr3[:, 0:W], m, m, OP.mult))
    k.w(nc.vector, k.dve)
    k.dve.inc(k.V.tensor_tensor(r, r, k.scr3[:, 0:W], OP.subtract))
    k.rsqrt(r, eps)
    k.w(nc.vector, k.act)
    k.dve.inc(k.V.scalar_tensor_tensor(m, m, -1.0, r, OP.mult, OP.mult))
    k.w(nc.vector, k.dve)
    for s in range(NS):
        k.dve.inc(k.V.tensor_tensor(k.scr3[:, 0:W], src_slabs[s], r,
                                          OP.mult))
        k.w(nc.vector, k.dve)
        if res_slabs is None:
            k.dve.inc(k.V.tensor_tensor(
                out_sb[:, out_off + s * out_w:out_off + s * out_w + W],
                k.scr3[:, 0:W], m, OP.add))
        else:
            k.dve.inc(k.V.tensor_tensor(
                k.scr3[:, 0:W], k.scr3[:, 0:W], m, OP.add))
            k.w(nc.vector, k.dve)
            k.dve.inc(k.V.tensor_tensor(res_slabs[s], res_slabs[s],
                                              k.scr3[:, 0:W], OP.add))
            if out_sb is not None:
                k.w(nc.scalar, k.dve)
                k.act.inc(k.A.copy(
                    out_sb[:, out_off + s * out_w:out_off + s * out_w + W],
                    k.scr3[:, 0:W]))
        k.w(nc.vector, k.dve)
    k.w(nc.scalar, k.dve)
    k.w(nc.tensor, k.dve)



def br_ap(br, nbr, bi, vh, cc, width):
    v = br[:].rearrange("p (b h g t) -> p b h g t", b=nbr, h=H, g=2)
    return v[:, bi, :, vh, cc:cc + width]


def emit_attn(k, var, xr_hbm, wts, dbg_write, gw1x_hbm=None):
    nc = k.nc
    T = k.T
    NSEG = T // SEG
    l2eps = 1e-12 if var == 0 else 1e-6
    FIRK = (3, 7, 15, 31) if var == 0 else (5, 64)
    FIRMAX = max(FIRK)
    PAD = FIRMAX - 1
    nbr = 6 if var == 0 else 4
    dbi = 5 if var == 0 else 2
    vbi = 4 if var == 0 else 3
    nlg = 24 if var == 0 else 16
    nmo_h1 = 16 if var == 0 else 8

    sc = contextlib.ExitStack()
    p = "a%d_" % var
    with sc:
        xr_seg = k.sb(p + "xr", [128, NS * SEG], F32, sc)
        ln_seg = k.sb(p + "ln", [128, NS * SEG], BF16, sc)
        pp = k.sb(p + "pp", [128, NS * (SEG + 3)], BF16, sc)
        pp1 = k.sb(p + "pp1", [128, NS * (SEG + 3)], BF16, sc)
        tails = {nm: k.sb(p + "tl" + nm, [128, NS * 3], BF16, sc)
                 for nm in "qkv"}
        qn = k.sb(p + "qn", [128, NS * SEG], BF16, sc)
        kn = k.sb(p + "kn", [128, NS * SEG], BF16, sc)
        vc = k.sb(p + "vc", [128, NS * (SEG + PAD)], BF16, sc)
        beta = k.sb(p + "beta", [H, SEG], F32, sc)
        beta_b = k.sb(p + "betab", [H, SEG], BF16, sc)
        bbc = k.sb(p + "bbc", [128, H * SEG], BF16, sc)
        kb = k.sb(p + "kb", [128, NS * SEG], BF16, sc)
        vb = k.sb(p + "vb", [128, NS * SEG], BF16, sc)
        S_sb = k.sb(p + "S_sb", [128, 2 * 1024], BF16, sc)
        br = k.sb(p + "br", [128, nbr * NS * SEG], BF16, sc)
        gw1db = (k.sb(p + "gw1db", [128, 2 * 1024], BF16, sc)
                 if gw1x_hbm is not None else None)

        idn = k.identb
        negsl = k.negslb
        triu = k.triub

        k.hb_cfg = True
        k.w(nc.vector, k.pe, k.act)
        k.dve.inc(k.V.memset(S_sb[:], 0.0))

        for sg in range(NSEG):
            t0 = sg * SEG
            glob0 = (sg == 0)
            globlast = (sg == NSEG - 1)

            k.w(nc.sync, k.dve, k.act, k.pe, k.dma)
            nc.sync.dma_start(
                out=xr_seg[:].rearrange("p (s t) -> p s t", s=NS),
                in_=xr_slab_ap(xr_hbm, t0, SEG)).then_inc(k.dma.sem, 16)
            k.dma.v += 16
            k.wall(k.dma)

            emit_ln(k, None, SEG, eps=1e-5,
                    src3=xr_seg[:].rearrange("p (s t) -> p s t", s=NS),
                    out3=ln_seg[:].rearrange("p (s t) -> p s t", s=NS))
            if k.asub <= 1:
                continue

            tens = (("q", qn, "wq", "cq", SEG, 0, True),
                    ("k", kn, "wk", "ck", SEG, 0, True),
                    ("v", vc, "wv", "cv", SEG + PAD, PAD, False))

            def proj_stage(ti):
                tnm, dstb, wname, wcn, pitch, doff, donorm = tens[ti]
                ppb = pp if ti % 2 == 0 else pp1
                wt = wts[wname]
                k.w(nc.vector, k.act)
                for s in range(NS):
                    if sg == 0:
                        k.dve.inc(k.V.memset(
                            ppb[:, s * (SEG + 3):s * (SEG + 3) + 3], 0.0))
                    else:
                        k.dve.inc(k.V.tensor_copy(
                            ppb[:, s * (SEG + 3):s * (SEG + 3) + 3],
                            tails[tnm][:, s * 3:(s + 1) * 3]))
                pend = None
                for mo in range(NS):
                    bank = k.w1 if mo % 2 == 0 else k.w2
                    k.w(nc.tensor, k.dve, k.act)
                    for ki in range(NS):
                        mm = nc.tensor.matmul(
                            bank[:, 0:SEG],
                            wt[:, (ki * NS + mo) * 128:(ki * NS + mo) * 128 + 128],
                            ln_seg[:, ki * SEG:(ki + 1) * SEG],
                            start=(ki == 0), stop=(ki == NS - 1))
                    k.pe.inc(mm)
                    if pend is not None:
                        k.evac(*pend, eng="act" if mo % 2 == 0 else "dve")
                    pend = (ppb[:, mo * (SEG + 3) + 3:
                                mo * (SEG + 3) + 3 + SEG], bank[:, 0:SEG])
                k.evac(*pend, eng="act")
                k.w(nc.vector, k.act)
                for s in range(NS):
                    k.dve.inc(k.V.tensor_copy(
                        tails[tnm][:, s * 3:(s + 1) * 3],
                        ppb[:, s * (SEG + 3) + SEG:s * (SEG + 3) + SEG + 3]))

            def post_stage(ti):
                tnm, dstb, wname, wcn, pitch, doff, donorm = tens[ti]
                ppb = pp if ti % 2 == 0 else pp1
                wconv = wts[wcn]
                for s in range(NS):
                    sb_in = ppb[:, s * (SEG + 3):(s + 1) * (SEG + 3)]
                    k.w(nc.vector, k.act)
                    k.dve.inc(k.V.tensor_scalar(
                        k.scr[:, 0:SEG], sb_in[:, 0:SEG],
                        wconv[:, s * 4:s * 4 + 1], None, OP.mult))
                    for j in range(1, 4):
                        k.dve.inc(k.V.scalar_tensor_tensor(
                            k.scr[:, 0:SEG], sb_in[:, j:j + SEG],
                            wconv[:, s * 4 + j:s * 4 + j + 1],
                            k.scr[:, 0:SEG], OP.mult, OP.add))
                    k.w(nc.scalar, k.dve)
                    k.act.inc(k.A.activation(
                        dstb[:, s * pitch + doff:s * pitch + doff + SEG],
                        k.scr[:, 0:SEG], AF.Silu))
                if donorm:
                    k.w(nc.scalar, k.act, k.dve)
                    k.act.inc(k.A.activation(k.sq[:, 0:NS * SEG],
                                             dstb[:, 0:NS * SEG],
                                             AF.Square))
                    k.w(nc.tensor, k.act, k.dve)
                    for h in range(H):
                        bank = k.w1 if h < 2 else k.w2
                        reg = bank[:, (h % 2) * SEG:(h % 2 + 1) * SEG]
                        nc.tensor.matmul(
                            reg, k.ones128[:],
                            k.sq[:, 2 * h * SEG:(2 * h + 1) * SEG],
                            start=True, stop=False)
                        mm = nc.tensor.matmul(
                            reg, k.ones128[:],
                            k.sq[:, (2 * h + 1) * SEG:(2 * h + 2) * SEG],
                            start=False, stop=True)
                    k.pe.inc(mm)
                    k.w(nc.vector, k.pe)
                    k.dve.inc(k.V.tensor_copy(k.wide2[:, 0:512],
                                              k.w1[:, 0:512]))
                    k.dve.inc(k.V.tensor_copy(k.wide2[:, 512:1024],
                                              k.w2[:, 0:512]))
                    k.rsqrt(k.wide2[:, 0:1024], l2eps)
                    l2b = k.wide2[:, 0:1024] \
                        .rearrange("p (h t) -> p h t", h=H) \
                        .unsqueeze(2).broadcast_to([128, H, 2, SEG])
                    d4 = dstb[:, 0:NS * SEG] \
                        .rearrange("p (h two t) -> p h two t", h=H, two=2)
                    k.w(nc.vector, k.act)
                    k.dve.inc(k.V.tensor_tensor(d4, d4, l2b, OP.mult))
                    k.w(nc.vector, k.dve)

            proj_stage(0)
            proj_stage(1)
            post_stage(0)
            proj_stage(2)
            post_stage(1)
            post_stage(2)
            k.w(nc.tensor, k.dve, k.act)
            for ki in range(NS):
                mm = nc.tensor.matmul(k.w1[0:H, 0:SEG],
                                      wts["wb"][:, ki * H:(ki + 1) * H],
                                      ln_seg[:, ki * SEG:(ki + 1) * SEG],
                                      start=(ki == 0), stop=(ki == NS - 1))
            k.pe.inc(mm)
            k.w(nc.scalar, k.pe)
            k.act.inc(k.A.activation(beta[:, 0:SEG], k.w1[0:H, 0:SEG],
                                           AF.Sigmoid))
            if sg == 0:
                for s in range(NS):
                    k.dve.inc(k.V.memset(
                        vc[:, s * (SEG + PAD):s * (SEG + PAD) + PAD], 0.0))
            k.w(nc.tensor, k.act)
            if k.asub <= 2:
                continue

            k.w(nc.vector, k.act)
            k.dve.inc(k.V.tensor_copy(beta_b[:, 0:SEG],
                                            beta[:, 0:SEG]))
            k.w(nc.tensor, k.dve)
            for h in range(H):
                k.pe.inc(nc.tensor.matmul(
                    k.w1[:, 0:SEG], k.rowrepb[0:H, h * 128:(h + 1) * 128],
                    beta_b[:, 0:SEG], start=True, stop=True))
                k.w(nc.vector, k.pe)
                k.dve.inc(k.V.tensor_copy(bbc[:, h * SEG:(h + 1) * SEG],
                                                k.w1[:, 0:SEG]))
                k.w(nc.tensor, k.dve)
            bbc4 = bbc[:].rearrange("p (h t) -> p h t", h=H) \
                .unsqueeze(2).broadcast_to([128, H, 2, SEG])
            kb4 = kb[:].rearrange("p (h two t) -> p h two t", h=H, two=2)
            kn4 = kn[:].rearrange("p (h two t) -> p h two t", h=H, two=2)
            vb4 = vb[:].rearrange("p (h two t) -> p h two t", h=H, two=2)
            vc4 = vc[:].rearrange("p (h two t) -> p h two t", h=H,
                                  two=2)[:, :, :, PAD:PAD + SEG]
            k.dve.inc(k.V.tensor_tensor(kb4, kn4, bbc4, OP.mult))
            k.w(nc.vector, k.dve)
            k.dve.inc(k.V.tensor_tensor(vb4, vc4, bbc4, OP.mult))
            k.w(nc.tensor, k.dve)
            if k.asub <= 3:
                continue

            GPS_SLABS = (3, 7)

            def emit_fir(which):
                for bi, Kk in enumerate(FIRK):
                    wf = wts["fir%d" % bi]
                    for s in range(NS):
                        if (s in GPS_SLABS) != (which == "gps"):
                            continue
                        base = s * (SEG + PAD) + PAD - (Kk - 1)
                        if which == "gps":
                            k.w(nc.gpsimd, k.dve, k.act)
                            scr = k.scr2[:, 0:SEG]
                            tmp = k.scr3[:, 256:256 + SEG]
                            wb = lambda j: wf[:, s * Kk + j:s * Kk + j + 1] \
                                .broadcast_to([128, SEG])
                            k.pool.inc(k.G.tensor_tensor(
                                scr, vc[:, base:base + SEG], wb(0),
                                OP.mult))
                            for j in range(1, Kk):
                                last = (j == Kk - 1)
                                outap = (br_ap(br, nbr, bi, s % 2, 0, SEG)
                                         [:, s // 2, :] if last else scr)
                                k.pool.inc(k.G.tensor_tensor(
                                    tmp, vc[:, base + j:base + j + SEG],
                                    wb(j), OP.mult))
                                k.pool.inc(k.G.tensor_tensor(
                                    outap, scr, tmp, OP.add))
                        else:
                            k.w(nc.vector, k.act)
                            scr = k.scr[:, 0:SEG]
                            k.dve.inc(k.V.tensor_scalar(
                                scr, vc[:, base:base + SEG],
                                wf[:, s * Kk:s * Kk + 1], None, OP.mult))
                            for j in range(1, Kk):
                                last = (j == Kk - 1)
                                outap = (br_ap(br, nbr, bi, s % 2, 0, SEG)
                                         [:, s // 2, :] if last else scr)
                                k.dve.inc(k.V.scalar_tensor_tensor(
                                    outap, vc[:, base + j:base + j + SEG],
                                    wf[:, s * Kk + j:s * Kk + j + 1],
                                    scr, OP.mult, OP.add))

            emit_fir("gps")

            dsc = contextlib.ExitStack()
            with dsc:
                sfx = "_%d" % sg
                x0a = k.sb(p + "x0" + sfx, [128, 512], BF16, dsc)
                kTM = k.sb(p + "kTM" + sfx, [128, NCHS * 256], BF16, dsc)
                u0a = k.sb(p + "u0" + sfx, [128, NCHS * 256], BF16, dsc)
                wFM = k.sb(p + "wFM" + sfx, [128, NCHS * 256], BF16, dsc)
                tja = k.sq[:, 0:512]
                t4 = k.sq[:, 512:1024]
                Nb = k.sb(p + "Nb" + sfx, [128, 128], BF16, dsc)
                Ntb = k.sb(p + "Ntb" + sfx, [128, 128], BF16, dsc)
                pw2 = k.sb(p + "pw" + sfx, [128, 6 * 128], BF16, dsc)
                mI = k.sb(p + "mI" + sfx, [128, 5 * 128], BF16, dsc)
                uTM = k.sb(p + "uTM" + sfx, [128, 256], BF16, dsc)
                uBD = k.sb(p + "uBD" + sfx, [128, 1024], BF16, dsc)
                attnT = k.sb(p + "aT" + sfx, [128, 128], BF16, dsc)
                emit_delta(k, locals())
            if k.asub <= 4 or (30 < k.asub < 40):
                continue

            for s in range(NS):
                k.dve.inc(k.V.tensor_copy(
                    br_ap(br, nbr, vbi, s % 2, 0, SEG)[:, s // 2, :],
                    vc[:, s * (SEG + PAD) + PAD:(s + 1) * (SEG + PAD)]))
            emit_fir("dve")
            k.w(nc.scalar, k.dve)
            k.w(nc.scalar, k.pool)
            k.w(nc.vector, k.pool)
            k.w(nc.tensor, k.dve)
            k.w(nc.tensor, k.pool)
            if k.asub <= 5:
                continue

            gsc = contextlib.ExitStack()
            with gsc:
                sfx = "_%d" % sg
                lg = k.sb(p + "lg" + sfx, [nlg, SEG], F32, gsc)
                lgh = (k.sb(p + "lgh" + sfx, [128, SEG], F32, gsc)
                       if var == 1 else None)
                stats = k.sb(p + "st" + sfx,
                             [nlg if var == 0 else 128, SEG], F32, gsc)
                stats_b = k.sb(p + "stb" + sfx,
                               [nlg if var == 0 else 128, SEG], BF16, gsc)
                ptm = k.sb(p + "ptm" + sfx, [128, (SEG // 128) * nlg], F32,
                           gsc)
                probs = k.sb(p + "pr" + sfx, [nlg, SEG], F32, gsc)
                probs_b = k.sb(p + "prb" + sfx, [nlg, SEG], BF16, gsc)
                pbcw = k.sb(p + "pbc" + sfx, [128, H * SEG], BF16, gsc)
                oc = k.wide
                ocb = k.sb(p + "ocb" + sfx, [128, NS * SEG], BF16, gsc)
                ao = k.wide2
                h1 = k.sb(p + "h1" + sfx, [128, nmo_h1 * SEG], BF16, gsc)

                if var == 0:
                    emit_gate_l0(k, wts, br, nbr, ln_seg, stats, stats_b, lg,
                                 ptm, probs, h1, gw1x_hbm, gw1db)
                else:
                    emit_gate_l1(k, wts, br, nbr, ln_seg, stats, stats_b, lgh,
                                 ptm, probs, h1)
                if k.asub <= 6:
                    continue

                k.w(nc.vector, k.dve, k.act)
                k.dve.inc(k.V.tensor_copy(probs_b[0:nlg, :],
                                                probs[0:nlg, :]))
                k.dve.inc(k.V.memset(oc[:, 0:NS * SEG], 0.0))
                br5 = br[:].rearrange("p (b h g t) -> p b h g t", b=nbr,
                                      h=H, g=2)
                pb4 = pbcw[:].rearrange("p (h t) -> p h t", h=H) \
                    .unsqueeze(2).broadcast_to([128, H, 2, SEG])
                tmpf = k.wide2[:, 0:NS * SEG]
                tmp4 = tmpf.rearrange("p (h two t) -> p h two t", h=H,
                                      two=2)
                for bi in range(nbr):
                    k.w(nc.tensor, k.dve)
                    for h in range(H):
                        row = (h * 6 + bi) if var == 0 else (h * 4 + bi)
                        bank = k.w1 if h < 2 else k.w2
                        mm = nc.tensor.matmul(
                            bank[:, (h % 2) * SEG:(h % 2 + 1) * SEG],
                            k.rowrepb[0:nlg, row * 128:(row + 1) * 128],
                            probs_b[0:nlg, 0:SEG], start=True, stop=True)
                    k.pe.inc(mm)
                    k.w(nc.vector, k.pe)
                    k.dve.inc(k.V.tensor_copy(pbcw[:, 0:2 * SEG],
                                              k.w1[:, 0:2 * SEG]))
                    k.dve.inc(k.V.tensor_copy(pbcw[:, 2 * SEG:4 * SEG],
                                              k.w2[:, 0:2 * SEG]))
                    k.w(nc.vector, k.dve)
                    k.dve.inc(k.V.tensor_tensor(tmp4, br5[:, bi], pb4,
                                                OP.mult))
                    k.w(nc.vector, k.dve)
                    k.dve.inc(k.V.tensor_tensor(oc[:, 0:NS * SEG],
                                                oc[:, 0:NS * SEG], tmpf,
                                                OP.add))
                    k.w(nc.vector, k.dve)
                k.w(nc.scalar, k.dve)
                k.act.inc(k.A.activation(k.sq[:, 0:NS * SEG],
                                         oc[:, 0:NS * SEG], AF.Square))
                k.w(nc.tensor, k.act, k.dve)
                for h in range(H):
                    bank = k.w1 if h < 2 else k.w2
                    reg = bank[:, (h % 2) * SEG:(h % 2 + 1) * SEG]
                    nc.tensor.matmul(reg, k.ones128[:],
                                     k.sq[:, 2 * h * SEG:(2 * h + 1) * SEG],
                                     start=True, stop=False)
                    mm = nc.tensor.matmul(
                        reg, k.ones128[:],
                        k.sq[:, (2 * h + 1) * SEG:(2 * h + 2) * SEG],
                        start=False, stop=True)
                k.pe.inc(mm)
                k.w(nc.vector, k.pe)
                k.dve.inc(k.V.tensor_copy(k.wide2[:, 0:512],
                                          k.w1[:, 0:512]))
                k.dve.inc(k.V.tensor_copy(k.wide2[:, 512:1024],
                                          k.w2[:, 0:512]))
                k.w(nc.vector, k.dve)
                k.dve.inc(k.V.tensor_scalar(k.wide2[:, 0:1024],
                                            k.wide2[:, 0:1024], 1.0 / DV,
                                            None, OP.mult))
                k.rsqrt(k.wide2[:, 0:1024], 1e-5)
                rms4 = k.wide2[:, 0:1024] \
                    .rearrange("p (h t) -> p h t", h=H) \
                    .unsqueeze(2).broadcast_to([128, H, 2, SEG])
                oc4 = oc[:, 0:NS * SEG] \
                    .rearrange("p (h two t) -> p h two t", h=H, two=2)
                k.w(nc.vector, k.act)
                k.dve.inc(k.V.tensor_tensor(oc4, oc4, rms4, OP.mult))
                k.w(nc.vector, k.dve)
                for vh in range(2):
                    ocb3 = ocb[:].rearrange("p (h two t) -> p h two t",
                                            h=H, two=2)[:, :, vh, :]
                    oc3 = oc[:, 0:NS * SEG].rearrange(
                        "p (h two t) -> p h two t", h=H, two=2)[:, :, vh, :]
                    k.dve.inc(k.V.tensor_scalar(
                        ocb3, oc3, wts["onw"][:, vh:vh + 1], None,
                        OP.mult))
                k.w(nc.vector, k.dve)
                pend = None
                for mo in range(NS):
                    bank = k.w1 if mo % 2 == 0 else k.w2
                    k.w(nc.tensor, k.dve, k.act)
                    for ki in range(NS):
                        mm = nc.tensor.matmul(
                            bank[:, 0:SEG],
                            wts["wo"][:, (ki * NS + mo) * 128:
                                      (ki * NS + mo) * 128 + 128],
                            ocb[:, ki * SEG:(ki + 1) * SEG],
                            start=(ki == 0), stop=(ki == NS - 1))
                    k.pe.inc(mm)
                    if pend is not None:
                        k.evac(*pend, eng="act" if mo % 2 == 0 else "dve")
                    pend = (ao[:, mo * SEG:(mo + 1) * SEG], bank[:, 0:SEG])
                k.evac(*pend, eng="act")

                if dbg_write is not None:
                    dbg_write(sg, locals())

                emit_ln(k, None, SEG, eps=1e-5,
                        src3=ao[:].rearrange("p (s t) -> p s t", s=NS),
                        res3=xr_seg[:].rearrange("p (s t) -> p s t", s=NS))
            k.w(nc.sync, k.dve)
            nc.sync.dma_start(
                out=xr_slab_ap(xr_hbm, t0, SEG),
                in_=xr_seg[:].rearrange("p (s t) -> p s t", s=NS)
            ).then_inc(k.dma.sem, 16)
            k.dma.v += 16

            if sg < NSEG - 1:
                for s in range(NS):
                    k.dve.inc(k.V.tensor_copy(
                        vc[:, s * (SEG + PAD):s * (SEG + PAD) + PAD],
                        vc[:, s * (SEG + PAD) + SEG:
                           s * (SEG + PAD) + SEG + PAD]))
    k.hb_cfg = None
    k.wall(k.dma)


def emit_delta(k, L):
    nc = k.nc
    (glob0, globlast, qn, kn, vc, kb, vb, x0a, kTM, u0a, wFM, tja, t4, Nb, Ntb, pw2,
     mI, uTM, uBD, attnT, S_sb, br, nbr, dbi, PAD, idn, negsl, triu) = (
        L["glob0"], L["globlast"], L["qn"], L["kn"], L["vc"], L["kb"],
        L["vb"], L["x0a"],
        L["kTM"], L["u0a"], L["wFM"], L["tja"], L["t4"], L["Nb"], L["Ntb"],
        L["pw2"], L["mI"], L["uTM"], L["uBD"], L["attnT"], L["S_sb"],
        L["br"],
        L["nbr"], L["dbi"], L["PAD"], L["idn"], L["negsl"], L["triu"])
    SEGPAD = SEG + PAD
    dl = k.asub - 30 if 30 < k.asub < 40 else 99

    for c in range(NCHS):
        cc = c * CH
        k.w(nc.tensor, k.dve, k.act)
        for h in range(H):
            for dsl in range(2):
                s = 2 * h + dsl
                nc.tensor.transpose(
                    k.w4b[32 * h:32 * h + 32, dsl * 128:dsl * 128 + 128],
                    kn[:, s * SEG + cc:s * SEG + cc + CH],
                    idn[:], tile_position=(0, 32 * h))
                nc.tensor.transpose(
                    k.w4b[32 * h:32 * h + 32,
                          256 + dsl * 128:256 + dsl * 128 + 128],
                    kb[:, s * SEG + cc:s * SEG + cc + CH],
                    idn[:], tile_position=(0, 32 * h))
                mm = nc.tensor.transpose(
                    k.w4b[32 * h:32 * h + 32,
                          512 + dsl * 128:512 + dsl * 128 + 128],
                    vb[:, s * SEG + cc:s * SEG + cc + CH],
                    idn[:], tile_position=(0, 32 * h))
        k.pe.inc(mm)
        k.w(nc.vector, k.pe)
        k.dve.inc(k.V.tensor_copy(kTM[:, c * 256:(c + 1) * 256],
                                        k.w4b[:, 0:256]))
        k.dve.inc(k.V.tensor_copy(
            x0a[:, 256:512], k.w4b[:, 256:512]))
        k.dve.inc(k.V.tensor_copy(
            x0a[:, 0:256], k.w4b[:, 512:768]))
        k.w(nc.tensor, k.dve)
        if dl <= 1:
            continue

        for h in range(H):
            for dsl in range(2):
                s = 2 * h + dsl
                mm = nc.tensor.matmul(
                    k.w1[32 * h:32 * h + 32, 32 * h:32 * h + 32],
                    kb[:, s * SEG + cc:s * SEG + cc + CH],
                    kn[:, s * SEG + cc:s * SEG + cc + CH],
                    start=(dsl == 0), stop=(dsl == 1),
                    tile_position=(0, 32 * h))
        k.pe.inc(mm)
        k.w(nc.vector, k.pe)
        k.dve.inc(k.V.tensor_tensor(
            Nb[:], k.w1[:, 0:128], negsl[:], OP.mult))
        k.w(nc.tensor, k.dve)
        k.pe.inc(nc.tensor.transpose(k.w4b[:, 0:128], Nb[:], idn[:]))
        k.w(nc.vector, k.pe)
        k.dve.inc(k.V.tensor_copy(Ntb[:], k.w4b[:, 0:128]))
        k.dve.inc(k.V.tensor_tensor(mI[:, 0:128], k.w4b[:, 0:128],
                                          idn[:], OP.add))
        if dl <= 2:
            continue

        P = lambda i: pw2[:, i * 128:(i + 1) * 128]
        MI = lambda i: mI[:, i * 128:(i + 1) * 128]

        def mmsq(bank, lhsT, rhs):
            k.w(nc.tensor, k.dve, k.act)
            k.pe.inc(nc.tensor.matmul(bank[:, 0:128], lhsT, rhs,
                                      start=True, stop=True))

        def ev_plain(dst, bank):
            k.w(nc.scalar, k.pe)
            k.act.inc(k.A.copy(dst, bank[:, 0:128]))

        def ev_plusI(dst, bank):
            k.w(nc.vector, k.pe)
            k.dve.inc(k.V.tensor_tensor(dst, bank[:, 0:128], idn[:],
                                              OP.add))

        mmsq(k.w2, Ntb[:], Nb[:])
        mmsq(k.w3, Nb[:], Ntb[:])
        ev_plain(P(0), k.w2)
        ev_plain(P(1), k.w3); ev_plusI(MI(1), k.w3)
        mmsq(k.w2, P(1), P(0))
        mmsq(k.w3, P(0), P(1))
        ev_plain(P(2), k.w2)
        ev_plain(P(3), k.w3); ev_plusI(MI(2), k.w3)
        mmsq(k.w2, P(3), P(2))
        mmsq(k.w3, P(2), P(3))
        ev_plain(P(4), k.w2)
        ev_plain(P(5), k.w3); ev_plusI(MI(3), k.w3)
        mmsq(k.w2, P(4), P(5))
        ev_plusI(MI(4), k.w2)
        if dl <= 3:
            continue

        src = x0a[:, 0:512]
        for jj, mi_i in enumerate((4, 3, 2, 1)):
            dst = tja[:] if mi_i != 1 else t4[:]
            k.w(nc.tensor, k.dve, k.act)
            k.pe.inc(nc.tensor.matmul(k.w3[:, 0:512], MI(mi_i), src,
                                      start=True, stop=True))
            k.w(nc.vector, k.pe)
            k.dve.inc(k.V.tensor_copy(dst, k.w3[:, 0:512]))
            src = dst
        k.w(nc.tensor, k.dve)
        k.pe.inc(nc.tensor.matmul(k.w3[:, 0:256], MI(0), t4[:, 0:256],
                                  start=True, stop=True))
        k.w(nc.vector, k.pe)
        k.dve.inc(k.V.tensor_copy(u0a[:, c * 256:(c + 1) * 256],
                                        k.w3[:, 0:256]))
        if dl <= 4:
            continue
        k.w(nc.tensor, k.dve)
        for dsl in range(2):
            tgt = k.w1 if dsl == 0 else k.w2
            lh = t4[:, 256 + dsl * 128:256 + dsl * 128 + 128]
            nc.tensor.matmul(tgt[:, 0:128], lh, Ntb[:],
                             start=True, stop=False)
            mm = nc.tensor.matmul(tgt[:, 0:128], lh, idn[:],
                                  start=False, stop=True)
        k.pe.inc(mm)
        k.w(nc.vector, k.pe)
        k.dve.inc(k.V.tensor_copy(wFM[:, c * 256:c * 256 + 128],
                                        k.w1[:, 0:128]))
        k.dve.inc(k.V.tensor_copy(wFM[:, c * 256 + 128:c * 256 + 256],
                                        k.w2[:, 0:128]))
        k.w(nc.tensor, k.dve)

    if dl <= 5:
        return
    for c in range(NCHS):
        cc = c * CH
        k.w(nc.tensor, k.dve, k.act)
        for h in range(H):
            for dsl in range(2):
                mm = nc.tensor.matmul(
                    k.w1[32 * h:32 * h + 32, 0:256],
                    wFM[:, c * 256 + dsl * 128 + 32 * h:
                        c * 256 + dsl * 128 + 32 * h + 32],
                    S_sb[:, dsl * 1024 + h * 256:dsl * 1024 + (h + 1) * 256],
                    start=(dsl == 0), stop=(dsl == 1),
                    tile_position=(0, 32 * h))
        k.pe.inc(mm)
        k.w(nc.vector, k.pe)
        k.dve.inc(k.V.tensor_tensor(
            uTM[:], u0a[:, c * 256:(c + 1) * 256], k.w1[:, 0:256],
            OP.subtract))
        if c == 0:
            k.dve.inc(k.V.memset(uBD[:], 0.0))
        k.w(nc.vector, k.pe, k.dve)
        for h in range(H):
            k.dve.inc(k.V.tensor_copy(
                uBD[32 * h:32 * h + 32, h * 256:(h + 1) * 256],
                uTM[32 * h:32 * h + 32, 0:256]))
        k.w(nc.tensor, k.dve)
        if dl <= 6:
            continue
        for h in range(H):
            for dsl in range(2):
                mm = nc.tensor.matmul(
                    k.w2[32 * h:32 * h + 32, 32 * h:32 * h + 32],
                    kn[:, (2 * h + dsl) * SEG + cc:
                       (2 * h + dsl) * SEG + cc + CH],
                    qn[:, (2 * h + dsl) * SEG + cc:
                       (2 * h + dsl) * SEG + cc + CH],
                    start=(dsl == 0), stop=(dsl == 1),
                    tile_position=(0, 32 * h))
        k.pe.inc(mm)
        k.w(nc.vector, k.pe)
        k.dve.inc(k.V.tensor_tensor(attnT[:], k.w2[:, 0:128], triu[:],
                                          OP.mult))
        k.w(nc.tensor, k.dve)
        if dl <= 7:
            continue
        first = True
        for h in range(H):
            for vh in range(2):
                for dsl in range(2):
                    nc.tensor.matmul(
                        k.w3[:, vh * 128 + 32 * h:vh * 128 + 32 * h + 32],
                        S_sb[:, dsl * 1024 + h * 256 + vh * 128:
                             dsl * 1024 + h * 256 + vh * 128 + 128],
                        qn[:, (2 * h + dsl) * SEG + cc:
                           (2 * h + dsl) * SEG + cc + CH],
                        start=first, stop=False)
                    first = False
        for vh in range(2):
            mm = nc.tensor.matmul(
                k.w3[:, vh * 128:vh * 128 + 128],
                uTM[:, vh * 128:vh * 128 + 128],
                attnT[:],
                start=False, stop=(vh == 1))
        k.pe.inc(mm)
        k.w(nc.vector, k.pe)
        for vh in range(2):
            k.dve.inc(k.V.tensor_copy(
                br_ap(br, nbr, dbi, vh, cc, CH),
                k.w3[:, vh * 128:vh * 128 + 128]
                .rearrange("p (h c) -> p h c", h=H)))
        k.w(nc.tensor, k.dve)
        if dl <= 8:
            continue
        for dsl in range(2):
            Sp = k.S0 if dsl == 0 else k.S1
            for half in range(2):
                bank_first = (glob0 and c == 0)
                bank_last = (globlast and c == NCHS - 1)
                mm = nc.tensor.matmul(
                    Sp[:, half * 512:(half + 1) * 512],
                    kTM[:, c * 256 + dsl * 128:c * 256 + dsl * 128 + 128],
                    uBD[:, half * 512:(half + 1) * 512],
                    start=bank_first, stop=bank_last,
                    skip_group_check=True)
        k.pe.inc(mm)
        k.w(nc.vector, k.pe)
        k.w(nc.scalar, k.pe)
        k.dve.inc(k.V.tensor_copy(S_sb[:, 0:1024], k.S0[:]))
        k.act.inc(k.A.copy(S_sb[:, 1024:2048], k.S1[:]))
        k.w(nc.tensor, k.dve, k.act)


def emit_gate_l0(k, wts, br, nbr, ln_seg, stats, stats_b, lg, ptm, probs,
                 h1, gw1x_hbm, gw1db):
    nc = k.nc
    k.w(nc.gpsimd, k.act, k.dve)
    k.str_e.inc(nc.gpsimd.dma_start(out=gw1db[:, 0:1024],
                                    in_=gw1x_hbm[:, 0:1024]), 16)
    first = True
    for bi in range(nbr):
        k.w(nc.scalar, k.pe, k.dve)
        for s in range(NS):
            vh, h = s % 2, s // 2
            k.act.inc(k.A.activation(
                k.sq[:, s * SEG:(s + 1) * SEG],
                br_ap(br, nbr, bi, vh, 0, SEG)[:, h, :], AF.Abs))
        k.w(nc.tensor, k.act)
        for h in range(H):
            row = h * 6 + bi
            for sl in (2 * h, 2 * h + 1):
                mm = nc.tensor.matmul(
                    k.w2[0:24, 0:SEG],
                    k.rowsel[:, row * 24:(row + 1) * 24],
                    k.sq[:, sl * SEG:(sl + 1) * SEG],
                    start=first,
                    stop=(bi == nbr - 1 and h == H - 1 and sl == 2 * h + 1))
                first = False
        k.pe.inc(mm)
    k.w(nc.vector, k.pe)
    k.dve.inc(k.V.tensor_scalar(stats[0:24, 0:SEG], k.w2[0:24, 0:SEG],
                                      1.0 / DV, None, OP.mult))
    k.w(nc.vector, k.dve)
    k.dve.inc(k.V.tensor_copy(stats_b[0:24, 0:SEG], stats[0:24, 0:SEG]))
    k.w(nc.tensor, k.dve)

    pend = None

    def l0_gelu(p):
        mo_, bank_ = p
        k.w(nc.scalar, k.pe)
        k.act.inc(k.A.activation(
            h1[:, mo_ * SEG:(mo_ + 1) * SEG], bank_[:, 0:SEG], AF.Gelu,
            bias=wts["gb1"][:, mo_:mo_ + 1]))

    for mo in range(16):
        bank = k.w1 if mo % 2 == 0 else k.w3
        if mo + 1 < 16:
            k.w(nc.gpsimd, k.pe)
            ctr = k.str_o if (mo + 1) % 2 else k.str_e
            ctr.inc(nc.gpsimd.dma_start(
                out=gw1db[:, ((mo + 1) % 2) * 1024:((mo + 1) % 2) * 1024 + 1024],
                in_=gw1x_hbm[:, (mo + 1) * 1024:(mo + 2) * 1024]), 16)
        k.w(nc.tensor, k.dve, k.act)
        pc = k.str_o if mo % 2 else k.str_e
        nc.tensor.wait_ge(pc.sem, pc.v)
        for ki in range(NS):
            nc.tensor.matmul(
                bank[:, 0:SEG],
                gw1db[:, (mo % 2) * 1024 + ki * 128:
                      (mo % 2) * 1024 + (ki + 1) * 128],
                ln_seg[:, ki * SEG:(ki + 1) * SEG],
                start=(ki == 0), stop=False)
        mm = nc.tensor.matmul(
            bank[:, 0:SEG], wts["gw1s"][0:24, mo * 128:(mo + 1) * 128],
            stats_b[0:24, 0:SEG], start=False, stop=True)
        k.pe.inc(mm)
        if pend is not None:
            l0_gelu(pend)
        pend = (mo, bank)
    l0_gelu(pend)
    k.w(nc.tensor, k.act)
    for ki in range(16):
        mm = nc.tensor.matmul(k.w2[0:24, 0:SEG],
                              wts["gw2"][:, ki * 24:(ki + 1) * 24],
                              h1[:, ki * SEG:(ki + 1) * SEG],
                              start=(ki == 0), stop=(ki == 15))
    k.pe.inc(mm)
    k.w(nc.vector, k.pe)
    k.dve.inc(k.V.tensor_scalar(lg[:, 0:SEG], k.w2[0:24, 0:SEG],
                                      wts["gb2"][0:24, 0:1], None, OP.add))
    k.w(nc.vector, k.dve)
    k.dve.inc(k.V.scalar_tensor_tensor(
        lg[:, 0:SEG], stats[0:24, 0:SEG], wts["alpha"][0:24, 0:1],
        lg[:, 0:SEG], OP.mult, OP.add))
    k.w(nc.scalar, k.dve)
    k.act.inc(k.A.activation(k.scr2[0:24, 0:1], wts["gtemp"][0:24, 0:1],
                             AF.Exp))
    k.act.inc(k.A.activation(k.scr2[0:24, 0:1], k.scr2[0:24, 0:1],
                             AF.Ln, bias=k.cst(1.0)[0:24]))
    k.w(nc.vector, k.act)
    k.dve.inc(k.V.tensor_scalar(k.scr2[0:24, 0:1], k.scr2[0:24, 0:1],
                                      1e-4, None, OP.add))
    k.w(nc.vector, k.dve)
    k.dve.inc(k.V.reciprocal(k.scr2[0:24, 0:1], k.scr2[0:24, 0:1]))
    k.w(nc.vector, k.dve)
    k.dve.inc(k.V.tensor_scalar(lg[:, 0:SEG], lg[:, 0:SEG],
                                      k.scr2[0:24, 0:1], None, OP.mult))
    emit_softmax_groups(k, lg, ptm, probs, nrows=24, ngrp=H, grp=6,
                        floor=0.05, floor_mode="last")


def emit_gate_l1(k, wts, br, nbr, ln_seg, stats, stats_b, lgh, ptm, probs,
                 h1):
    nc = k.nc
    for h in range(H):
        first = True
        for bi in range(nbr):
            k.w(nc.scalar, k.pe, k.dve)
            for vh in range(2):
                bsl = br_ap(br, nbr, bi, vh, 0, SEG)[:, h, :]
                k.act.inc(k.A.activation(
                    k.sq[:, vh * SEG:(vh + 1) * SEG], bsl, AF.Square))
                k.act.inc(k.A.activation(
                    k.sq[:, (2 + vh) * SEG:(3 + vh) * SEG], bsl, AF.Abs))
            k.w(nc.tensor, k.act)
            for vh in range(2):
                bsl = br_ap(br, nbr, bi, vh, 0, SEG)[:, h, :]
                rS, rSq, rA, rL2 = 4 + bi, 8 + bi, 12 + bi, bi
                po = k.w2[32 * h:32 * h + 16, 0:SEG]
                nc.tensor.matmul(po, k.rowsel16[:, rS * 16:(rS + 1) * 16],
                                 bsl, start=first, stop=False,
                                 tile_position=(0, 32 * h))
                first = False
                nc.tensor.matmul(po,
                                 k.rowsel16[:, rSq * 16:(rSq + 1) * 16],
                                 k.sq[:, vh * SEG:(vh + 1) * SEG],
                                 start=False, stop=False,
                                 tile_position=(0, 32 * h))
                nc.tensor.matmul(po,
                                 k.rowsel16[:, rL2 * 16:(rL2 + 1) * 16],
                                 k.sq[:, vh * SEG:(vh + 1) * SEG],
                                 start=False, stop=False,
                                 tile_position=(0, 32 * h))
                mm = nc.tensor.matmul(
                    po, k.rowsel16[:, rA * 16:(rA + 1) * 16],
                    k.sq[:, (2 + vh) * SEG:(3 + vh) * SEG],
                    start=False, stop=(bi == nbr - 1 and vh == 1),
                    tile_position=(0, 32 * h))
            k.pe.inc(mm)
        k.w(nc.vector, k.pe)
        allv = stats[32 * h:32 * h + 16, 0:SEG]
        k.dve.inc(k.V.tensor_scalar(
            allv, k.w2[32 * h:32 * h + 16, 0:SEG], 1.0 / DV, None, OP.mult))
        k.w(nc.vector, k.dve)
        k.dve.inc(k.V.tensor_tensor(
            k.scr3[32 * h:32 * h + 16, 0:SEG], allv, allv, OP.mult))
        k.w(nc.tensor, k.dve)
        k.pe.inc(nc.tensor.matmul(
            k.w1[32 * h:32 * h + 16, 0:SEG],
            k.shv16[32 * h:32 * h + 16, 0:16],
            k.scr3[32 * h:32 * h + 16, 0:SEG], start=True, stop=True,
            tile_position=(32 * h, 32 * h)))
        k.w(nc.vector, k.pe)
        k.dve.inc(k.V.tensor_tensor(
            allv, allv, k.w1[32 * h:32 * h + 16, 0:SEG], OP.subtract))
        view = stats[32 * h:32 * h + 4, 0:SEG]
        k.w(nc.scalar, k.dve)
        k.act.inc(k.A.activation(view, view, AF.Ln,
                                 bias=k.cst(1e-37)[32 * h:32 * h + 4],
                                 scale=float(DV)))
        k.act.inc(k.A.activation(view, view, AF.Exp, scale=0.5))
        k.w(nc.vector, k.act)
        k.dve.inc(k.V.tensor_copy(stats_b[32 * h:32 * h + 16, 0:SEG],
                                        stats[32 * h:32 * h + 16, 0:SEG]))
    k.w(nc.tensor, k.dve)

    for h in range(H):
        pend = None

        def l1_gelu(p):
            mo_, bank_ = p
            k.w(nc.scalar, k.pe)
            k.act.inc(k.A.activation(
                h1[:, mo_ * SEG:(mo_ + 1) * SEG], bank_[:, 0:SEG],
                AF.Gelu_apprx_tanh, bias=wts["gb1"][:, mo_:mo_ + 1]))

        for mo in range(8):
            bank = k.w1 if mo % 2 == 0 else k.w3
            k.w(nc.tensor, k.dve, k.act)
            for ki in range(NS):
                nc.tensor.matmul(
                    bank[:, 0:SEG],
                    wts["gw1x"][:, (ki * 8 + mo) * 128:(ki * 8 + mo) * 128 + 128],
                    ln_seg[:, ki * SEG:(ki + 1) * SEG],
                    start=(ki == 0), stop=False)
            mm = nc.tensor.matmul(
                bank[:, 0:SEG],
                wts["gw1s"][32 * h:32 * h + 16, mo * 128:(mo + 1) * 128],
                stats_b[32 * h:32 * h + 16, 0:SEG], start=False, stop=True,
                tile_position=(32 * h, 0))
            k.pe.inc(mm)
            if pend is not None:
                l1_gelu(pend)
            pend = (mo, bank)
        l1_gelu(pend)
        k.w(nc.tensor, k.act)
        for ki in range(8):
            mm = nc.tensor.matmul(k.w2[32 * h:32 * h + 4, 0:SEG],
                                  wts["gw2"][:, ki * 4:(ki + 1) * 4],
                                  h1[:, ki * SEG:(ki + 1) * SEG],
                                  start=(ki == 0), stop=(ki == 7),
                                  tile_position=(0, 32 * h))
        k.pe.inc(mm)
        hs = slice(32 * h, 32 * h + 4)
        k.w(nc.vector, k.pe)
        k.dve.inc(k.V.tensor_scalar(
            lgh[hs, 0:SEG], k.w2[hs, 0:SEG], wts["gb2"][hs, 0:1], None,
            OP.add))
        k.w(nc.vector, k.dve)
        k.dve.inc(k.V.tensor_scalar(
            lgh[hs, 0:SEG], lgh[hs, 0:SEG],
            wts["gcbd"][hs, 0:1], None, OP.add))
        k.w(nc.vector, k.dve)
        k.dve.inc(k.V.tensor_scalar(
            lgh[hs, 0:SEG], lgh[hs, 0:SEG], wts["grt"][hs, 0:1], None,
            OP.mult))
        k.w(nc.vector, k.dve)
    k.dve.inc(k.V.memset(k.scr3[:, 0:SEG], 0.0))
    k.w(nc.vector, k.dve)
    for h in range(H):
        k.dve.inc(k.V.tensor_copy(k.scr3[32 * h:32 * h + 4, 0:SEG],
                                  lgh[32 * h:32 * h + 4, 0:SEG]))
    k.w(nc.tensor, k.dve)
    k.pe.inc(nc.tensor.matmul(k.w2[0:16, 0:SEG], wts["sel16"][:, 0:16],
                              k.scr3[:, 0:SEG], start=True, stop=True))
    k.w(nc.vector, k.pe)
    k.dve.inc(k.V.tensor_copy(lgh[0:16, 0:SEG], k.w2[0:16, 0:SEG]))
    k.w(nc.vector, k.dve)
    DECAY = 1.0 - 1.0 / 3000.0
    emit_softmax_groups(k, lgh, ptm, probs, nrows=16, ngrp=H, grp=4,
                        floor=0.08 * DECAY, floor_mode="all", lgh=None)


def emit_ffn0(k, xr_hbm, wgu_hbm, wd_hbm):
    nc = k.nc
    T = k.T
    FSEG = min(1024, T)
    NF = T // FSEG
    NTT = FSEG // 256
    sc = contextlib.ExitStack()
    with sc:
        xrf = k.sb("f0_xr", [128, NS * FSEG], F32, sc)
        lnf = k.sb("f0_ln", [128, NS * FSEG], BF16, sc)
        hs = k.sb("f0_hs", [128, 24 * FSEG], BF16, sc)
        gu_db = k.sb("f0_gu", [128, 2 * 2048], BF16, sc)
        wd_db = k.sb("f0_wd", [128, 2 * 3072], BF16, sc)
        gsb = k.sb("f0_gsb", [128, 1024], F32, sc)
        ao = k.sb("f0_ao", [128, NS * FSEG], F32, sc)
        SC = 1.0 / 127.0
        TWD = min(512, FSEG)
        NTW = FSEG // TWD
        k.hb_cfg = True

        for f in range(NF):
            t0 = f * FSEG
            k.w(nc.sync, k.dve, k.act, k.pe, k.dma)
            nc.sync.dma_start(
                out=xrf[:].rearrange("p (s t) -> p s t", s=NS),
                in_=xr_slab_ap(xr_hbm, t0, FSEG)).then_inc(k.dma.sem, 16)
            k.dma.v += 16
            k.wall(k.dma)
            xrf3 = xrf[:].rearrange("p (s t) -> p s t", s=NS)
            lnf3 = lnf[:].rearrange("p (s t) -> p s t", s=NS)
            for sub in range(NTT):
                emit_ln(k, None, 256, eps=1e-5,
                        src3=xrf3[:, :, sub * 256:sub * 256 + 256],
                        out3=lnf3[:, :, sub * 256:sub * 256 + 256])
            def f0_consume(p):
                mo_, tt_, gb_, ub_, i_ = p
                gs = gsb[:, i_ * TWD:(i_ + 1) * TWD]
                k.w(nc.scalar, k.pe)
                k.act.inc(k.A.activation(gs, gb_[:, 0:TWD], AF.Silu,
                                         scale=SC))
                k.w(nc.vector, k.pe, k.act)
                k.dve.inc(k.V.scalar_tensor_tensor(
                    hs[:, mo_ * FSEG + tt_ * TWD:
                       mo_ * FSEG + tt_ * TWD + TWD],
                    ub_[:, 0:TWD], SC, gs, OP.mult, OP.mult))

            k.w(nc.gpsimd, k.act, k.dve)
            k.str_e.inc(nc.gpsimd.dma_start(out=gu_db[:, 0:2048],
                                            in_=wgu_hbm[:, 0:2048]), 16)
            pend = None
            for mo in range(24):
                par = mo % 2
                if mo + 1 < 24:
                    k.w(nc.gpsimd, k.pe)
                    ctr = k.str_o if (mo + 1) % 2 else k.str_e
                    ctr.inc(nc.gpsimd.dma_start(
                        out=gu_db[:, ((mo + 1) % 2) * 2048:
                                  ((mo + 1) % 2) * 2048 + 2048],
                        in_=wgu_hbm[:, (mo + 1) * 2048:(mo + 2) * 2048]), 16)
                pc = k.str_o if par else k.str_e
                nc.tensor.wait_ge(pc.sem, pc.v)
                for tt in range(NTW):
                    i = (mo * NTW + tt) % 2
                    gb = k.w1 if i == 0 else k.w3
                    ub = k.w2 if i == 0 else k.S0
                    k.w(nc.tensor, k.dve, k.act)
                    for ki in range(NS):
                        nc.tensor.matmul(
                            gb[:, 0:TWD],
                            gu_db[:, par * 2048 + ki * 128:
                                  par * 2048 + (ki + 1) * 128],
                            lnf[:, ki * FSEG + tt * TWD:
                                ki * FSEG + tt * TWD + TWD],
                            start=(ki == 0), stop=(ki == NS - 1))
                        mm = nc.tensor.matmul(
                            ub[:, 0:TWD],
                            gu_db[:, par * 2048 + 1024 + ki * 128:
                                  par * 2048 + 1024 + (ki + 1) * 128],
                            lnf[:, ki * FSEG + tt * TWD:
                                ki * FSEG + tt * TWD + TWD],
                            start=(ki == 0), stop=(ki == NS - 1))
                    k.pe.inc(mm)
                    if pend is not None:
                        f0_consume(pend)
                    pend = (mo, tt, gb, ub, i)
            f0_consume(pend)
            def f0_dconsume(p):
                mo_, tt_, bank_ = p
                k.w(nc.scalar, k.pe)
                k.act.inc(k.A.activation(
                    ao[:, mo_ * FSEG + tt_ * TWD:
                       mo_ * FSEG + tt_ * TWD + TWD],
                    bank_[:, 0:TWD], AF.Copy, scale=SC))

            k.w(nc.gpsimd, k.dve, k.act)
            k.str_e.inc(nc.gpsimd.dma_start(out=wd_db[:, 0:3072],
                                            in_=wd_hbm[:, 0:3072]), 16)
            pend = None
            for mo in range(NS):
                par = mo % 2
                if mo + 1 < NS:
                    k.w(nc.gpsimd, k.pe)
                    ctr = k.str_o if (mo + 1) % 2 else k.str_e
                    ctr.inc(nc.gpsimd.dma_start(
                        out=wd_db[:, ((mo + 1) % 2) * 3072:
                                  ((mo + 1) % 2) * 3072 + 3072],
                        in_=wd_hbm[:, (mo + 1) * 3072:(mo + 2) * 3072]), 16)
                pc = k.str_o if par else k.str_e
                nc.tensor.wait_ge(pc.sem, pc.v)
                for tt in range(NTW):
                    bank = k.w1 if (mo * NTW + tt) % 2 == 0 else k.w3
                    k.w(nc.tensor, k.dve, k.act)
                    for ki in range(24):
                        mm = nc.tensor.matmul(
                            bank[:, 0:TWD],
                            wd_db[:, par * 3072 + ki * 128:
                                  par * 3072 + (ki + 1) * 128],
                            hs[:, ki * FSEG + tt * TWD:
                               ki * FSEG + tt * TWD + TWD],
                            start=(ki == 0), stop=(ki == 23))
                    k.pe.inc(mm)
                    if pend is not None:
                        f0_dconsume(pend)
                    pend = (mo, tt, bank)
            f0_dconsume(pend)
            k.w(nc.vector, k.act)
            ao3 = ao[:].rearrange("p (s t) -> p s t", s=NS)
            for sub in range(NTT):
                emit_ln(k, None, 256, eps=1e-5,
                        src3=ao3[:, :, sub * 256:sub * 256 + 256],
                        res3=xrf3[:, :, sub * 256:sub * 256 + 256])
            k.w(nc.sync, k.dve)
            nc.sync.dma_start(
                out=xr_slab_ap(xr_hbm, t0, FSEG),
                in_=xrf[:].rearrange("p (s t) -> p s t", s=NS)
            ).then_inc(k.dma.sem, 16)
            k.dma.v += 16
    k.hb_cfg = None
    k.wall(k.dma)



def emit_dummy_y(k, y_out):
    nc = k.nc
    T = k.T
    zz = k.sb("dummy_y", [128, D], F32)
    k.w(nc.vector, k.dma, k.dmag)
    k.dve.inc(k.V.memset(zz[:], 0.0))
    k.w(nc.sync, k.dve)
    for n in range(T // 128):
        nc.sync.dma_start(out=y_out[n * 128:(n + 1) * 128, :],
                          in_=zz[:]).then_inc(k.dma.sem, 16)
        k.dma.v += 16
    k.wall(k.dma)


def emit_moe(k, xr_hbm, win, y_out, lim=99, wbh=None):
    nc = k.nc
    T = k.T
    NC = k.NC
    N = NC * T
    E, TK, LGK = 32, 4, 4
    EPB = E // NC
    CAP = max(1, int(1.25 * N * TK / E))
    CAPF = -(-CAP // 128) * 128
    PT = T // 16
    FLOOR_BIG = -3.0e38
    assert T % 16 == 0 and PT <= 128

    xf_pay = nc.dram_tensor("xf_pay", [T, D], BF16).ap()
    wf_pay = nc.dram_tensor("wf_pay", [E, T], BF16).ap()
    xf_all = nc.dram_tensor("xf_all", [N, D], BF16).ap()
    wf_all = nc.dram_tensor("wf_all", [NC * E, T], BF16).ap()
    eo_pay = nc.dram_tensor("eo_pay", [EPB * CAPF, D], BF16).ap()
    eo_all = nc.dram_tensor("eo_all", [NC * EPB * CAPF, D], BF16).ap()
    xr_tm = nc.dram_tensor("xr_tm", [T, D], F32).ap()
    z_hbm = nc.dram_tensor("z_hbm", [T, D], F32).ap()
    slot_hbm = nc.dram_tensor("slot_hbm", [(E + 4) * CAPF, 1], F32).ap()
    slot_all = nc.dram_tensor("slot_all", [(E + 4) * CAPF, 1], F32).ap()
    pos_hbm = nc.dram_tensor("pos_hbm", [128, 4 * (N // 16)], F32).ap()

    sc = contextlib.ExitStack()
    with sc:
        idxa = k.sb("m_idx", [128, 16 * TK], F32, sc)
        wna = k.sb("m_wn", [128, 16 * TK], F32, sc)
        p1 = contextlib.ExitStack()
        p1.__enter__()
        xrm = k.sb("m_xr", [128, NS * 256], F32, p1)
        xff = k.sb("m_xff", [128, NS * T], BF16, p1)
        hshf = k.sb("m_hshf", [128, NS * T], BF16, p1)
        lgf = k.sb("m_lgf", [32, T], F32, p1)
        stm = k.sb("m_stm", [128, 32], F32, p1)
        wdt = k.sb("m_wd", [128, 32], BF16, p1)
        io32 = k.sb("m_io32", [128, 32], F32, p1)
        io32g = k.sb("m_io32g", [128, 32], F32, p1)
        m32 = k.sb("m_m32", [128, 32], F32, p1)
        v8 = k.sb("m_v8", [128, 8], F32, p1)
        i8 = k.sb("m_i8", [128, 8], U32, p1)
        i8f = k.sb("m_i8f", [128, 8], F32, p1)
        tmx = k.sb("m_tmx", [128, 32], F32, p1)
        ztm = k.sb("m_ztm", [128, D], F32, p1)

        k.pool.inc(k.G.iota(io32[:], pattern=[[1, 32]], base=0,
                            channel_multiplier=0,
                            allow_small_or_imprecise_dtypes=True))
        k.pool.inc(k.G.iota(io32g[:].rearrange("p (g j) -> p g j", g=8),
                            pattern=[[1, 8], [0, 4]], base=0,
                            channel_multiplier=0,
                            allow_small_or_imprecise_dtypes=True))

        wgt = k.sb("m_wg", [128, 8 * 32], BF16, p1)
        sw13 = k.sb("m_sw13", [128, 2 * 64 * 128], BF16, p1)
        sw2 = k.sb("m_sw2", [128, 64 * 128], BF16, p1)
        k.hload(wgt[:], win["moe_wg"])
        k.hload(sw13[:], win["moe_sw13"])
        k.hload(sw2[:], win["moe_sw2"])
        k.wall(k.dmag)
        k.wall(k.pool)

        for sub in range(T // 256):
            t0 = sub * 256
            k.w(nc.sync, k.dve, k.act, k.pe, k.dma)
            nc.sync.dma_start(
                out=xrm[:].rearrange("p (s t) -> p s t", s=NS),
                in_=xr_slab_ap(xr_hbm, t0, 256)).then_inc(k.dma.sem, 16)
            k.dma.v += 16
            k.wall(k.dma)
            for tt in range(2):
                k.w(nc.vector, k.dma)
                for s in range(NS):
                    k.w(nc.tensor, k.dve)
                    k.pe.inc(nc.tensor.transpose(
                        (k.w1 if s % 2 == 0 else k.w2)[:, 0:128],
                        xrm[:, s * 256 + tt * 128:s * 256 + tt * 128 + 128],
                        k.identf[:]))
                    k.evac(ztm[:, s * 128:(s + 1) * 128],
                           (k.w1 if s % 2 == 0 else k.w2)[:, 0:128])
                k.w(nc.sync, k.dve)
                nc.sync.dma_start(
                    out=xr_tm[t0 + tt * 128:t0 + tt * 128 + 128, :],
                    in_=ztm[:, 0:D]).then_inc(k.dma.sem, 16)
                k.dma.v += 16
            k.w(nc.vector, k.dma)
            emit_ln(k, None, 256, eps=1e-5,
                    src3=xrm[:].rearrange("p (s t) -> p s t", s=NS),
                    out3=xff[:].rearrange("p (s t) -> p s t", s=NS)
                    [:, :, t0:t0 + 256])
            k.w(nc.tensor, k.dve)
            for ki in range(NS):
                mm = nc.tensor.matmul(k.w1[0:32, 0:256],
                                      wgt[:, ki * 32:(ki + 1) * 32],
                                      xff[:, ki * T + t0:ki * T + t0 + 256],
                                      start=(ki == 0), stop=(ki == NS - 1))
            k.pe.inc(mm)
            k.w(nc.vector, k.pe)
            k.dve.inc(k.V.tensor_copy(lgf[0:32, t0:t0 + 256],
                                      k.w1[0:32, 0:256]))
            for mo in range(NS):
                k.w(nc.tensor, k.dve, k.act)
                for ki in range(NS):
                    nc.tensor.matmul(
                        k.w2[:, 0:256],
                        sw13[:, (ki * NS + mo) * 128:(ki * NS + mo) * 128 + 128],
                        xff[:, ki * T + t0:ki * T + t0 + 256],
                        start=(ki == 0), stop=(ki == NS - 1))
                    mm = nc.tensor.matmul(
                        k.w3[:, 0:256],
                        sw13[:, 8192 + (ki * NS + mo) * 128:
                              8192 + (ki * NS + mo) * 128 + 128],
                        xff[:, ki * T + t0:ki * T + t0 + 256],
                        start=(ki == 0), stop=(ki == NS - 1))
                k.pe.inc(mm)
                k.w(nc.scalar, k.pe)
                k.act.inc(k.A.activation(k.scr[:, 0:256], k.w2[:, 0:256],
                                         AF.Silu))
                k.w(nc.vector, k.pe, k.act)
                k.dve.inc(k.V.tensor_tensor(
                    hshf[:, mo * T + t0:mo * T + t0 + 256], k.w3[:, 0:256],
                    k.scr[:, 0:256], OP.mult))
        k.w(nc.tensor, k.dve)

        PTs = slice(0, PT)
        for a in range(16):
            sl = lambda base: slice(base + a, base + T, 16)
            k.w(nc.vector, k.dma)
            k.w(nc.tensor, k.dve, k.act)
            for half in range(2):
                for ki in range(NS):
                    mm = nc.tensor.matmul(
                        (k.w2 if half == 0 else k.w3)[0:PT, 0:512],
                        hshf[:, sl(ki * T)],
                        sw2[:, ki * 1024 + half * 512:
                            ki * 1024 + half * 512 + 512],
                        start=(ki == 0), stop=(ki == NS - 1))
            k.pe.inc(mm)
            k.w(nc.vector, k.pe)
            k.dve.inc(k.V.tensor_copy(ztm[PTs, 0:512], k.w2[0:PT, 0:512]))
            k.dve.inc(k.V.tensor_copy(ztm[PTs, 512:1024],
                                      k.w3[0:PT, 0:512]))
            k.w(nc.sync, k.dve)
            nc.sync.dma_start(out=z_hbm[a * PT:(a + 1) * PT, :],
                              in_=ztm[PTs, 0:D]).then_inc(k.dma.sem, 16)
            k.dma.v += 16
            k.w(nc.tensor, k.dve)
            k.pe.inc(nc.tensor.transpose(k.w1[0:PT, 0:32],
                                         lgf[0:32, sl(0)],
                                         k.identf[0:32, 0:32]))
            k.w(nc.vector, k.pe)
            k.dve.inc(k.V.tensor_copy(stm[PTs, :], k.w1[0:PT, 0:32]))
            k.dve.inc(k.V.tensor_reduce(k.scr[PTs, 0:1], stm[PTs, :], AX.X,
                                        OP.max))
            k.dve.inc(k.V.tensor_scalar(stm[PTs, :], stm[PTs, :],
                                        k.scr[PTs, 0:1], None, OP.subtract))
            k.w(nc.scalar, k.dve)
            k.act.inc(k.A.activation(stm[PTs, :], stm[PTs, :], AF.Exp))
            k.w(nc.vector, k.act)
            k.dve.inc(k.V.tensor_reduce(k.scr[PTs, 0:1], stm[PTs, :], AX.X,
                                        OP.add))
            k.dve.inc(k.V.reciprocal(k.scr[PTs, 1:2], k.scr[PTs, 0:1]))
            k.dve.inc(k.V.tensor_scalar(stm[PTs, :], stm[PTs, :],
                                        k.scr[PTs, 1:2], None, OP.mult))
            k.dve.inc(k.V.tensor_reduce(
                tmx[PTs, 0:8],
                stm[PTs, :].rearrange("p (g j) -> p g j", g=8), AX.X,
                OP.max))
            k.dve.inc(k.V.max(v8[PTs, :], tmx[PTs, 0:8]))
            k.dve.inc(k.V.max_index(i8[PTs, :], v8[PTs, :], tmx[PTs, 0:8]))
            k.dve.inc(k.V.tensor_copy(i8f[PTs, :], i8[PTs, :]))
            k.dve.inc(k.V.memset(m32[:], 0.0))
            for kk in range(LGK):
                k.dve.inc(k.V.tensor_scalar(tmx[PTs, :], io32g[PTs, :],
                                            i8f[PTs, kk:kk + 1], None,
                                            OP.is_equal))
                k.dve.inc(k.V.tensor_tensor(m32[PTs, :], m32[PTs, :],
                                            tmx[PTs, :], OP.max))
            k.dve.inc(k.V.tensor_tensor(tmx[PTs, :], stm[PTs, :],
                                        m32[PTs, :], OP.mult))
            k.dve.inc(k.V.tensor_scalar(m32[PTs, :], m32[PTs, :],
                                        -FLOOR_BIG, FLOOR_BIG, OP.mult,
                                        OP.add))
            k.dve.inc(k.V.tensor_tensor(tmx[PTs, :], tmx[PTs, :],
                                        m32[PTs, :], OP.add))
            k.dve.inc(k.V.max(v8[PTs, :], tmx[PTs, :]))
            k.dve.inc(k.V.max_index(i8[PTs, :], v8[PTs, :], tmx[PTs, :]))
            k.dve.inc(k.V.tensor_copy(i8f[PTs, :], i8[PTs, :]))
            k.dve.inc(k.V.tensor_reduce(k.scr[PTs, 2:3], v8[PTs, 0:TK],
                                        AX.X, OP.add))
            k.dve.inc(k.V.tensor_scalar(k.scr[PTs, 2:3], k.scr[PTs, 2:3],
                                        1e-9, None, OP.add))
            k.dve.inc(k.V.reciprocal(k.scr[PTs, 3:4], k.scr[PTs, 2:3]))
            k.dve.inc(k.V.tensor_scalar(wna[PTs, a * TK:(a + 1) * TK],
                                        v8[PTs, 0:TK], k.scr[PTs, 3:4],
                                        None, OP.mult))
            k.dve.inc(k.V.tensor_copy(idxa[PTs, a * TK:(a + 1) * TK],
                                      i8f[PTs, 0:TK]))
            k.dve.inc(k.V.memset(wdt[:], 0.0))
            for kk in range(TK):
                k.dve.inc(k.V.tensor_scalar(tmx[PTs, :], io32[PTs, :],
                                            i8f[PTs, kk:kk + 1], None,
                                            OP.is_equal))
                k.dve.inc(k.V.scalar_tensor_tensor(
                    wdt[PTs, :], tmx[PTs, :],
                    wna[PTs, a * TK + kk:a * TK + kk + 1], wdt[PTs, :],
                    OP.mult, OP.add))
            k.w(nc.tensor, k.dve)
            k.pe.inc(nc.tensor.transpose(k.w4b[0:32, 0:128], wdt[:],
                                         k.identb[:]))
            k.w(nc.vector, k.pe)
            k.dve.inc(k.V.tensor_copy(k.sq[0:32, 1024:1024 + PT],
                                      k.w4b[0:32, 0:PT]))
            k.w(nc.sync, k.dve, k.dma)
            nc.sync.dma_start(out=wf_pay[:, a * PT:(a + 1) * PT],
                              in_=k.sq[0:32, 1024:1024 + PT]).then_inc(
                k.dma.sem, 16)
            k.dma.v += 16
            for s in range(NS):
                k.w(nc.tensor, k.dve)
                k.pe.inc(nc.tensor.transpose(
                    k.w4b[0:PT, s * 128:(s + 1) * 128], xff[:, sl(s * T)],
                    k.identb[:]))
            k.w(nc.vector, k.pe)
            k.dve.inc(k.V.tensor_copy(k.sq[0:PT, 0:1024],
                                      k.w4b[0:PT, 0:1024]))
            k.w(nc.sync, k.dve)
            nc.sync.dma_start(out=xf_pay[a * PT:(a + 1) * PT, :],
                              in_=k.sq[0:PT, 0:1024]).then_inc(k.dma.sem,
                                                               16)
            k.dma.v += 16
        k.wall(k.dma)

        p1.close()
        if lim <= 4:
            return

        k.cc.inc(nc.gpsimd.collective_compute(
            "AllGather", OP.bypass, replica_groups=[list(range(NC))],
            ins=[xf_pay], outs=[xf_all]))
        k.cc.inc(nc.gpsimd.collective_compute(
            "AllGather", OP.bypass, replica_groups=[list(range(NC))],
            ins=[wf_pay], outs=[wf_all]))
        k.wall(k.cc)
        if lim <= 5:
            return

        emit_moe_scatter(k, win, wf_all, idxa, wna, slot_hbm, pos_hbm)
        k.w(nc.gpsimd, k.dma, k.dmag)
        k.cc.inc(nc.gpsimd.collective_compute(
            "AllReduce", OP.add, replica_groups=[list(range(NC))],
            ins=[slot_hbm], outs=[slot_all]))
        k.wall(k.cc)
        if lim <= 6:
            return

        emit_moe_expert(k, win, xf_all, slot_all, eo_pay, eo_all,
                        wbh)
        if lim <= 7:
            return
        k.wall(k.cc)
        if lim <= 8:
            return

        emit_moe_origin(k, win, eo_all, xr_tm, z_hbm, y_out, idxa, wna,
                        pos_hbm)


def emit_moe_scatter(k, win, wf_all, idxa, wna, slot_hbm, pos_hbm):
    nc = k.nc
    T = k.T
    NC = k.NC
    N = NC * T
    E, TK, NG = 32, 4, 4
    CAP = max(1, int(1.25 * N * TK / E))
    CAPF = -(-CAP // 128) * 128
    NF16 = N // 16
    PT = T // 16
    PTs = slice(0, PT)
    SLOTC = (E + 4) * CAPF // 128

    sc = contextlib.ExitStack()
    with sc:
        wgrp = k.sb("s_wgrp", [128, NF16], BF16, sc)
        mgrp = k.sb("s_mgrp", [128, NF16], F32, sc)
        csum = k.sb("s_csum", [8, NF16], F32, sc)
        cs2 = k.sb("s_cs2", [8, NF16], F32, sc)
        ones8 = k.sb("s_ones8", [8, NF16], F32, sc)
        pos_all = k.sb("s_pos", [128, NG * NF16], F32, sc)
        pos_own = k.sb("s_posown", [128, NG * PT], F32, sc)
        myf0 = k.sb("s_myf0", [1, 1], U32, sc)
        posTM = k.sb("s_posTM", [128, 32], F32, sc)
        iot32 = k.sb("s_iot32", [128, 32], F32, sc)
        iotp = k.sb("s_iotp", [128, 1], F32, sc)
        rowb = k.sb("s_rowb", [128, 16], F32, sc)
        eq = k.sb("s_eq", [128, TK * 32], F32, sc)
        pek = k.sb("s_pek", [128, TK], F32, sc)
        keep = k.sb("s_keep", [128, TK], F32, sc)
        sidx = k.sb("s_sidx", [128, TK], F32, sc)
        tb = k.sb("s_tb", [128, TK], F32, sc)
        sidxu = k.sb("s_sidxu", [128, TK], U32, sc)
        zs = k.sb("s_zs", [128, SLOTC], F32, sc)

        nc.sync.dma_start(out=myf0[:], in_=win["moe_myf0"]).then_inc(
            k.dma.sem, 16)
        k.dma.v += 16
        nc.sync.dma_start(out=rowb[:], in_=win["moe_rowb"]).then_inc(
            k.dma.sem, 16)
        k.dma.v += 16
        nc.sync.wait_ge(k.dma.sem, k.dma.v)
        k.w(nc.gpsimd, k.dma, k.cc)
        k.pool.inc(k.G.iota(iot32[:], pattern=[[1, 32]], base=0,
                            channel_multiplier=0,
                            allow_small_or_imprecise_dtypes=True))
        k.pool.inc(k.G.iota(iotp[:], pattern=[[0, 1]], base=0,
                            channel_multiplier=1,
                            allow_small_or_imprecise_dtypes=True))
        k.dve.inc(k.V.memset(ones8[:], 1.0))
        k.dve.inc(k.V.memset(zs[:], 0.0))
        for kk in range(TK):
            k.w(nc.vector, k.pool)
            k.dve.inc(k.V.tensor_scalar(tb[:, kk:kk + 1], iotp[:],
                                        float(E * CAPF + kk * 128), None,
                                        OP.add))
        k.w(nc.sync, k.dve)
        nc.sync.dma_start(
            out=slot_hbm.rearrange("(p r) one -> p (r one)", p=128),
            in_=zs[:]).then_inc(k.dma.sem, 16)
        k.dma.v += 16

        for g in range(NG):
            k.w(nc.sync, k.dve, k.pe, k.dma, k.cc)
            nc.sync.dma_start(
                out=wgrp[:],
                in_=wf_all.rearrange("(c e) t -> e c t", e=E)
                [g * 8:(g + 1) * 8, :, :]
                .rearrange("e c (pp p) -> (e pp) c p", pp=16),
            ).then_inc(k.dma.sem, 16)
            k.dma.v += 16
            k.wall(k.dma)
            k.dve.inc(k.V.tensor_scalar(mgrp[:], wgrp[:], 0.0, None,
                                        OP.is_gt))
            k.w(nc.tensor, k.dve, k.act)
            for nn in range(0, NF16, 512):
                w = min(512, NF16 - nn)
                k.w(nc.tensor, k.dve)
                nc.tensor.matmul(k.w1[:, 0:w], k.su16f[:],
                                 mgrp[:, nn:nn + w], start=True, stop=True)
                mm = nc.tensor.matmul(k.w2[0:8, 0:w], k.esum8f[:],
                                      mgrp[:, nn:nn + w], start=True,
                                      stop=True)
                k.pe.inc(mm)
                k.w(nc.vector, k.pe)
                k.dve.inc(k.V.tensor_copy(csum[0:8, nn:nn + w],
                                          k.w2[0:8, 0:w]))
                k.dve.inc(k.V.tensor_copy(
                    pos_all[:, g * NF16 + nn:g * NF16 + nn + w],
                    k.w1[:, 0:w]))
            k.w(nc.vector, k.dve)
            k.dve.inc(k.V.tensor_tensor_scan(cs2[0:8, :], ones8[0:8, :],
                                             csum[0:8, :], 0.0, OP.mult,
                                             OP.add))
            k.w(nc.vector, k.dve)
            k.dve.inc(k.V.tensor_tensor(cs2[0:8, :], cs2[0:8, :],
                                        csum[0:8, :], OP.subtract))
            for nn in range(0, NF16, 512):
                w = min(512, NF16 - nn)
                k.w(nc.tensor, k.dve)
                k.pe.inc(nc.tensor.matmul(k.w1[:, 0:w], k.exp8f[:],
                                          cs2[0:8, nn:nn + w], start=True,
                                          stop=True))
                k.w(nc.vector, k.pe)
                k.dve.inc(k.V.tensor_tensor(
                    pos_all[:, g * NF16 + nn:g * NF16 + nn + w],
                    pos_all[:, g * NF16 + nn:g * NF16 + nn + w],
                    k.w1[:, 0:w], OP.add))
        k.w(nc.sync, k.dve)
        nc.sync.dma_start(out=pos_hbm, in_=pos_all[:]).then_inc(k.dma.sem,
                                                                16)
        k.dma.v += 16
        nc.sync.wait_ge(k.dma.sem, k.dma.v)
        k.w(nc.gpsimd, k.dma)
        for g in range(NG):
            f0g = nc.gpsimd.value_load(myf0[0:1, 0:1])
            k.dmag.inc(nc.gpsimd.dma_start(
                out=pos_own[:, g * PT:(g + 1) * PT],
                in_=pos_hbm[:, bass.ds(f0g + g * NF16, PT)]), 16)
        k.wall(k.dma, k.dmag)

        for a in range(16):
            k.w(nc.tensor, k.dve, k.act)
            for g in range(NG):
                mm = nc.tensor.matmul(
                    k.w1[0:32, 0:PT],
                    k.collf[:, (a * NG + g) * 32:(a * NG + g) * 32 + 32],
                    pos_own[:, g * PT:(g + 1) * PT],
                    start=(g == 0), stop=(g == NG - 1))
            k.pe.inc(mm)
            k.w(nc.vector, k.pe)
            k.dve.inc(k.V.tensor_copy(k.scr2[0:32, 0:PT], k.w1[0:32, 0:PT]))
            k.w(nc.tensor, k.dve)
            k.pe.inc(nc.tensor.transpose(k.w1[0:PT, 0:32],
                                         k.scr2[0:32, 0:PT],
                                         k.identf[0:32, 0:32]))
            k.w(nc.vector, k.pe)
            k.dve.inc(k.V.tensor_copy(posTM[PTs, :], k.w1[0:PT, 0:32]))
            eq4 = eq[PTs, :].rearrange("p (kk e) -> p kk e", kk=TK)
            io4 = iot32[PTs, :].unsqueeze(1).broadcast_to([PT, TK, 32])
            idx4c = idxa[PTs, a * TK:(a + 1) * TK]
            idx4 = idx4c.unsqueeze(2).broadcast_to([PT, TK, 32])
            ptm4 = posTM[PTs, :].unsqueeze(1).broadcast_to([PT, TK, 32])
            k.w(nc.vector, k.dmag)
            k.dve.inc(k.V.tensor_tensor(eq4, io4, idx4, OP.is_equal))
            k.w(nc.vector, k.dve)
            k.dve.inc(k.V.tensor_tensor(eq4, eq4, ptm4, OP.mult))
            k.w(nc.vector, k.dve)
            k.dve.inc(k.V.tensor_reduce(pek[PTs, 0:TK], eq4, AX.X, OP.add))
            k.w(nc.vector, k.dve)
            k.dve.inc(k.V.tensor_scalar(keep[PTs, :], pek[PTs, :],
                                        float(CAP), None, OP.is_lt))
            k.w(nc.vector, k.dve)
            k.dve.inc(k.V.tensor_scalar(sidx[PTs, :], idx4c, float(CAPF),
                                        None, OP.mult))
            k.w(nc.vector, k.dve)
            k.dve.inc(k.V.tensor_tensor(sidx[PTs, :], sidx[PTs, :],
                                        pek[PTs, :], OP.add))
            k.w(nc.vector, k.dve)
            k.dve.inc(k.V.tensor_tensor(sidx[PTs, :], sidx[PTs, :],
                                        tb[PTs, :], OP.subtract))
            k.w(nc.vector, k.dve)
            k.dve.inc(k.V.tensor_tensor(sidx[PTs, :], sidx[PTs, :],
                                        keep[PTs, :], OP.mult))
            k.w(nc.vector, k.dve)
            k.dve.inc(k.V.tensor_tensor(sidx[PTs, :], sidx[PTs, :],
                                        tb[PTs, :], OP.add))
            k.w(nc.vector, k.dve)
            k.dve.inc(k.V.tensor_copy(sidxu[PTs, :], sidx[PTs, :]))
            k.w(nc.gpsimd, k.dve, k.dma)
            for kk in range(TK):
                k.dmag.inc(nc.gpsimd.indirect_dma_start(
                    out=slot_hbm, out_offset=bass.IndirectOffsetOnAxis(
                        sidxu[PTs, kk:kk + 1], axis=0),
                    in_=rowb[PTs, a:a + 1], in_offset=None), 16)
        k.wall(k.dmag)


def emit_moe_expert(k, win, xf_all, slot_all, eo_pay, eo_all, wbh):
    nc = k.nc
    T = k.T
    NC = k.NC
    N = NC * T
    E, TK = 32, 4
    EPB = E // NC
    CAP = max(1, int(1.25 * N * TK / E))
    CAPF = -(-CAP // 128) * 128
    CHKW = min(512, CAPF)
    NCHK = CAPF // CHKW
    NSUB = CHKW // 128

    NRE = CAPF // 128
    assert EPB * NRE <= 128

    sc = contextlib.ExitStack()
    with sc:
        myeb = k.sb("e_myeb", [1, 1], U32, sc)
        myebf = k.sb("e_myebf", [1, 1], F32, sc)
        iotc = k.sb("e_iotc", [128, 1], F32, sc)
        bcf = k.sb("e_bcf", [128, 1], F32, sc)
        idxg = k.sb("e_idxg", [128, 1], U32, sc)
        svr = k.sb("e_svr", [128, 128], F32, sc)
        sv = k.sb("e_sv", [128, 128], F32, sc)
        NQ = EPB * NRE
        tidf20 = k.sb("e_tidf", [128, NQ], F32, sc)
        tid20 = k.sb("e_tidu", [128, NQ], U32, sc)
        xg2 = k.sb("e_xg", [128, 4 * D], BF16, sc)
        we13 = k.sb("e_w13", [128, 2 * 2 * 64 * 128], BF16, sc)
        we2 = k.sb("e_w2", [128, 2 * 64 * 128], BF16, sc)
        xeT = k.sb("e_xeT", [128, NS * CHKW], BF16, sc)
        he = k.sb("e_he", [128, NS * CHKW], BF16, sc)
        osl = k.sb("e_osl", [128, 2 * (CHKW // 128) * D], BF16, sc)

        nc.sync.dma_start(out=myeb[:], in_=win["moe_myeb"]).then_inc(
            k.dma.sem, 16)
        k.dma.v += 16
        slotv = slot_all.rearrange("(r p) one -> r (p one)", p=128)
        xeT3 = xeT[:].rearrange("p (s q) -> p s q", s=NS)
        w4b3 = k.w4b[:].rearrange("p (s q) -> p s q", s=NS)

        k.w(nc.vector, k.dma, k.cc)
        k.dve.inc(k.V.tensor_copy(myebf[:], myeb[:]))
        k.pool.inc(k.G.iota(iotc[:], pattern=[[0, 1]], base=0,
                            channel_multiplier=1,
                            allow_small_or_imprecise_dtypes=True))
        k.w(nc.gpsimd, k.pool)
        k.pool.inc(k.G.affine_select(
            out=iotc[:], in_=iotc[:], compare_op=OP.is_ge,
            fill=float(EPB * NRE - 1), base=EPB * NRE - 1,
            pattern=[[0, 1]], channel_multiplier=-1))
        k.w(nc.tensor, k.dve, k.act)
        k.pe.inc(nc.tensor.matmul(k.w2[:, 0:1], k.ones128f[0:1, 0:128],
                                  myebf[0:1, 0:1], start=True, stop=True))
        k.w(nc.vector, k.pe, k.pool)
        k.dve.inc(k.V.tensor_scalar(bcf[:], k.w2[:, 0:1], float(NRE), None,
                                    OP.mult))
        k.w(nc.vector, k.dve)
        k.dve.inc(k.V.tensor_tensor(bcf[:], bcf[:], iotc[:], OP.add))
        k.w(nc.vector, k.dve)
        k.dve.inc(k.V.tensor_copy(idxg[:], bcf[:]))
        k.w(nc.gpsimd, k.dve, k.cc)
        k.dmag.inc(nc.gpsimd.indirect_dma_start(
            out=svr[:], out_offset=None, in_=slotv,
            in_offset=bass.IndirectOffsetOnAxis(idxg[:, 0:1], axis=0)), 16)
        nc.tensor.wait_ge(k.dmag.sem, k.dmag.v)
        k.w(nc.tensor, k.dve)
        k.pe.inc(nc.tensor.transpose(k.w1[:, 0:128], svr[:], k.identf[:]))
        k.w(nc.vector, k.pe)
        k.dve.inc(k.V.tensor_copy(sv[:], k.w1[:, 0:128]))

        k.w(nc.vector, k.dve)
        k.dve.inc(k.V.tensor_scalar(tidf20[:, 0:NQ], sv[:, 0:NQ], -1.0,
                                    None, OP.add))
        k.w(nc.vector, k.dve)
        k.dve.inc(k.V.tensor_scalar_max(tidf20[:, 0:NQ], tidf20[:, 0:NQ],
                                        0.0))
        k.w(nc.vector, k.dve)
        k.dve.inc(k.V.tensor_copy(tid20[:, 0:NQ], tidf20[:, 0:NQ]))

        gv = {}

        def issue_gather(q):
            if q >= NQ:
                return
            ctr = k.gat[q % 4]
            k.w(nc.gpsimd, k.dve, k.pe, k.cc)
            ctr.inc(nc.gpsimd.indirect_dma_start(
                out=xg2[:, (q % 4) * D:(q % 4) * D + D], out_offset=None,
                in_=xf_all,
                in_offset=bass.IndirectOffsetOnAxis(tid20[:, q:q + 1],
                                                    axis=0)), 16)
            gv[q] = (ctr, ctr.v)

        for _q0 in range(4):
            issue_gather(_q0)

        k.hb_cfg = True
        wb13_hbm, wb2_hbm = wbh
        nc.sync.wait_ge(k.wcast.sem, k.wcast.v)
        k.w(nc.sync, k.pe, k.dve)
        nc.sync.dma_start(out=we13[:, 0:16384],
                          in_=wb13_hbm[:, 0:16384]).then_inc(k.we_e.sem, 16)
        nc.sync.dma_start(out=we2[:, 0:8192],
                          in_=wb2_hbm[:, 0:8192]).then_inc(k.we_e.sem, 16)
        k.we_e.v += 32
        wl = {0: (k.we_e, k.we_e.v)}
        ov = {}

        for j in range(EPB):
            jb = j % 2
            wb13 = we13[:, jb * 16384:jb * 16384 + 16384]
            wb2 = we2[:, jb * 8192:jb * 8192 + 8192]
            for ch in range(NCHK):
                for sb_ in range(NSUB):
                    q = j * NRE + ch * NSUB + sb_
                    nc.tensor.wait_ge(gv[q][0].sem, gv[q][1])
                    k.w(nc.tensor, k.dve)
                    for s in range(NS):
                        mm = nc.tensor.transpose(
                            k.w4b[:, s * 128:(s + 1) * 128],
                            xg2[:, (q % 4) * D + s * 128:
                                (q % 4) * D + (s + 1) * 128], k.identb[:])
                    k.pe.inc(mm)
                    k.w(nc.vector, k.pe)
                    k.dve.inc(k.V.tensor_copy(
                        xeT3[:, :, sb_ * 128:(sb_ + 1) * 128], w4b3[:]))
                    issue_gather(q + 4)
                nc.tensor.wait_ge(k.dve.sem, k.dve.v)
                if ch == 0:
                    nc.tensor.wait_ge(wl[j][0].sem, wl[j][1])

                def e_consume(p):
                    mo_, gb_, ub_, scr_ = p
                    k.w(nc.scalar, k.pe)
                    k.act.inc(k.A.activation(scr_, gb_, AF.Silu))
                    k.w(nc.vector, k.pe, k.act)
                    k.dve.inc(k.V.tensor_tensor(
                        he[:, mo_ * CHKW:(mo_ + 1) * CHKW], ub_, scr_,
                        OP.mult))

                pend = None
                for mo in range(NS):
                    gb = (k.w1 if mo % 2 == 0 else k.w3)[:, 0:CHKW]
                    ub = (k.w2 if mo % 2 == 0 else k.S0)[:, 0:CHKW]
                    scrh = (k.scr if mo % 2 == 0 else k.scr2)[:, 0:CHKW]
                    k.w(nc.tensor, k.dve, k.act)
                    for ki in range(NS):
                        nc.tensor.matmul(
                            gb,
                            wb13[:, (ki * NS + mo) * 128:
                                 (ki * NS + mo) * 128 + 128],
                            xeT[:, ki * CHKW:(ki + 1) * CHKW],
                            start=(ki == 0), stop=(ki == NS - 1))
                        mm = nc.tensor.matmul(
                            ub,
                            wb13[:, 8192 + (ki * NS + mo) * 128:
                                 8192 + (ki * NS + mo) * 128 + 128],
                            xeT[:, ki * CHKW:(ki + 1) * CHKW],
                            start=(ki == 0), stop=(ki == NS - 1))
                    k.pe.inc(mm)
                    if pend is not None:
                        e_consume(pend)
                    pend = (mo, gb, ub, scrh)
                e_consume(pend)

                ci = j * NCHK + ch
                ob = (ci % 2) * (CHKW // 128) * D

                def o_consume(p):
                    st_, b0_, b1_ = p
                    if ci - 2 in ov:
                        nc.vector.wait_ge(ov[ci - 2][0].sem, ov[ci - 2][1])
                        nc.scalar.wait_ge(ov[ci - 2][0].sem, ov[ci - 2][1])
                    k.w(nc.vector, k.pe)
                    k.dve.inc(k.V.tensor_copy(
                        osl[:, ob + st_ * D:ob + st_ * D + 512],
                        b0_[:, 0:512]))
                    k.w(nc.scalar, k.pe)
                    k.act.inc(k.A.copy(
                        osl[:, ob + st_ * D + 512:ob + (st_ + 1) * D],
                        b1_[:, 0:512]))

                pend = None
                for st in range(CHKW // 128):
                    b0 = (k.w2 if st % 2 == 0 else k.S0[:, 0:512])
                    b1 = (k.w3 if st % 2 == 0 else k.S0[:, 512:1024])
                    k.w(nc.tensor, k.dve, k.act)
                    for half in range(2):
                        for ki in range(NS):
                            mm = nc.tensor.matmul(
                                (b0 if half == 0 else b1)[:, 0:512],
                                he[:, ki * CHKW + st * 128:
                                   ki * CHKW + st * 128 + 128],
                                wb2[:, ki * 1024 + half * 512:
                                    ki * 1024 + half * 512 + 512],
                                start=(ki == 0), stop=(ki == NS - 1))
                    k.pe.inc(mm)
                    if pend is not None:
                        o_consume(pend)
                    pend = (st, b0, b1)
                o_consume(pend)
                ectr = k.eo_e if ci % 2 == 0 else k.eo_o
                k.w(nc.sync, k.dve, k.act)
                nc.sync.dma_start(
                    out=eo_pay[j * CAPF + ch * CHKW:
                               j * CAPF + (ch + 1) * CHKW, :]
                    .rearrange("(st p) d -> p st d", p=128),
                    in_=osl[:, ob:ob + (CHKW // 128) * D]
                    .rearrange("p (st d) -> p st d", st=CHKW // 128),
                ).then_inc(ectr.sem, 16)
                ectr.v += 16
                ov[ci] = (ectr, ectr.v)
                if ch == 0 and j + 1 < EPB:
                    nj = (j + 1) % 2
                    wctr = k.we_e if (j + 1) % 2 == 0 else k.we_o
                    k.w(nc.sync, k.pe)
                    nc.sync.dma_start(
                        out=we13[:, nj * 16384:nj * 16384 + 16384],
                        in_=wb13_hbm[:, (j + 1) * 16384:
                                     (j + 2) * 16384]).then_inc(
                        wctr.sem, 16)
                    nc.sync.dma_start(
                        out=we2[:, nj * 8192:nj * 8192 + 8192],
                        in_=wb2_hbm[:, (j + 1) * 8192:
                                    (j + 2) * 8192]).then_inc(wctr.sem, 16)
                    wctr.v += 32
                    wl[j + 1] = (wctr, wctr.v)
            nc.gpsimd.wait_ge(k.eo_e.sem, k.eo_e.v)
            nc.gpsimd.wait_ge(k.eo_o.sem, k.eo_o.v)
            k.cc.inc(nc.gpsimd.collective_compute(
                "AllGather", OP.bypass, replica_groups=[list(range(NC))],
                ins=[eo_pay[j * CAPF:(j + 1) * CAPF, :]],
                outs=[eo_all[j * NC * CAPF:(j + 1) * NC * CAPF, :]]))
        k.hb_cfg = None
        k.wall(k.dma, k.eo_e, k.eo_o, k.we_e, k.we_o)


def emit_moe_origin(k, win, eo_all, xr_tm, z_hbm, y_out, idxa, wna,
                    pos_hbm):
    nc = k.nc
    T = k.T
    NC = k.NC
    N = NC * T
    E, TK = 32, 4
    EPB = E // NC
    CAP = max(1, int(1.25 * N * TK / E))
    CAPF = -(-CAP // 128) * 128
    NF16 = N // 16
    PT = T // 16
    NG = 4
    PTs = slice(0, PT)

    sc = contextlib.ExitStack()
    with sc:
        pos_own = k.sb("o_posown", [128, NG * PT], F32, sc)
        myf0 = k.sb("o_myf0", [1, 1], U32, sc)
        posTM = k.sb("o_posTM", [128, 32], F32, sc)
        iot32 = k.sb("o_iot32", [128, 32], F32, sc)
        eq = k.sb("o_eq", [128, TK * 32], F32, sc)
        pek = k.sb("o_pek", [128, TK], F32, sc)
        keep = k.sb("o_keep", [128, TK], F32, sc)
        wt4 = k.sb("o_wt4", [128, TK], F32, sc)
        gbi = k.sb("o_gbi", [128, TK], F32, sc)
        ii32 = k.sb("o_ii32", [128, TK], I32, sc)
        iqf = k.sb("o_iqf", [128, TK], F32, sc)
        gbu = k.sb("o_gbu", [128, TK], U32, sc)
        gbrow = k.sb("o_gbrow", [128, TK * D], BF16, sc)
        acc = k.sb("o_acc", [128, D], F32, sc)
        ztm = k.sb("o_ztm", [128, D], F32, sc)
        x3t = k.sb("o_x3t", [128, D], F32, sc)
        sq2 = k.sb("o_sq2", [128, D], F32, sc)

        k.hb_cfg = True
        nc.sync.dma_start(out=myf0[:], in_=win["moe_myf0"]).then_inc(
            k.dma.sem, 16)
        k.dma.v += 16
        nc.sync.wait_ge(k.dma.sem, k.dma.v)
        k.w(nc.gpsimd, k.dma)
        k.pool.inc(k.G.iota(iot32[:], pattern=[[1, 32]], base=0,
                            channel_multiplier=0,
                            allow_small_or_imprecise_dtypes=True))
        for g in range(NG):
            f0g = nc.gpsimd.value_load(myf0[0:1, 0:1])
            k.dmag.inc(nc.gpsimd.dma_start(
                out=pos_own[:, g * PT:(g + 1) * PT],
                in_=pos_hbm[:, bass.ds(f0g + g * NF16, PT)]), 16)
        k.wall(k.dma, k.dmag)

        for a in range(16):
            k.w(nc.tensor, k.dve, k.act)
            for g in range(NG):
                mm = nc.tensor.matmul(
                    k.w1[0:32, 0:PT],
                    k.collf[:, (a * NG + g) * 32:(a * NG + g) * 32 + 32],
                    pos_own[:, g * PT:(g + 1) * PT],
                    start=(g == 0), stop=(g == NG - 1))
            k.pe.inc(mm)
            k.w(nc.vector, k.pe)
            k.dve.inc(k.V.tensor_copy(k.scr2[0:32, 0:PT], k.w1[0:32, 0:PT]))
            k.w(nc.tensor, k.dve)
            k.pe.inc(nc.tensor.transpose(k.w1[0:PT, 0:32],
                                         k.scr2[0:32, 0:PT],
                                         k.identf[0:32, 0:32]))
            k.w(nc.vector, k.pe)
            k.dve.inc(k.V.tensor_copy(posTM[PTs, :], k.w1[0:PT, 0:32]))
            k.w(nc.sync, k.dve, k.dma)
            nc.sync.dma_start(out=ztm[PTs, 0:D],
                              in_=z_hbm[a * PT:(a + 1) * PT, :]).then_inc(
                k.dma.sem, 16)
            k.dma.v += 16
            nc.sync.dma_start(
                out=x3t[PTs, 0:D],
                in_=xr_tm.rearrange("(f q) d -> f q d", q=16)[:, a, :]
            ).then_inc(k.dma.sem, 16)
            k.dma.v += 16
            eq4 = eq[PTs, :].rearrange("p (kk e) -> p kk e", kk=TK)
            io4 = iot32[PTs, :].unsqueeze(1).broadcast_to([PT, TK, 32])
            idx4c = idxa[PTs, a * TK:(a + 1) * TK]
            idx4 = idx4c.unsqueeze(2).broadcast_to([PT, TK, 32])
            ptm4 = posTM[PTs, :].unsqueeze(1).broadcast_to([PT, TK, 32])
            k.dve.inc(k.V.tensor_tensor(eq4, io4, idx4, OP.is_equal))
            k.w(nc.vector, k.dve)
            k.dve.inc(k.V.tensor_tensor(eq4, eq4, ptm4, OP.mult))
            k.w(nc.vector, k.dve)
            k.dve.inc(k.V.tensor_reduce(pek[PTs, 0:TK], eq4, AX.X, OP.add))
            k.w(nc.vector, k.dve)
            k.dve.inc(k.V.tensor_scalar(keep[PTs, :], pek[PTs, :],
                                        float(CAP), None, OP.is_lt))
            k.w(nc.vector, k.dve)
            k.dve.inc(k.V.tensor_tensor(
                wt4[PTs, 0:TK], wna[PTs, a * TK:(a + 1) * TK],
                keep[PTs, :], OP.mult))
            EPBL = EPB.bit_length() - 1
            k.dve.inc(k.V.tensor_copy(ii32[PTs, :], idx4c))
            k.w(nc.vector, k.dve)
            k.dve.inc(k.V.tensor_scalar(ii32[PTs, :], ii32[PTs, :], EPBL,
                                        None, OP.arith_shift_right))
            k.w(nc.vector, k.dve)
            k.dve.inc(k.V.tensor_copy(iqf[PTs, :], ii32[PTs, :]))
            k.w(nc.vector, k.dve)
            k.dve.inc(k.V.scalar_tensor_tensor(
                gbi[PTs, :], iqf[PTs, :], float(-EPB), idx4c,
                OP.mult, OP.add))
            k.w(nc.vector, k.dve)
            k.dve.inc(k.V.tensor_scalar(gbi[PTs, :], gbi[PTs, :],
                                        float(NC * CAPF), None, OP.mult))
            k.w(nc.vector, k.dve)
            k.dve.inc(k.V.scalar_tensor_tensor(
                gbi[PTs, :], iqf[PTs, :], float(CAPF), gbi[PTs, :],
                OP.mult, OP.add))
            k.w(nc.vector, k.dve)
            k.dve.inc(k.V.tensor_tensor(gbi[PTs, :], gbi[PTs, :],
                                        pek[PTs, :], OP.add))
            k.w(nc.vector, k.dve)
            k.dve.inc(k.V.tensor_tensor(gbi[PTs, :], gbi[PTs, :],
                                        keep[PTs, :], OP.mult))
            k.w(nc.vector, k.dve)
            k.dve.inc(k.V.tensor_copy(gbu[PTs, 0:TK], gbi[PTs, :]))
            k.w(nc.vector, k.dve)
            k.w(nc.gpsimd, k.dve, k.cc)
            for kk in range(TK):
                k.dmag.inc(nc.gpsimd.indirect_dma_start(
                    out=gbrow[PTs, kk * D:(kk + 1) * D], out_offset=None,
                    in_=eo_all,
                    in_offset=bass.IndirectOffsetOnAxis(
                        gbu[PTs, kk:kk + 1], axis=0)), 16)
            nc.vector.wait_ge(k.dmag.sem, k.dmag.v)
            k.w(nc.vector, k.dma)
            k.dve.inc(k.V.tensor_scalar(acc[PTs, :],
                                        gbrow[PTs, 0:D],
                                        wt4[PTs, 0:1], None, OP.mult))
            for kk in range(1, TK):
                k.w(nc.vector, k.dve)
                k.dve.inc(k.V.scalar_tensor_tensor(
                    acc[PTs, :], gbrow[PTs, kk * D:(kk + 1) * D],
                    wt4[PTs, kk:kk + 1], acc[PTs, :], OP.mult, OP.add))
            k.w(nc.vector, k.dve)
            k.dve.inc(k.V.tensor_tensor(acc[PTs, :], acc[PTs, :],
                                        ztm[PTs, 0:D], OP.add))
            k.w(nc.vector, k.dve)
            k.dve.inc(k.V.tensor_reduce(k.scr[PTs, 0:1], acc[PTs, :], AX.X,
                                        OP.add))
            k.w(nc.scalar, k.dve)
            k.act.inc(k.A.activation(sq2[PTs, 0:D], acc[PTs, :],
                                     AF.Square))
            k.w(nc.vector, k.act)
            k.dve.inc(k.V.tensor_reduce(k.scr[PTs, 1:2], sq2[PTs, 0:D],
                                        AX.X, OP.add))
            k.w(nc.vector, k.dve)
            k.dve.inc(k.V.tensor_scalar(k.scr[PTs, 0:1], k.scr[PTs, 0:1],
                                        1.0 / D, None, OP.mult))
            k.dve.inc(k.V.tensor_scalar(k.scr[PTs, 1:2], k.scr[PTs, 1:2],
                                        1.0 / D, None, OP.mult))
            k.w(nc.vector, k.dve)
            k.dve.inc(k.V.tensor_tensor(k.scr[PTs, 2:3], k.scr[PTs, 0:1],
                                        k.scr[PTs, 0:1], OP.mult))
            k.w(nc.vector, k.dve)
            k.dve.inc(k.V.tensor_tensor(k.scr[PTs, 1:2], k.scr[PTs, 1:2],
                                        k.scr[PTs, 2:3], OP.subtract))
            k.rsqrt(k.scr[PTs, 1:2], 1e-5)
            k.w(nc.vector, k.act)
            k.dve.inc(k.V.tensor_scalar(acc[PTs, :], acc[PTs, :],
                                        k.scr[PTs, 0:1], None, OP.subtract))
            k.w(nc.vector, k.dve)
            k.dve.inc(k.V.tensor_scalar(acc[PTs, :], acc[PTs, :],
                                        k.scr[PTs, 1:2], None, OP.mult))
            k.w(nc.vector, k.dve)
            k.dve.inc(k.V.tensor_tensor(acc[PTs, :], acc[PTs, :],
                                        x3t[PTs, 0:D], OP.add))
            k.w(nc.sync, k.dve)
            nc.sync.dma_start(
                out=y_out.rearrange("(f q) d -> f q d", q=16)[:, a, :],
                in_=acc[PTs, 0:D]).then_inc(k.dma.sem, 16)
            k.dma.v += 16
        k.hb_cfg = None
        k.wall(k.dma)


def emit_softmax_groups(k, lg, ptm, probs, nrows, ngrp, grp, floor,
                        floor_mode, lgh=None):
    nc = k.nc
    ntile = SEG // 128
    k.w(nc.tensor, k.dve, k.act)
    for tt in range(ntile):
        if lgh is None:
            mm = nc.tensor.transpose(k.w1[:, tt * nrows:(tt + 1) * nrows],
                                     lg[0:nrows, tt * 128:(tt + 1) * 128],
                                     k.identf[0:nrows, 0:nrows])
        else:
            for h in range(ngrp):
                mm = nc.tensor.transpose(
                    k.w1[:, tt * nrows + h * grp:tt * nrows + (h + 1) * grp],
                    lgh[32 * h:32 * h + grp, tt * 128:(tt + 1) * 128],
                    k.identf[32 * h:32 * h + grp, 32 * h:32 * h + grp],
                    tile_position=(32 * h, 0))
    k.pe.inc(mm)
    k.w(nc.vector, k.pe)
    k.dve.inc(k.V.tensor_copy(ptm[:, 0:ntile * nrows],
                                    k.w1[:, 0:ntile * nrows]))
    k.w(nc.vector, k.dve)
    for tt in range(ntile):
        tp = ptm[:, tt * nrows:(tt + 1) * nrows]
        tp3 = tp.rearrange("p (g j) -> p g j", g=ngrp)
        mx = k.scr[:, 0:ngrp]
        k.dve.inc(k.V.tensor_reduce(mx, tp3, AX.X, OP.max))
        k.w(nc.vector, k.dve)
        for g in range(ngrp):
            k.dve.inc(k.V.tensor_scalar(
                tp[:, g * grp:(g + 1) * grp], tp[:, g * grp:(g + 1) * grp],
                mx[:, g:g + 1], None, OP.subtract))
        k.w(nc.scalar, k.dve)
        k.act.inc(k.A.activation(tp, tp, AF.Exp))
        k.w(nc.vector, k.act)
        sm = k.scr[:, 8:8 + ngrp]
        k.dve.inc(k.V.tensor_reduce(sm, tp3, AX.X, OP.add))
        k.w(nc.vector, k.dve)
        k.dve.inc(k.V.reciprocal(sm, sm))
        k.w(nc.vector, k.dve)
        for g in range(ngrp):
            k.dve.inc(k.V.tensor_scalar(
                tp[:, g * grp:(g + 1) * grp], tp[:, g * grp:(g + 1) * grp],
                sm[:, g:g + 1], None, OP.mult))
        k.w(nc.vector, k.dve)
        if floor_mode == "last":
            k.dve.inc(k.V.tensor_scalar(tp, tp, 1.0 - floor, None,
                                              OP.mult))
            k.w(nc.vector, k.dve)
            lastcols = tp.rearrange("p (g j) -> p g j", g=ngrp)[:, :, grp - 1]
            k.dve.inc(k.V.tensor_scalar(lastcols, lastcols, floor, None,
                                              OP.add))
        else:
            k.dve.inc(k.V.tensor_scalar(tp, tp, 1.0 - grp * floor,
                                              floor, OP.mult, OP.add))
        k.w(nc.vector, k.dve)
    k.w(nc.tensor, k.dve)
    first = True
    for tt in range(ntile):
        mm = nc.tensor.transpose(k.w2[0:nrows, tt * 128:(tt + 1) * 128],
                                 ptm[:, tt * nrows:(tt + 1) * nrows],
                                 k.identf[:, 0:128])
        first = False
    k.pe.inc(mm)
    k.w(nc.vector, k.pe)
    k.dve.inc(k.V.tensor_copy(probs[0:nrows, 0:SEG],
                                    k.w2[0:nrows, 0:SEG]))
    k.w(nc.vector, k.dve)
    k.w(nc.tensor, k.dve)



E_MOE = 32


def build(T, n_cores, dbg=(), upto=None, asub=99):
    lim = {"ln0": 0, "l0a": 1, "f0": 2, "l1a": 3, "moe1": 4, "ag1": 5,
           "scat": 6, "exp": 7, "ag2": 8, None: 99}[upto]
    nc = bass.Bass("TRN2", target_bir_lowering=False)
    stack = contextlib.ExitStack()
    k = K(nc, stack, T, n_cores)
    k.asub = asub

    x_in = nc.dram_tensor("x", [D, T], F32, kind="ExternalInput").ap()
    EPB_ = E_MOE // n_cores
    wb13_hbm = nc.dram_tensor("wb13_hbm", [128, EPB_ * 16384], BF16).ap()
    wb2_hbm = nc.dram_tensor("wb2_hbm", [128, EPB_ * 8192], BF16).ap()
    consts_in = nc.dram_tensor("consts", [128, 512], F32,
                               kind="ExternalInput").ap()
    rs24_in = nc.dram_tensor("rowsel24", [128, 24 * 24], F32,
                             kind="ExternalInput").ap()
    rs16_in = nc.dram_tensor("rowsel16", [128, 16 * 16], F32,
                             kind="ExternalInput").ap()
    rr24_in = nc.dram_tensor("rowrep24", [24, 24 * 128], F32,
                             kind="ExternalInput").ap()
    shv_in = nc.dram_tensor("shv16", [128, 16], F32,
                            kind="ExternalInput").ap()
    y_out = nc.dram_tensor("y", [T, D], F32, kind="ExternalOutput").ap()
    xr_hbm = nc.dram_tensor("xr_hbm", [D, T], F32).ap()

    dbg_outs = set(dbg)

    win = {}

    def wtin(nm, shape, dtype=F32):
        win[nm] = nc.dram_tensor("w_" + nm, shape, dtype,
                                 kind="ExternalInput").ap()

    wtin("l0a_wq", [128, 64 * 128])
    wtin("l0a_wk", [128, 64 * 128])
    wtin("l0a_wv", [128, 64 * 128])
    wtin("l0a_wb", [128, 8 * 4])
    wtin("l0a_cq", [128, 32])
    wtin("l0a_ck", [128, 32])
    wtin("l0a_cv", [128, 32])
    wtin("l0a_fir0", [128, NS * 3])
    wtin("l0a_fir1", [128, NS * 7])
    wtin("l0a_fir2", [128, NS * 15])
    wtin("l0a_fir3", [128, NS * 31])
    wtin("l0a_gw1x", [128, 128 * 128])
    wtin("l0a_gw1s", [24, 16 * 128])
    wtin("l0a_gw2", [128, 16 * 24])
    wtin("l0a_gb1", [128, 16])
    wtin("l0a_gb2", [24, 1])
    wtin("l0a_alpha", [24, 1])
    wtin("l0a_gtemp", [24, 1])
    wtin("l0a_onw", [128, 2])
    wtin("l0a_wo", [128, 64 * 128])
    wtin("f0_wgu", [128, 24 * 2048], I32)
    wtin("f0_wd", [128, 8 * 3072], I32)
    wtin("l1a_wq", [128, 64 * 128])
    wtin("l1a_wk", [128, 64 * 128])
    wtin("l1a_wv", [128, 64 * 128])
    wtin("l1a_wb", [128, 8 * 4])
    wtin("l1a_cq", [128, 32])
    wtin("l1a_ck", [128, 32])
    wtin("l1a_cv", [128, 32])
    wtin("l1a_fir0", [128, NS * 5])
    wtin("l1a_fir1", [128, NS * 64])
    wtin("l1a_gw1x", [128, 64 * 128])
    wtin("l1a_gw1s", [128, 8 * 128])
    wtin("l1a_gw2", [128, 8 * 4])
    wtin("l1a_sel16", [128, 16])
    wtin("l1a_gb1", [128, 8])
    wtin("l1a_gb2", [128, 1])
    wtin("l1a_gcb", [128, 1])
    wtin("l1a_gtemp", [128, 1])
    wtin("l1a_onw", [128, 2])
    wtin("l1a_wo", [128, 64 * 128])
    wtin("moe_wg", [128, 8 * 32])
    wtin("moe_sw13", [128, 2 * 64 * 128])
    wtin("moe_sw2", [128, 64 * 128])
    wtin("moe_w13", [128, (E_MOE // n_cores) * 2 * 64 * 128])
    wtin("moe_w2", [128, (E_MOE // n_cores) * 64 * 128])
    wtin("moe_myeb", [1, 1], U32)
    wtin("moe_myf0", [1, 1], U32)
    wtin("moe_rowb", [128, 16])
    wtin("moe_cst", [128, 128 + 8 + 16 * 4 * 32])
    wtin("moe_exp8", [8, 128])

    with stack:
        k.ones128 = k.sb("ones128", [128, 128], BF16)
        k.ones128f = k.sb("ones128f", [128, 128], F32)
        k._consts_tensor = k.sb("constcols", [128, 16], F32)
        k.sq = k.sb("gsq", [128, NS * SEG], BF16)
        k.scr = k.sb("gscr", [128, 512], F32)
        k.scr2 = k.sb("gscr2", [128, 512], F32)
        k.scr3 = k.sb("gscr3", [128, 512], F32)
        k.wide = k.sb("gwide", [128, 2048], F32)
        k.wide2 = k.sb("gwide2", [128, 2048], F32)
        k.w1 = stack.enter_context(nc.psum_tensor("pw1", [128, 512], F32))
        k.w2 = stack.enter_context(nc.psum_tensor("pw2", [128, 512], F32))
        k.w3 = stack.enter_context(nc.psum_tensor("pw3", [128, 512], F32))
        k.w4b = stack.enter_context(nc.psum_tensor("pw4b", [128, 1024], BF16))
        k.S0 = stack.enter_context(nc.psum_tensor("S0", [128, 1024], F32))
        k.S1 = stack.enter_context(nc.psum_tensor("S1", [128, 1024], F32))

        k.dve.inc(k.V.memset(k.ones128[:], 1.0))
        k.dve.inc(k.V.memset(k.ones128f[:], 1.0))
        for _c in (1e-5, 1e-6, 1e-12, 1e-9, 1e-4, 1e-30, 1e-37, 1.0):
            k.cst(_c)

        cst4 = k.sb("cst4", [128, 512], F32)
        k.identb = k.sb("identb", [128, 128], BF16)
        k.identf = k.sb("identf", [128, 128], F32)
        k.negslb = k.sb("negslb", [128, 128], BF16)
        k.triub = k.sb("triub", [128, 128], BF16)
        k.rowsel = k.sb("rowsel_sb", [128, 24 * 24], BF16)
        k.rowsel16 = k.sb("rowsel16_sb", [128, 16 * 16], BF16)
        k.rowrepb = k.sb("rowrep_sb", [24, 24 * 128], BF16)
        k.shv16 = k.sb("shv16_sb", [128, 16], F32)
        nc.sync.dma_start(out=cst4[:], in_=consts_in).then_inc(k.dma.sem, 16)
        nc.sync.dma_start(out=k.shv16[:], in_=shv_in).then_inc(k.dma.sem, 16)
        nc.sync.dma_start(out=xr_hbm, in_=x_in).then_inc(k.dma.sem, 16)
        k.dma.v += 48
        k.hload(k.rowsel[:], rs24_in)
        k.hload(k.rowsel16[:], rs16_in)
        k.hload(k.rowrepb[:], rr24_in)
        k.wall(k.dmag)
        k.wall(k.dma)
        k.dve.inc(k.V.tensor_copy(k.identb[:], cst4[:, 0:128]))
        k.dve.inc(k.V.tensor_copy(k.identf[:], cst4[:, 0:128]))
        k.dve.inc(k.V.tensor_copy(k.negslb[:], cst4[:, 128:256]))
        k.dve.inc(k.V.tensor_copy(k.triub[:], cst4[:, 256:384]))
        k.wall(k.dve)

        if lim > 3:
            k.wcast.inc(nc.gpsimd.dma_start(out=wb13_hbm,
                                            in_=win["moe_w13"]), 16)
            k.wcast.inc(nc.gpsimd.dma_start(out=wb2_hbm,
                                            in_=win["moe_w2"]), 16)

        wsc = contextlib.ExitStack()
        with wsc:
            wts = {}
            for nm, wd in (("wq", 64 * 128), ("wk", 64 * 128),
                           ("wv", 64 * 128), ("wo", 64 * 128)):
                wts[nm] = k.sb("w_" + nm, [128, wd], BF16, wsc)
                k.hload(wts[nm][:], win["l0a_" + nm])
            wts["wb"] = k.sb("w_wb", [128, 32], BF16, wsc)
            k.hload(wts["wb"][:], win["l0a_wb"])
            wts["gw1s"] = k.sb("w_gw1s", [24, 16 * 128], BF16, wsc)
            k.hload(wts["gw1s"][:], win["l0a_gw1s"])
            wts["gw2"] = k.sb("w_gw2", [128, 16 * 24], BF16, wsc)
            k.hload(wts["gw2"][:], win["l0a_gw2"])
            for nm, wd in (("cq", 32), ("ck", 32), ("cv", 32),
                           ("fir0", NS * 3), ("fir1", NS * 7),
                           ("fir2", NS * 15), ("fir3", NS * 31),
                           ("gb1", 16), ("onw", 2)):
                wts[nm] = k.sb("w_" + nm, [128, wd], F32, wsc)
                k.hload(wts[nm][:], win["l0a_" + nm])
            for nm, rows in (("gb2", 24), ("alpha", 24), ("gtemp", 24)):
                wts[nm] = k.sb("w_" + nm, [rows, 1], F32, wsc)
                k.hload(wts[nm][:], win["l0a_" + nm])
            k.wall(k.dmag)

            dbgw = make_dbgw(k, dbg_outs, 0) if dbg else None
            if lim > 0:
                emit_attn(k, 0, xr_hbm, wts, dbgw,
                          gw1x_hbm=win["l0a_gw1x"])

        if lim > 1:
            emit_ffn0(k, xr_hbm, win["f0_wgu"], win["f0_wd"])

        wsc = contextlib.ExitStack()
        with wsc:
            wts = {}
            for nm, wd in (("wq", 64 * 128), ("wk", 64 * 128),
                           ("wv", 64 * 128), ("wo", 64 * 128),
                           ("gw1x", 64 * 128), ("gw1s", 8 * 128)):
                wts[nm] = k.sb("w1_" + nm, [128, wd], BF16, wsc)
                k.hload(wts[nm][:], win["l1a_" + nm])
            wts["wb"] = k.sb("w1_wb", [128, 32], BF16, wsc)
            k.hload(wts["wb"][:], win["l1a_wb"])
            wts["gw2"] = k.sb("w1_gw2", [128, 8 * 4], BF16, wsc)
            k.hload(wts["gw2"][:], win["l1a_gw2"])
            wts["sel16"] = k.sb("w1_sel16", [128, 16], F32, wsc)
            k.hload(wts["sel16"][:], win["l1a_sel16"])
            for nm, wd in (("cq", 32), ("ck", 32), ("cv", 32),
                           ("fir0", NS * 5), ("fir1", NS * 64),
                           ("gb1", 8), ("onw", 2)):
                wts[nm] = k.sb("w1_" + nm, [128, wd], F32, wsc)
                k.hload(wts[nm][:], win["l1a_" + nm])
            for nm in ("gb2", "gcb", "gtemp"):
                wts[nm] = k.sb("w1_" + nm, [128, 1], F32, wsc)
                k.hload(wts[nm][:], win["l1a_" + nm])
            wts["gcbd"] = k.sb("w1_gcbd", [128, 1], F32, wsc)
            wts["grt"] = k.sb("w1_grt", [128, 1], F32, wsc)
            k.wall(k.dmag)
            DECAY = 1.0 - 1.0 / 3000.0
            k.dve.inc(k.V.tensor_scalar(wts["gcbd"][:], wts["gcb"][:],
                                        DECAY, None, OP.mult))
            k.act.inc(k.A.activation(wts["grt"][:], wts["gtemp"][:], AF.Exp,
                                     scale=-1.0))
            k.wall(k.dve, k.act)

            dbgw1 = make_dbgw(k, dbg_outs, 1) if dbg else None
            if lim > 2:
                emit_attn(k, 1, xr_hbm, wts, dbgw1, gw1x_hbm=None)

        mc = k.sb("moecst", [128, 128 + 8 + 16 * 4 * 32], F32)
        k.exp8f = k.sb("exp8f", [8, 128], F32)
        nc.sync.dma_start(out=mc[:], in_=win["moe_cst"]).then_inc(
            k.dma.sem, 16)
        nc.sync.dma_start(out=k.exp8f[:], in_=win["moe_exp8"]).then_inc(
            k.dma.sem, 16)
        k.dma.v += 32
        k.su16f = mc[:, 0:128]
        k.esum8f = mc[:, 128:136]
        k.collf = mc[:, 136:136 + 2048]
        k.wall(k.dma)

        if lim > 3:
            emit_moe(k, xr_hbm, {nm: win[nm] for nm in win}, y_out,
                     lim=lim, wbh=(wb13_hbm, wb2_hbm))
        if lim < 99:
            emit_dummy_y(k, y_out)


    return nc, dbg_outs


def make_dbgw(k, dbg_outs, var):
    nc = k.nc
    NSEG = k.T // SEG
    state = {}

    def w(sg, ls):
        keys = {"ln": "ln_seg", "qn": "qn", "kn": "kn", "br": "br",
                "probs": "probs", "oc": "oc", "ao": "ao", "beta": "beta",
                "stats": "stats", "u0": "u0a", "wfm": "wFM", "x0": "x0a",
                "ssb": "S_sb", "vc": "vc", "kTM": "kTM", "ocb": "ocb",
                "h1": "h1", "kb": "kb", "vb": "vb", "lg": "lg"}
        srcs = {kk: ls[vv] for kk, vv in keys.items() if vv in ls}
        for key, src in srcs.items():
            nm = "a%d_%s" % (var, key)
            if nm not in dbg_outs:
                continue
            if nm not in state:
                rows, wdt = src.shape
                hnd = nc.dram_tensor("dbg_" + nm, [rows, NSEG * wdt], F32,
                                     kind="ExternalOutput").ap()
                state[nm] = (hnd, rows, wdt)
            hnd, rows, wdt = state[nm]
            k.wall(k.dve, k.act, k.pe)
            k.dmag.inc(nc.gpsimd.dma_start(
                out=hnd[:, sg * wdt:(sg + 1) * wdt], in_=src[:]), 16)
            nc.gpsimd.wait_ge(k.dmag.sem, k.dmag.v)
            k.wall(k.dmag)
    return w


def emit_output(k, xr_hbm, y_out):
    nc = k.nc
    T = k.T
    sc = contextlib.ExitStack()
    with sc:
        out_tm = k.sb("out_tm", [128, D], F32, sc)
        xseg = k.sb("oxseg", [128, NS * 128], F32, sc)
        for tt in range(T // 128):
            k.w(nc.sync, k.dve)
            nc.sync.dma_start(
                out=xseg[:].rearrange("p (s t) -> p s t", s=NS),
                in_=xr_slab_ap(xr_hbm, tt * 128, 128)).then_inc(k.dma.sem, 16)
            k.dma.v += 16
            k.wall(k.dma)
            for s in range(NS):
                k.w(nc.tensor, k.dve)
                mm = nc.tensor.transpose(
                    k.w1[:, 0:128] if s % 2 == 0 else k.w2[:, 0:128],
                    xseg[:, s * 128:(s + 1) * 128], k.identf[:])
                k.pe.inc(mm)
                k.evac(out_tm[:, s * 128:(s + 1) * 128],
                       (k.w1 if s % 2 == 0 else k.w2)[:, 0:128])
            k.w(nc.sync, k.dve)
            nc.sync.dma_start(out=y_out[tt * 128:(tt + 1) * 128, :],
                              in_=out_tm[:]).then_inc(k.dma.sem, 16)
            k.dma.v += 16
        nc.sync.wait_ge(k.dma.sem, k.dma.v)



def _sel16():
    s = np.zeros((128, 16), np.float32)
    for h in range(4):
        for j in range(4):
            s[32 * h + j, 4 * h + j] = 1.0
    return s


def make_in_maps(inputs, T, n_cores):
    x = np.asarray(inputs["L0_attn_hidden_states"])
    shared = {
        "consts": host_consts(),
        "rowsel24": host_rowsel(24),
        "rowsel16": host_rowsel(16),
        "rowrep24": host_rowrep(24),
        "shv16": host_shv16(),
        "w_l0a_wq": pack_mm(inputs["L0_attn_q_proj_w"]),
        "w_l0a_wk": pack_mm(inputs["L0_attn_k_proj_w"]),
        "w_l0a_wv": pack_mm(inputs["L0_attn_v_proj_w"]),
        "w_l0a_wb": pack_thin(inputs["L0_attn_b_proj_w"]),
        "w_l0a_cq": pack_chan(inputs["L0_attn_q_conv_w"]),
        "w_l0a_ck": pack_chan(inputs["L0_attn_k_conv_w"]),
        "w_l0a_cv": pack_chan(inputs["L0_attn_v_conv_w"]),
        "w_l0a_fir0": pack_chan(inputs["L0_attn_fir_w3"]),
        "w_l0a_fir1": pack_chan(inputs["L0_attn_fir_w7"]),
        "w_l0a_fir2": pack_chan(inputs["L0_attn_fir_w15"]),
        "w_l0a_fir3": pack_chan(inputs["L0_attn_fir_w31"]),
        "w_l0a_gw1x": pack_mm(np.asarray(inputs["L0_attn_gate_w1"])[:, :1024], mo_major=True),
        "w_l0a_gw1s": np.ascontiguousarray(
            np.asarray(inputs["L0_attn_gate_w1"])[:, 1024:].T),
        "w_l0a_gw2": pack_thin(inputs["L0_attn_gate_w2"]),
        "w_l0a_gb1": np.ascontiguousarray(
            np.asarray(inputs["L0_attn_gate_b1"]).reshape(16, 128).T),
        "w_l0a_gb2": np.asarray(inputs["L0_attn_gate_b2"])
        .reshape(24, 1).astype(np.float32),
        "w_l0a_alpha": np.ascontiguousarray(
            np.tile(np.asarray(inputs["L0_attn_alpha_stat"]), 4)
            .reshape(24, 1)).astype(np.float32),
        "w_l0a_gtemp": np.ascontiguousarray(
            np.repeat(np.asarray(inputs["L0_attn_gate_log_temp"]), 6)
            .reshape(24, 1)).astype(np.float32),
        "w_l0a_onw": np.ascontiguousarray(
            np.asarray(inputs["L0_attn_o_norm_w"]).reshape(2, 128).T)
        .astype(np.float32),
        "w_l0a_wo": pack_mm(inputs["L0_attn_o_proj_w"]),
        "w_f0_wgu": np.concatenate(
            [pack_mm(np.asarray(inputs["L0_ffn_w_gate"]),
                     mo_major=True).reshape(128, 24, 1024),
             pack_mm(np.asarray(inputs["L0_ffn_w_up"]),
                     mo_major=True).reshape(128, 24, 1024)],
            axis=2).reshape(128, -1),
        "w_f0_wd": pack_mm(np.asarray(inputs["L0_ffn_w_down"]),
                           mo_major=True),
        "w_l1a_wq": pack_mm(inputs["L1_attn_Wq"]),
        "w_l1a_wk": pack_mm(inputs["L1_attn_Wk"]),
        "w_l1a_wv": pack_mm(inputs["L1_attn_Wv"]),
        "w_l1a_wb": pack_thin(inputs["L1_attn_Wb"]),
        "w_l1a_cq": pack_chan(inputs["L1_attn_conv_q_w"]),
        "w_l1a_ck": pack_chan(inputs["L1_attn_conv_k_w"]),
        "w_l1a_cv": pack_chan(inputs["L1_attn_conv_v_w"]),
        "w_l1a_fir0": pack_chan(
            np.asarray(inputs["L1_attn_fir_short_filt"]).reshape(1024, 5)),
        "w_l1a_fir1": pack_chan(
            np.asarray(inputs["L1_attn_fir_long_filt"]).reshape(1024, 64)),
        "w_l1a_gw1x": pack_mm(np.asarray(inputs["L1_attn_gate_W1"])[:, :1024]),
        "w_l1a_gw1s": _l1_gw1s(np.asarray(inputs["L1_attn_gate_W1"])),
        "w_l1a_gw2": pack_thin(inputs["L1_attn_gate_W2"]),
        "w_l1a_gb1": np.ascontiguousarray(
            np.asarray(inputs["L1_attn_gate_b1"]).reshape(8, 128).T),
        "w_l1a_gb2": np.ascontiguousarray(
            np.tile(np.asarray(inputs["L1_attn_gate_b2"]), 32)
            .reshape(128, 1)),
        "w_l1a_gcb": _l1_gcb(np.asarray(inputs["L1_attn_gate_copy_bias"])),
        "w_l1a_gtemp": np.ascontiguousarray(
            np.repeat(np.asarray(inputs["L1_attn_gate_log_temp"]), 32)
            .reshape(128, 1)),
        "w_l1a_onw": np.ascontiguousarray(
            np.asarray(inputs["L1_attn_o_norm_w"]).reshape(2, 128).T)
        .astype(np.float32),
        "w_l1a_wo": pack_mm(inputs["L1_attn_Wo"]),
        "w_l1a_sel16": _sel16(),
        "w_moe_wg": pack_thin(inputs["L1_ffn_wg"]),
        "w_moe_sw13": np.concatenate(
            [pack_mm(np.asarray(inputs["L1_ffn_sw1"]).T),
             pack_mm(np.asarray(inputs["L1_ffn_sw3"]).T)], axis=1),
        "w_moe_sw2": pack_rhs(inputs["L1_ffn_sw2"]),
        "w_moe_cst": host_moe_consts(),
        "w_moe_exp8": host_moe_exp8(),
    }
    shared = {kk: np.ascontiguousarray(
        vv if vv.dtype == np.int32 else vv.astype(np.float32))
        for kk, vv in shared.items()}
    w1e = np.asarray(inputs["L1_ffn_w1"])
    w2e = np.asarray(inputs["L1_ffn_w2"])
    w3e = np.asarray(inputs["L1_ffn_w3"])
    EPB = 32 // n_cores
    PT = T // 16
    maps = []
    for c in range(n_cores):
        m = dict(shared)
        m["x"] = np.ascontiguousarray(x[c, :T].T.astype(np.float32))
        m["w_moe_w13"] = np.ascontiguousarray(np.concatenate(
            [np.concatenate([pack_mm(w1e[c * EPB + j].T),
                             pack_mm(w3e[c * EPB + j].T)], axis=1)
             for j in range(EPB)], axis=1)).astype(np.float32)
        m["w_moe_w2"] = np.ascontiguousarray(np.concatenate(
            [pack_rhs(w2e[c * EPB + j]) for j in range(EPB)],
            axis=1)).astype(np.float32)
        m["w_moe_myeb"] = np.array([[c * EPB]], np.uint32)
        m["w_moe_myf0"] = np.array([[c * PT]], np.uint32)
        m["w_moe_rowb"] = np.ascontiguousarray(
            (c * T + np.arange(16)[None, :] * PT
             + np.arange(128)[:, None] + 1).astype(np.float32))
        maps.append(m)
    return maps


def kernel(**inputs):
    x = inputs["L0_attn_hidden_states"]
    B, T, _ = x.shape
    n_cores = 8
    nc, _ = build(T, n_cores)
    library_overlay.lower_extended_insts(nc)
    in_maps = make_in_maps(inputs, T, n_cores)
    res = run_bass_kernel_spmd(nc, in_maps, core_ids=list(range(n_cores)))
    return np.stack([r["y"] for r in res.results], axis=0)


# version 16 (final)
# speedup vs baseline: 1.0115x; 1.0115x over previous
import contextlib

import numpy as np

import concourse.bass as bass
import concourse.mybir as mybir
from concourse import library_config
from concourse import library_overlay
from concourse.bass_utils import run_bass_kernel_spmd

F32 = mybir.dt.float32
BF16 = mybir.dt.bfloat16
I16 = mybir.dt.int16
I32 = mybir.dt.int32
U32 = mybir.dt.uint32
AF = mybir.ActivationFunctionType
OP = mybir.AluOpType
AX = mybir.AxisListType

D = 1024
NS = 8
H = 4
DK = DV = 256
CH = 32
SEG = 256
NCHS = SEG // CH



def pack_mm(W, mo_major=False):
    out_d, in_d = W.shape
    nki, nmo = in_d // 128, out_d // 128
    g = np.ascontiguousarray(np.asarray(W).T).reshape(nki, 128, nmo, 128)
    g = g.transpose(1, 2, 0, 3) if mo_major else g.transpose(1, 0, 2, 3)
    return np.ascontiguousarray(g.reshape(128, nki * nmo * 128))


def pack_thin(W):
    out_d, in_d = W.shape
    nki = in_d // 128
    g = np.ascontiguousarray(np.asarray(W).T).reshape(nki, 128, out_d)
    return np.ascontiguousarray(g.transpose(1, 0, 2).reshape(128, nki * out_d))


def pack_chan(w):
    w = np.asarray(w)
    Kk = w.shape[-1]
    return np.ascontiguousarray(
        w.reshape(NS, 128, Kk).transpose(1, 0, 2).reshape(128, NS * Kk))


def host_consts():
    i = np.arange(128)
    ident = (i[:, None] == i[None, :]).astype(np.float32)
    within = (i[:, None] // 32) == (i[None, :] // 32)
    sl = (((i[:, None] % 32) > (i[None, :] % 32)) & within).astype(np.float32)
    tu = (((i[:, None] % 32) <= (i[None, :] % 32)) & within).astype(np.float32)
    z = np.zeros((128, 128), np.float32)
    return np.concatenate([ident, -sl, tu, z], axis=1)


def host_rowsel(nrows):
    m = np.zeros((128, nrows * nrows), np.float32)
    for r in range(nrows):
        m[:, r * nrows + r] = 1.0
    return m


def host_shv16():
    m = np.zeros((128, 16), np.float32)
    for blk in range(4):
        for bi in range(4):
            m[32 * blk + 4 + bi, 8 + bi] = 1.0
    return m


def _l1_gcb(cb):
    z = np.zeros((128, 1), np.float32)
    for h in range(4):
        z[32 * h + 3, 0] = cb[h]
    return z


def _l1_gw1s(W1):
    s = np.ascontiguousarray(np.asarray(W1)[:, 1024:1040].T)
    perm = np.zeros(16, np.int64)
    for bi in range(4):
        perm[bi] = bi * 4 + 3
        perm[4 + bi] = bi * 4
        perm[8 + bi] = bi * 4 + 1
        perm[12 + bi] = bi * 4 + 2
    sp = s[perm]
    z = np.zeros((128, 1024), np.float32)
    for b in range(4):
        z[32 * b:32 * b + 16] = sp
    return z


def pack_rhs(W):
    W = np.asarray(W)
    out_d = W.shape[1]
    return np.ascontiguousarray(
        W.reshape(8, 128, out_d).transpose(1, 0, 2).reshape(128, 8 * out_d))


def host_moe_consts():
    i = np.arange(128)
    su16 = (((i[:, None] // 16) == (i[None, :] // 16))
            & ((i[:, None] % 16) < (i[None, :] % 16))).astype(np.float32)
    esum8 = np.zeros((128, 8), np.float32)
    for e in range(8):
        esum8[e * 16:(e + 1) * 16, e] = 1.0
    coll = np.zeros((128, 16 * 4 * 32), np.float32)
    for a in range(16):
        for g in range(4):
            for e in range(32):
                if e // 8 == g:
                    coll[(e % 8) * 16 + a, (a * 4 + g) * 32 + e] = 1.0
    return np.concatenate([su16, esum8, coll], axis=1)


def host_moe_exp8():
    m = np.zeros((8, 128), np.float32)
    for r in range(8):
        m[r, r * 16:(r + 1) * 16] = 1.0
    return m


def host_rowrep(nrows):
    m = np.zeros((nrows, nrows * 128), np.float32)
    for r in range(nrows):
        m[r, r * 128:(r + 1) * 128] = 1.0
    return m



class Ctr:
    def __init__(self, stack, nc, name):
        self.sem = stack.enter_context(nc.semaphore(name))
        self.v = 0

    def inc(self, inst, n=1):
        inst.then_inc(self.sem, n)
        self.v += n
        return inst


class AutoEng:

    NO_SELF_WAIT = False
    _PASS = {"wait_ge", "dma_start", "indirect_dma_start", "load_library",
             "dma_start_transpose"}

    def __init__(self, eng, ctr, k=None):
        self._eng = eng
        self._ctr = ctr
        self._k = k

    def __getattr__(self, name):
        f = getattr(self._eng, name)
        if name.startswith("_") or name in self._PASS or not callable(f):
            return f

        def g(*a, **kw):
            if self._k is not None:
                self._k.hb_tick()
            if not AutoEng.NO_SELF_WAIT:
                self._eng.wait_ge(self._ctr.sem, self._ctr.v)
            return f(*a, **kw)
        return g


class K:
    def __init__(self, nc, stack, T, NC):
        self.nc = nc
        self.stack = stack
        self.T = T
        self.NC = NC
        self.pe = Ctr(stack, nc, "c_pe")
        self.dve = Ctr(stack, nc, "c_dve")
        self.act = Ctr(stack, nc, "c_act")
        self.pool = Ctr(stack, nc, "c_pool")
        self.dma = Ctr(stack, nc, "d_sp")
        self.dmag = Ctr(stack, nc, "d_pl")
        self.str_e = Ctr(stack, nc, "d_se")
        self.str_o = Ctr(stack, nc, "d_so")
        self.gat = [Ctr(stack, nc, "d_g%d" % i) for i in range(4)]
        self.eo_e = Ctr(stack, nc, "d_ee")
        self.eo_o = Ctr(stack, nc, "d_eo")
        self.wcast = Ctr(stack, nc, "d_wc")
        self.we_e = Ctr(stack, nc, "d_we")
        self.we_o = Ctr(stack, nc, "d_wo")
        self.cc = Ctr(stack, nc, "c_cc")
        self._consts = {}
        self.V = AutoEng(nc.vector, self.dve, self)
        self.A = AutoEng(nc.scalar, self.act, self)
        self.G = AutoEng(nc.gpsimd, self.pool)
        self.hb_cfg = None
        self._hbn = 0

    def hb_tick(self):
        if self.hb_cfg is None:
            return
        self._hbn += 1
        if self._hbn % 3:
            return
        self.nc.tensor.ldweights(self.identb[:, 0:4])

    def sb(self, name, shape, dtype, scope=None):
        st = scope if scope is not None else self.stack
        return st.enter_context(self.nc.sbuf_tensor(name, shape, dtype))

    def cst(self, value):
        if value not in self._consts:
            j = len(self._consts)
            assert j < self._consts_tensor.shape[1]
            self.dve.inc(self.V.memset(
                self._consts_tensor[:, j:j + 1], float(value)))
            self._consts[value] = j
        j = self._consts[value]
        return self._consts_tensor[:, j:j + 1]

    def w(self, eng, *ctrs):
        for c in ctrs:
            eng.wait_ge(c.sem, c.v)

    def wall(self, *ctrs):
        nc = self.nc
        for eng in (nc.tensor, nc.vector, nc.scalar, nc.gpsimd, nc.sync):
            for c in ctrs:
                eng.wait_ge(c.sem, c.v)

    def evac(self, out_ap, ps_ap, eng="dve"):
        nc = self.nc
        if eng == "act":
            self.w(nc.scalar, self.pe)
            self.act.inc(self.A.copy(out_ap, ps_ap))
        else:
            self.w(nc.vector, self.pe)
            self.dve.inc(self.V.tensor_copy(out_ap, ps_ap))

    def rsqrt(self, ap, eps):
        nc = self.nc
        self.w(nc.scalar, self.dve)
        self.act.inc(self.A.activation(ap, ap, AF.Ln,
                                       bias=self.cst(eps)[0:ap.shape[0]]))
        self.w(nc.scalar, self.act)
        self.act.inc(self.A.activation(ap, ap, AF.Exp, scale=-0.5))

    def hload(self, dst_sb, src_ap):
        self.dmag.inc(self.nc.gpsimd.dma_start(out=dst_sb, in_=src_ap), 16)


def xr_slab_ap(xr_hbm, t0, w):
    return xr_hbm.rearrange("(s p) t -> p s t", p=128)[:, :, t0:t0 + w]



def emit_ln(k, src_slabs, W, out_sb=None, out_off=0, out_w=None, eps=1e-5,
            res_slabs=None, src3=None, out3=None, res3=None):
    nc = k.nc
    if out_w is None:
        out_w = W
    if src3 is not None:
        sq3 = k.sq[:].rearrange("p (s t) -> p s t", s=NS)[:, :, 0:W]
        k.w(nc.scalar, k.pe, k.dve)
        k.act.inc(k.A.activation(sq3, src3, AF.Square))
        k.w(nc.tensor, k.act, k.dve)
        for s in range(NS):
            ones = k.ones128f if src3.dtype == F32 else k.ones128
            nc.tensor.matmul(k.w1[:, 0:W], ones[:], src3[:, s, :],
                             start=(s == 0), stop=(s == NS - 1))
            mm = nc.tensor.matmul(k.w2[:, 0:W], k.ones128[:], sq3[:, s, :],
                                  start=(s == 0), stop=(s == NS - 1))
        k.pe.inc(mm)
        r = k.scr[:, 0:W]
        m = k.scr2[:, 0:W]
        k.w(nc.vector, k.pe)
        k.dve.inc(k.V.tensor_scalar(m, k.w1[:, 0:W], 1.0 / D, None,
                                    OP.mult))
        k.dve.inc(k.V.tensor_scalar(r, k.w2[:, 0:W], 1.0 / D, None,
                                    OP.mult))
        k.w(nc.vector, k.dve)
        k.dve.inc(k.V.tensor_tensor(k.scr3[:, 0:W], m, m, OP.mult))
        k.w(nc.vector, k.dve)
        k.dve.inc(k.V.tensor_tensor(r, r, k.scr3[:, 0:W], OP.subtract))
        k.rsqrt(r, eps)
        k.w(nc.vector, k.act)
        k.dve.inc(k.V.scalar_tensor_tensor(m, m, -1.0, r, OP.mult,
                                           OP.mult))
        rb = r.unsqueeze(1).broadcast_to([128, NS, W])
        mb = m.unsqueeze(1).broadcast_to([128, NS, W])
        w3 = k.wide[:].rearrange("p (s t) -> p s t", s=NS)[:, :, 0:W]
        k.w(nc.vector, k.dve)
        k.dve.inc(k.V.tensor_tensor(w3, src3, rb, OP.mult))
        k.w(nc.vector, k.dve)
        if res3 is None:
            k.dve.inc(k.V.tensor_tensor(out3, w3, mb, OP.add))
        else:
            k.dve.inc(k.V.tensor_tensor(w3, w3, mb, OP.add))
            k.w(nc.vector, k.dve)
            k.dve.inc(k.V.tensor_tensor(res3, res3, w3, OP.add))
            if out3 is not None:
                k.w(nc.scalar, k.dve)
                k.act.inc(k.A.copy(out3, w3))
        k.w(nc.vector, k.dve)
        k.w(nc.scalar, k.dve)
        k.w(nc.tensor, k.dve)
        return
    k.w(nc.scalar, k.pe, k.dve)
    for s in range(NS):
        k.act.inc(k.A.activation(k.sq[:, s * W:(s + 1) * W],
                                       src_slabs[s], AF.Square))
    k.w(nc.tensor, k.act, k.dve)
    for s in range(NS):
        ones = k.ones128f if src_slabs[s].dtype == F32 else k.ones128
        nc.tensor.matmul(k.w1[:, 0:W], ones[:], src_slabs[s],
                         start=(s == 0), stop=(s == NS - 1))
        mm = nc.tensor.matmul(k.w2[:, 0:W], k.ones128[:],
                              k.sq[:, s * W:(s + 1) * W],
                              start=(s == 0), stop=(s == NS - 1))
    k.pe.inc(mm)
    r = k.scr[:, 0:W]
    m = k.scr2[:, 0:W]
    k.w(nc.vector, k.pe)
    k.dve.inc(k.V.tensor_scalar(m, k.w1[:, 0:W], 1.0 / D, None, OP.mult))
    k.dve.inc(k.V.tensor_scalar(r, k.w2[:, 0:W], 1.0 / D, None, OP.mult))
    k.w(nc.vector, k.dve)
    k.dve.inc(k.V.tensor_tensor(k.scr3[:, 0:W], m, m, OP.mult))
    k.w(nc.vector, k.dve)
    k.dve.inc(k.V.tensor_tensor(r, r, k.scr3[:, 0:W], OP.subtract))
    k.rsqrt(r, eps)
    k.w(nc.vector, k.act)
    k.dve.inc(k.V.scalar_tensor_tensor(m, m, -1.0, r, OP.mult, OP.mult))
    k.w(nc.vector, k.dve)
    for s in range(NS):
        k.dve.inc(k.V.tensor_tensor(k.scr3[:, 0:W], src_slabs[s], r,
                                          OP.mult))
        k.w(nc.vector, k.dve)
        if res_slabs is None:
            k.dve.inc(k.V.tensor_tensor(
                out_sb[:, out_off + s * out_w:out_off + s * out_w + W],
                k.scr3[:, 0:W], m, OP.add))
        else:
            k.dve.inc(k.V.tensor_tensor(
                k.scr3[:, 0:W], k.scr3[:, 0:W], m, OP.add))
            k.w(nc.vector, k.dve)
            k.dve.inc(k.V.tensor_tensor(res_slabs[s], res_slabs[s],
                                              k.scr3[:, 0:W], OP.add))
            if out_sb is not None:
                k.w(nc.scalar, k.dve)
                k.act.inc(k.A.copy(
                    out_sb[:, out_off + s * out_w:out_off + s * out_w + W],
                    k.scr3[:, 0:W]))
        k.w(nc.vector, k.dve)
    k.w(nc.scalar, k.dve)
    k.w(nc.tensor, k.dve)



def br_ap(br, nbr, bi, vh, cc, width):
    v = br[:].rearrange("p (b h g t) -> p b h g t", b=nbr, h=H, g=2)
    return v[:, bi, :, vh, cc:cc + width]


def emit_attn(k, var, xr_hbm, wts, dbg_write, gw1x_hbm=None):
    nc = k.nc
    T = k.T
    NSEG = T // SEG
    l2eps = 1e-12 if var == 0 else 1e-6
    FIRK = (3, 7, 15, 31) if var == 0 else (5, 64)
    FIRMAX = max(FIRK)
    PAD = FIRMAX - 1
    nbr = 6 if var == 0 else 4
    dbi = 5 if var == 0 else 2
    vbi = 4 if var == 0 else 3
    nlg = 24 if var == 0 else 16
    nmo_h1 = 16 if var == 0 else 8

    sc = contextlib.ExitStack()
    p = "a%d_" % var
    with sc:
        xr_seg = k.sb(p + "xr", [128, NS * SEG], F32, sc)
        ln_seg = k.sb(p + "ln", [128, NS * SEG], BF16, sc)
        pp = k.sb(p + "pp", [128, NS * (SEG + 3)], BF16, sc)
        tails = {nm: k.sb(p + "tl" + nm, [128, NS * 3], BF16, sc)
                 for nm in "qkv"}
        qn = k.sb(p + "qn", [128, NS * SEG], BF16, sc)
        kn = k.sb(p + "kn", [128, NS * SEG], BF16, sc)
        vc = k.sb(p + "vc", [128, NS * (SEG + PAD)], BF16, sc)
        beta = k.sb(p + "beta", [H, SEG], F32, sc)
        beta_b = k.sb(p + "betab", [H, SEG], BF16, sc)
        bbc = k.sb(p + "bbc", [128, H * SEG], BF16, sc)
        kb = k.sb(p + "kb", [128, NS * SEG], BF16, sc)
        vb = k.sb(p + "vb", [128, NS * SEG], BF16, sc)
        S_sb = k.sb(p + "S_sb", [128, 2 * 1024], BF16, sc)
        br = k.sb(p + "br", [128, nbr * NS * SEG], BF16, sc)
        gw1db = (k.sb(p + "gw1db", [128, 2 * 1024], BF16, sc)
                 if gw1x_hbm is not None else None)

        idn = k.identb
        negsl = k.negslb
        triu = k.triub

        k.hb_cfg = True
        k.w(nc.vector, k.pe, k.act)
        k.dve.inc(k.V.memset(S_sb[:], 0.0))

        for sg in range(NSEG):
            t0 = sg * SEG
            glob0 = (sg == 0)
            globlast = (sg == NSEG - 1)

            k.w(nc.sync, k.dve, k.act, k.pe, k.dma)
            nc.sync.dma_start(
                out=xr_seg[:].rearrange("p (s t) -> p s t", s=NS),
                in_=xr_slab_ap(xr_hbm, t0, SEG)).then_inc(k.dma.sem, 16)
            k.dma.v += 16
            k.wall(k.dma)

            emit_ln(k, None, SEG, eps=1e-5,
                    src3=xr_seg[:].rearrange("p (s t) -> p s t", s=NS),
                    out3=ln_seg[:].rearrange("p (s t) -> p s t", s=NS))
            if k.asub <= 1:
                continue

            for (tnm, dstb, wname, wcn, pitch, doff, donorm) in (
                    ("q", qn, "wq", "cq", SEG, 0, True),
                    ("k", kn, "wk", "ck", SEG, 0, True),
                    ("v", vc, "wv", "cv", SEG + PAD, PAD, False)):
                wt = wts[wname]
                k.w(nc.vector, k.act)
                for s in range(NS):
                    if sg == 0:
                        k.dve.inc(k.V.memset(
                            pp[:, s * (SEG + 3):s * (SEG + 3) + 3], 0.0))
                    else:
                        k.dve.inc(k.V.tensor_copy(
                            pp[:, s * (SEG + 3):s * (SEG + 3) + 3],
                            tails[tnm][:, s * 3:(s + 1) * 3]))
                pend = None
                for mo in range(NS):
                    bank = k.w1 if mo % 2 == 0 else k.w2
                    k.w(nc.tensor, k.dve, k.act)
                    for ki in range(NS):
                        mm = nc.tensor.matmul(
                            bank[:, 0:SEG],
                            wt[:, (ki * NS + mo) * 128:(ki * NS + mo) * 128 + 128],
                            ln_seg[:, ki * SEG:(ki + 1) * SEG],
                            start=(ki == 0), stop=(ki == NS - 1))
                    k.pe.inc(mm)
                    if pend is not None:
                        k.evac(*pend, eng="act" if mo % 2 == 0 else "dve")
                    pend = (pp[:, mo * (SEG + 3) + 3:
                               mo * (SEG + 3) + 3 + SEG], bank[:, 0:SEG])
                k.evac(*pend, eng="act")
                k.w(nc.vector, k.act)
                for s in range(NS):
                    k.dve.inc(k.V.tensor_copy(
                        tails[tnm][:, s * 3:(s + 1) * 3],
                        pp[:, s * (SEG + 3) + SEG:s * (SEG + 3) + SEG + 3]))
                wconv = wts[wcn]
                for s in range(NS):
                    sb_in = pp[:, s * (SEG + 3):(s + 1) * (SEG + 3)]
                    k.w(nc.vector, k.dve, k.act)
                    k.dve.inc(k.V.tensor_scalar(
                        k.scr[:, 0:SEG], sb_in[:, 0:SEG],
                        wconv[:, s * 4:s * 4 + 1], None, OP.mult))
                    for j in range(1, 4):
                        k.w(nc.vector, k.dve)
                        k.dve.inc(k.V.scalar_tensor_tensor(
                            k.scr[:, 0:SEG], sb_in[:, j:j + SEG],
                            wconv[:, s * 4 + j:s * 4 + j + 1],
                            k.scr[:, 0:SEG], OP.mult, OP.add))
                    k.w(nc.scalar, k.dve)
                    k.act.inc(k.A.activation(
                        dstb[:, s * pitch + doff:s * pitch + doff + SEG],
                        k.scr[:, 0:SEG], AF.Silu))
                if donorm:
                    k.w(nc.scalar, k.act, k.dve)
                    k.act.inc(k.A.activation(k.sq[:, 0:NS * SEG],
                                             dstb[:, 0:NS * SEG],
                                             AF.Square))
                    k.w(nc.tensor, k.act, k.dve)
                    for h in range(H):
                        bank = k.w1 if h < 2 else k.w2
                        reg = bank[:, (h % 2) * SEG:(h % 2 + 1) * SEG]
                        nc.tensor.matmul(
                            reg, k.ones128[:],
                            k.sq[:, 2 * h * SEG:(2 * h + 1) * SEG],
                            start=True, stop=False)
                        mm = nc.tensor.matmul(
                            reg, k.ones128[:],
                            k.sq[:, (2 * h + 1) * SEG:(2 * h + 2) * SEG],
                            start=False, stop=True)
                    k.pe.inc(mm)
                    k.w(nc.vector, k.pe)
                    k.dve.inc(k.V.tensor_copy(k.wide2[:, 0:512],
                                              k.w1[:, 0:512]))
                    k.dve.inc(k.V.tensor_copy(k.wide2[:, 512:1024],
                                              k.w2[:, 0:512]))
                    k.rsqrt(k.wide2[:, 0:1024], l2eps)
                    l2b = k.wide2[:, 0:1024] \
                        .rearrange("p (h t) -> p h t", h=H) \
                        .unsqueeze(2).broadcast_to([128, H, 2, SEG])
                    d4 = dstb[:, 0:NS * SEG] \
                        .rearrange("p (h two t) -> p h two t", h=H, two=2)
                    k.w(nc.vector, k.act)
                    k.dve.inc(k.V.tensor_tensor(d4, d4, l2b, OP.mult))
                    k.w(nc.vector, k.dve)
            k.w(nc.tensor, k.dve, k.act)
            for ki in range(NS):
                mm = nc.tensor.matmul(k.w1[0:H, 0:SEG],
                                      wts["wb"][:, ki * H:(ki + 1) * H],
                                      ln_seg[:, ki * SEG:(ki + 1) * SEG],
                                      start=(ki == 0), stop=(ki == NS - 1))
            k.pe.inc(mm)
            k.w(nc.scalar, k.pe)
            k.act.inc(k.A.activation(beta[:, 0:SEG], k.w1[0:H, 0:SEG],
                                           AF.Sigmoid))
            if sg == 0:
                for s in range(NS):
                    k.dve.inc(k.V.memset(
                        vc[:, s * (SEG + PAD):s * (SEG + PAD) + PAD], 0.0))
            k.w(nc.tensor, k.act)
            if k.asub <= 2:
                continue

            k.w(nc.vector, k.act)
            k.dve.inc(k.V.tensor_copy(beta_b[:, 0:SEG],
                                            beta[:, 0:SEG]))
            k.w(nc.tensor, k.dve)
            for h in range(H):
                k.pe.inc(nc.tensor.matmul(
                    k.w1[:, 0:SEG], k.rowrepb[0:H, h * 128:(h + 1) * 128],
                    beta_b[:, 0:SEG], start=True, stop=True))
                k.w(nc.vector, k.pe)
                k.dve.inc(k.V.tensor_copy(bbc[:, h * SEG:(h + 1) * SEG],
                                                k.w1[:, 0:SEG]))
                k.w(nc.tensor, k.dve)
            bbc4 = bbc[:].rearrange("p (h t) -> p h t", h=H) \
                .unsqueeze(2).broadcast_to([128, H, 2, SEG])
            kb4 = kb[:].rearrange("p (h two t) -> p h two t", h=H, two=2)
            kn4 = kn[:].rearrange("p (h two t) -> p h two t", h=H, two=2)
            vb4 = vb[:].rearrange("p (h two t) -> p h two t", h=H, two=2)
            vc4 = vc[:].rearrange("p (h two t) -> p h two t", h=H,
                                  two=2)[:, :, :, PAD:PAD + SEG]
            k.dve.inc(k.V.tensor_tensor(kb4, kn4, bbc4, OP.mult))
            k.w(nc.vector, k.dve)
            k.dve.inc(k.V.tensor_tensor(vb4, vc4, bbc4, OP.mult))
            k.w(nc.tensor, k.dve)
            if k.asub <= 3:
                continue

            GPS_SLABS = (2, 5, 7)

            def emit_fir(which):
                for bi, Kk in enumerate(FIRK):
                    wf = wts["fir%d" % bi]
                    for s in range(NS):
                        if (s in GPS_SLABS) != (which == "gps"):
                            continue
                        base = s * (SEG + PAD) + PAD - (Kk - 1)
                        if which == "gps":
                            k.w(nc.gpsimd, k.dve, k.act)
                            scr = k.scr2[:, 0:SEG]
                            tmp = k.scr3[:, 256:256 + SEG]
                            wb = lambda j: wf[:, s * Kk + j:s * Kk + j + 1] \
                                .broadcast_to([128, SEG])
                            k.pool.inc(k.G.tensor_tensor(
                                scr, vc[:, base:base + SEG], wb(0),
                                OP.mult))
                            for j in range(1, Kk):
                                last = (j == Kk - 1)
                                outap = (br_ap(br, nbr, bi, s % 2, 0, SEG)
                                         [:, s // 2, :] if last else scr)
                                k.pool.inc(k.G.tensor_tensor(
                                    tmp, vc[:, base + j:base + j + SEG],
                                    wb(j), OP.mult))
                                k.pool.inc(k.G.tensor_tensor(
                                    outap, scr, tmp, OP.add))
                        else:
                            k.w(nc.vector, k.act)
                            scr = k.scr[:, 0:SEG]
                            k.dve.inc(k.V.tensor_scalar(
                                scr, vc[:, base:base + SEG],
                                wf[:, s * Kk:s * Kk + 1], None, OP.mult))
                            for j in range(1, Kk):
                                last = (j == Kk - 1)
                                outap = (br_ap(br, nbr, bi, s % 2, 0, SEG)
                                         [:, s // 2, :] if last else scr)
                                k.dve.inc(k.V.scalar_tensor_tensor(
                                    outap, vc[:, base + j:base + j + SEG],
                                    wf[:, s * Kk + j:s * Kk + j + 1],
                                    scr, OP.mult, OP.add))

            emit_fir("gps")

            dsc = contextlib.ExitStack()
            with dsc:
                sfx = "_%d" % sg
                x0a = k.sb(p + "x0" + sfx, [128, 512], BF16, dsc)
                kTM = k.sb(p + "kTM" + sfx, [128, NCHS * 256], BF16, dsc)
                u0a = k.sb(p + "u0" + sfx, [128, NCHS * 256], BF16, dsc)
                wFM = k.sb(p + "wFM" + sfx, [128, NCHS * 256], BF16, dsc)
                tja = k.sb(p + "tja" + sfx, [128, 512], BF16, dsc)
                t4 = k.sb(p + "t4" + sfx, [128, 512], BF16, dsc)
                Nb = k.sb(p + "Nb" + sfx, [128, 128], BF16, dsc)
                Ntb = k.sb(p + "Ntb" + sfx, [128, 128], BF16, dsc)
                pw2 = k.sb(p + "pw" + sfx, [128, 6 * 128], BF16, dsc)
                mI = k.sb(p + "mI" + sfx, [128, 5 * 128], BF16, dsc)
                uTM = k.sb(p + "uTM" + sfx, [128, 256], BF16, dsc)
                uBD = k.sb(p + "uBD" + sfx, [128, 1024], BF16, dsc)
                attnT = k.sb(p + "aT" + sfx, [128, 128], BF16, dsc)
                emit_delta(k, locals())
            if k.asub <= 4 or (30 < k.asub < 40):
                continue

            for s in range(NS):
                k.dve.inc(k.V.tensor_copy(
                    br_ap(br, nbr, vbi, s % 2, 0, SEG)[:, s // 2, :],
                    vc[:, s * (SEG + PAD) + PAD:(s + 1) * (SEG + PAD)]))
            emit_fir("dve")
            k.w(nc.scalar, k.dve)
            k.w(nc.scalar, k.pool)
            k.w(nc.vector, k.pool)
            k.w(nc.tensor, k.dve)
            k.w(nc.tensor, k.pool)
            if k.asub <= 5:
                continue

            gsc = contextlib.ExitStack()
            with gsc:
                sfx = "_%d" % sg
                lg = k.sb(p + "lg" + sfx, [nlg, SEG], F32, gsc)
                lgh = (k.sb(p + "lgh" + sfx, [128, SEG], F32, gsc)
                       if var == 1 else None)
                stats = k.sb(p + "st" + sfx,
                             [nlg if var == 0 else 128, SEG], F32, gsc)
                stats_b = k.sb(p + "stb" + sfx,
                               [nlg if var == 0 else 128, SEG], BF16, gsc)
                ptm = k.sb(p + "ptm" + sfx, [128, (SEG // 128) * nlg], F32,
                           gsc)
                probs = k.sb(p + "pr" + sfx, [nlg, SEG], F32, gsc)
                probs_b = k.sb(p + "prb" + sfx, [nlg, SEG], BF16, gsc)
                pbcw = k.sb(p + "pbc" + sfx, [128, H * SEG], BF16, gsc)
                oc = k.wide
                ocb = k.sb(p + "ocb" + sfx, [128, NS * SEG], BF16, gsc)
                ao = k.wide2
                h1 = k.sb(p + "h1" + sfx, [128, nmo_h1 * SEG], BF16, gsc)

                if var == 0:
                    emit_gate_l0(k, wts, br, nbr, ln_seg, stats, stats_b, lg,
                                 ptm, probs, h1, gw1x_hbm, gw1db)
                else:
                    emit_gate_l1(k, wts, br, nbr, ln_seg, stats, stats_b, lgh,
                                 ptm, probs, h1)
                if k.asub <= 6:
                    continue

                k.w(nc.vector, k.dve, k.act)
                k.dve.inc(k.V.tensor_copy(probs_b[0:nlg, :],
                                                probs[0:nlg, :]))
                k.dve.inc(k.V.memset(oc[:, 0:NS * SEG], 0.0))
                br5 = br[:].rearrange("p (b h g t) -> p b h g t", b=nbr,
                                      h=H, g=2)
                pb4 = pbcw[:].rearrange("p (h t) -> p h t", h=H) \
                    .unsqueeze(2).broadcast_to([128, H, 2, SEG])
                tmpf = k.wide2[:, 0:NS * SEG]
                tmp4 = tmpf.rearrange("p (h two t) -> p h two t", h=H,
                                      two=2)
                for bi in range(nbr):
                    k.w(nc.tensor, k.dve)
                    for h in range(H):
                        row = (h * 6 + bi) if var == 0 else (h * 4 + bi)
                        bank = k.w1 if h < 2 else k.w2
                        mm = nc.tensor.matmul(
                            bank[:, (h % 2) * SEG:(h % 2 + 1) * SEG],
                            k.rowrepb[0:nlg, row * 128:(row + 1) * 128],
                            probs_b[0:nlg, 0:SEG], start=True, stop=True)
                    k.pe.inc(mm)
                    k.w(nc.vector, k.pe)
                    k.dve.inc(k.V.tensor_copy(pbcw[:, 0:2 * SEG],
                                              k.w1[:, 0:2 * SEG]))
                    k.dve.inc(k.V.tensor_copy(pbcw[:, 2 * SEG:4 * SEG],
                                              k.w2[:, 0:2 * SEG]))
                    k.w(nc.vector, k.dve)
                    k.dve.inc(k.V.tensor_tensor(tmp4, br5[:, bi], pb4,
                                                OP.mult))
                    k.w(nc.vector, k.dve)
                    k.dve.inc(k.V.tensor_tensor(oc[:, 0:NS * SEG],
                                                oc[:, 0:NS * SEG], tmpf,
                                                OP.add))
                    k.w(nc.vector, k.dve)
                k.w(nc.scalar, k.dve)
                k.act.inc(k.A.activation(k.sq[:, 0:NS * SEG],
                                         oc[:, 0:NS * SEG], AF.Square))
                k.w(nc.tensor, k.act, k.dve)
                for h in range(H):
                    bank = k.w1 if h < 2 else k.w2
                    reg = bank[:, (h % 2) * SEG:(h % 2 + 1) * SEG]
                    nc.tensor.matmul(reg, k.ones128[:],
                                     k.sq[:, 2 * h * SEG:(2 * h + 1) * SEG],
                                     start=True, stop=False)
                    mm = nc.tensor.matmul(
                        reg, k.ones128[:],
                        k.sq[:, (2 * h + 1) * SEG:(2 * h + 2) * SEG],
                        start=False, stop=True)
                k.pe.inc(mm)
                k.w(nc.vector, k.pe)
                k.dve.inc(k.V.tensor_copy(k.wide2[:, 0:512],
                                          k.w1[:, 0:512]))
                k.dve.inc(k.V.tensor_copy(k.wide2[:, 512:1024],
                                          k.w2[:, 0:512]))
                k.w(nc.vector, k.dve)
                k.dve.inc(k.V.tensor_scalar(k.wide2[:, 0:1024],
                                            k.wide2[:, 0:1024], 1.0 / DV,
                                            None, OP.mult))
                k.rsqrt(k.wide2[:, 0:1024], 1e-5)
                rms4 = k.wide2[:, 0:1024] \
                    .rearrange("p (h t) -> p h t", h=H) \
                    .unsqueeze(2).broadcast_to([128, H, 2, SEG])
                oc4 = oc[:, 0:NS * SEG] \
                    .rearrange("p (h two t) -> p h two t", h=H, two=2)
                k.w(nc.vector, k.act)
                k.dve.inc(k.V.tensor_tensor(oc4, oc4, rms4, OP.mult))
                k.w(nc.vector, k.dve)
                for vh in range(2):
                    ocb3 = ocb[:].rearrange("p (h two t) -> p h two t",
                                            h=H, two=2)[:, :, vh, :]
                    oc3 = oc[:, 0:NS * SEG].rearrange(
                        "p (h two t) -> p h two t", h=H, two=2)[:, :, vh, :]
                    k.dve.inc(k.V.tensor_scalar(
                        ocb3, oc3, wts["onw"][:, vh:vh + 1], None,
                        OP.mult))
                k.w(nc.vector, k.dve)
                pend = None
                for mo in range(NS):
                    bank = k.w1 if mo % 2 == 0 else k.w2
                    k.w(nc.tensor, k.dve, k.act)
                    for ki in range(NS):
                        mm = nc.tensor.matmul(
                            bank[:, 0:SEG],
                            wts["wo"][:, (ki * NS + mo) * 128:
                                      (ki * NS + mo) * 128 + 128],
                            ocb[:, ki * SEG:(ki + 1) * SEG],
                            start=(ki == 0), stop=(ki == NS - 1))
                    k.pe.inc(mm)
                    if pend is not None:
                        k.evac(*pend, eng="act" if mo % 2 == 0 else "dve")
                    pend = (ao[:, mo * SEG:(mo + 1) * SEG], bank[:, 0:SEG])
                k.evac(*pend, eng="act")

                if dbg_write is not None:
                    dbg_write(sg, locals())

                emit_ln(k, None, SEG, eps=1e-5,
                        src3=ao[:].rearrange("p (s t) -> p s t", s=NS),
                        res3=xr_seg[:].rearrange("p (s t) -> p s t", s=NS))
            k.w(nc.sync, k.dve)
            nc.sync.dma_start(
                out=xr_slab_ap(xr_hbm, t0, SEG),
                in_=xr_seg[:].rearrange("p (s t) -> p s t", s=NS)
            ).then_inc(k.dma.sem, 16)
            k.dma.v += 16

            if sg < NSEG - 1:
                for s in range(NS):
                    k.dve.inc(k.V.tensor_copy(
                        vc[:, s * (SEG + PAD):s * (SEG + PAD) + PAD],
                        vc[:, s * (SEG + PAD) + SEG:
                           s * (SEG + PAD) + SEG + PAD]))
    k.hb_cfg = None
    k.wall(k.dma)


def emit_delta(k, L):
    nc = k.nc
    (glob0, globlast, qn, kn, vc, kb, vb, x0a, kTM, u0a, wFM, tja, t4, Nb, Ntb, pw2,
     mI, uTM, uBD, attnT, S_sb, br, nbr, dbi, PAD, idn, negsl, triu) = (
        L["glob0"], L["globlast"], L["qn"], L["kn"], L["vc"], L["kb"],
        L["vb"], L["x0a"],
        L["kTM"], L["u0a"], L["wFM"], L["tja"], L["t4"], L["Nb"], L["Ntb"],
        L["pw2"], L["mI"], L["uTM"], L["uBD"], L["attnT"], L["S_sb"],
        L["br"],
        L["nbr"], L["dbi"], L["PAD"], L["idn"], L["negsl"], L["triu"])
    SEGPAD = SEG + PAD
    dl = k.asub - 30 if 30 < k.asub < 40 else 99

    for c in range(NCHS):
        cc = c * CH
        k.w(nc.tensor, k.dve, k.act)
        for h in range(H):
            for dsl in range(2):
                s = 2 * h + dsl
                nc.tensor.transpose(
                    k.w4b[32 * h:32 * h + 32, dsl * 128:dsl * 128 + 128],
                    kn[:, s * SEG + cc:s * SEG + cc + CH],
                    idn[:], tile_position=(0, 32 * h))
                nc.tensor.transpose(
                    k.w4b[32 * h:32 * h + 32,
                          256 + dsl * 128:256 + dsl * 128 + 128],
                    kb[:, s * SEG + cc:s * SEG + cc + CH],
                    idn[:], tile_position=(0, 32 * h))
                mm = nc.tensor.transpose(
                    k.w4b[32 * h:32 * h + 32,
                          512 + dsl * 128:512 + dsl * 128 + 128],
                    vb[:, s * SEG + cc:s * SEG + cc + CH],
                    idn[:], tile_position=(0, 32 * h))
        k.pe.inc(mm)
        k.w(nc.vector, k.pe)
        k.dve.inc(k.V.tensor_copy(kTM[:, c * 256:(c + 1) * 256],
                                        k.w4b[:, 0:256]))
        k.dve.inc(k.V.tensor_copy(
            x0a[:, 256:512], k.w4b[:, 256:512]))
        k.dve.inc(k.V.tensor_copy(
            x0a[:, 0:256], k.w4b[:, 512:768]))
        k.w(nc.tensor, k.dve)
        if dl <= 1:
            continue

        for h in range(H):
            for dsl in range(2):
                s = 2 * h + dsl
                mm = nc.tensor.matmul(
                    k.w1[32 * h:32 * h + 32, 32 * h:32 * h + 32],
                    kb[:, s * SEG + cc:s * SEG + cc + CH],
                    kn[:, s * SEG + cc:s * SEG + cc + CH],
                    start=(dsl == 0), stop=(dsl == 1),
                    tile_position=(0, 32 * h))
        k.pe.inc(mm)
        k.w(nc.vector, k.pe)
        k.dve.inc(k.V.tensor_tensor(
            Nb[:], k.w1[:, 0:128], negsl[:], OP.mult))
        k.w(nc.tensor, k.dve)
        k.pe.inc(nc.tensor.transpose(k.w4b[:, 0:128], Nb[:], idn[:]))
        k.w(nc.vector, k.pe)
        k.dve.inc(k.V.tensor_copy(Ntb[:], k.w4b[:, 0:128]))
        k.dve.inc(k.V.tensor_tensor(mI[:, 0:128], k.w4b[:, 0:128],
                                          idn[:], OP.add))
        if dl <= 2:
            continue

        P = lambda i: pw2[:, i * 128:(i + 1) * 128]
        MI = lambda i: mI[:, i * 128:(i + 1) * 128]

        def mmsq(bank, lhsT, rhs):
            k.w(nc.tensor, k.dve, k.act)
            k.pe.inc(nc.tensor.matmul(bank[:, 0:128], lhsT, rhs,
                                      start=True, stop=True))

        def ev_plain(dst, bank):
            k.w(nc.scalar, k.pe)
            k.act.inc(k.A.copy(dst, bank[:, 0:128]))

        def ev_plusI(dst, bank):
            k.w(nc.vector, k.pe)
            k.dve.inc(k.V.tensor_tensor(dst, bank[:, 0:128], idn[:],
                                              OP.add))

        mmsq(k.w2, Ntb[:], Nb[:])
        mmsq(k.w3, Nb[:], Ntb[:])
        ev_plain(P(0), k.w2)
        ev_plain(P(1), k.w3); ev_plusI(MI(1), k.w3)
        mmsq(k.w2, P(1), P(0))
        mmsq(k.w3, P(0), P(1))
        ev_plain(P(2), k.w2)
        ev_plain(P(3), k.w3); ev_plusI(MI(2), k.w3)
        mmsq(k.w2, P(3), P(2))
        mmsq(k.w3, P(2), P(3))
        ev_plain(P(4), k.w2)
        ev_plain(P(5), k.w3); ev_plusI(MI(3), k.w3)
        mmsq(k.w2, P(4), P(5))
        ev_plusI(MI(4), k.w2)
        if dl <= 3:
            continue

        src = x0a[:, 0:512]
        for jj, mi_i in enumerate((4, 3, 2, 1)):
            dst = tja[:] if mi_i != 1 else t4[:]
            k.w(nc.tensor, k.dve, k.act)
            k.pe.inc(nc.tensor.matmul(k.w3[:, 0:512], MI(mi_i), src,
                                      start=True, stop=True))
            if jj == 1:
                k.w(nc.scalar, k.pe)
                k.act.inc(k.A.copy(dst, k.w3[:, 0:512]))
            else:
                k.w(nc.vector, k.pe)
                k.dve.inc(k.V.tensor_copy(dst, k.w3[:, 0:512]))
            src = dst
        k.w(nc.tensor, k.dve)
        k.pe.inc(nc.tensor.matmul(k.w3[:, 0:256], MI(0), t4[:, 0:256],
                                  start=True, stop=True))
        k.w(nc.vector, k.pe)
        k.dve.inc(k.V.tensor_copy(u0a[:, c * 256:(c + 1) * 256],
                                        k.w3[:, 0:256]))
        if dl <= 4:
            continue
        k.w(nc.tensor, k.dve)
        for dsl in range(2):
            tgt = k.w1 if dsl == 0 else k.w2
            lh = t4[:, 256 + dsl * 128:256 + dsl * 128 + 128]
            nc.tensor.matmul(tgt[:, 0:128], lh, Ntb[:],
                             start=True, stop=False)
            mm = nc.tensor.matmul(tgt[:, 0:128], lh, idn[:],
                                  start=False, stop=True)
        k.pe.inc(mm)
        k.w(nc.vector, k.pe)
        k.dve.inc(k.V.tensor_copy(wFM[:, c * 256:c * 256 + 128],
                                        k.w1[:, 0:128]))
        k.dve.inc(k.V.tensor_copy(wFM[:, c * 256 + 128:c * 256 + 256],
                                        k.w2[:, 0:128]))
        k.w(nc.tensor, k.dve)

    if dl <= 5:
        return
    for c in range(NCHS):
        cc = c * CH
        k.w(nc.tensor, k.dve, k.act)
        for h in range(H):
            for dsl in range(2):
                mm = nc.tensor.matmul(
                    k.w1[32 * h:32 * h + 32, 0:256],
                    wFM[:, c * 256 + dsl * 128 + 32 * h:
                        c * 256 + dsl * 128 + 32 * h + 32],
                    S_sb[:, dsl * 1024 + h * 256:dsl * 1024 + (h + 1) * 256],
                    start=(dsl == 0), stop=(dsl == 1),
                    tile_position=(0, 32 * h))
        k.pe.inc(mm)
        k.w(nc.vector, k.pe)
        k.dve.inc(k.V.tensor_tensor(
            uTM[:], u0a[:, c * 256:(c + 1) * 256], k.w1[:, 0:256],
            OP.subtract))
        if c == 0:
            k.dve.inc(k.V.memset(uBD[:], 0.0))
        k.w(nc.vector, k.pe, k.dve)
        for h in range(H):
            k.dve.inc(k.V.tensor_copy(
                uBD[32 * h:32 * h + 32, h * 256:(h + 1) * 256],
                uTM[32 * h:32 * h + 32, 0:256]))
        k.w(nc.tensor, k.dve)
        if dl <= 6:
            continue
        for h in range(H):
            for dsl in range(2):
                mm = nc.tensor.matmul(
                    k.w2[32 * h:32 * h + 32, 32 * h:32 * h + 32],
                    kn[:, (2 * h + dsl) * SEG + cc:
                       (2 * h + dsl) * SEG + cc + CH],
                    qn[:, (2 * h + dsl) * SEG + cc:
                       (2 * h + dsl) * SEG + cc + CH],
                    start=(dsl == 0), stop=(dsl == 1),
                    tile_position=(0, 32 * h))
        k.pe.inc(mm)
        k.w(nc.vector, k.pe)
        k.dve.inc(k.V.tensor_tensor(attnT[:], k.w2[:, 0:128], triu[:],
                                          OP.mult))
        k.w(nc.tensor, k.dve)
        if dl <= 7:
            continue
        first = True
        for h in range(H):
            for vh in range(2):
                for dsl in range(2):
                    nc.tensor.matmul(
                        k.w3[:, vh * 128 + 32 * h:vh * 128 + 32 * h + 32],
                        S_sb[:, dsl * 1024 + h * 256 + vh * 128:
                             dsl * 1024 + h * 256 + vh * 128 + 128],
                        qn[:, (2 * h + dsl) * SEG + cc:
                           (2 * h + dsl) * SEG + cc + CH],
                        start=first, stop=False)
                    first = False
        for vh in range(2):
            mm = nc.tensor.matmul(
                k.w3[:, vh * 128:vh * 128 + 128],
                uTM[:, vh * 128:vh * 128 + 128],
                attnT[:],
                start=False, stop=(vh == 1))
        k.pe.inc(mm)
        k.w(nc.vector, k.pe)
        for vh in range(2):
            k.dve.inc(k.V.tensor_copy(
                br_ap(br, nbr, dbi, vh, cc, CH),
                k.w3[:, vh * 128:vh * 128 + 128]
                .rearrange("p (h c) -> p h c", h=H)))
        k.w(nc.tensor, k.dve)
        if dl <= 8:
            continue
        for dsl in range(2):
            Sp = k.S0 if dsl == 0 else k.S1
            for half in range(2):
                bank_first = (glob0 and c == 0)
                bank_last = (globlast and c == NCHS - 1)
                mm = nc.tensor.matmul(
                    Sp[:, half * 512:(half + 1) * 512],
                    kTM[:, c * 256 + dsl * 128:c * 256 + dsl * 128 + 128],
                    uBD[:, half * 512:(half + 1) * 512],
                    start=bank_first, stop=bank_last,
                    skip_group_check=True)
        k.pe.inc(mm)
        k.w(nc.vector, k.pe)
        k.w(nc.scalar, k.pe)
        k.dve.inc(k.V.tensor_copy(S_sb[:, 0:1024], k.S0[:]))
        k.act.inc(k.A.copy(S_sb[:, 1024:2048], k.S1[:]))
        k.w(nc.tensor, k.dve, k.act)


def emit_gate_l0(k, wts, br, nbr, ln_seg, stats, stats_b, lg, ptm, probs,
                 h1, gw1x_hbm, gw1db):
    nc = k.nc
    k.w(nc.gpsimd, k.act, k.dve)
    k.str_e.inc(nc.gpsimd.dma_start(out=gw1db[:, 0:1024],
                                    in_=gw1x_hbm[:, 0:1024]), 16)
    first = True
    for bi in range(nbr):
        k.w(nc.scalar, k.pe, k.dve)
        for s in range(NS):
            vh, h = s % 2, s // 2
            k.act.inc(k.A.activation(
                k.sq[:, s * SEG:(s + 1) * SEG],
                br_ap(br, nbr, bi, vh, 0, SEG)[:, h, :], AF.Abs))
        k.w(nc.tensor, k.act)
        for h in range(H):
            row = h * 6 + bi
            for sl in (2 * h, 2 * h + 1):
                mm = nc.tensor.matmul(
                    k.w2[0:24, 0:SEG],
                    k.rowsel[:, row * 24:(row + 1) * 24],
                    k.sq[:, sl * SEG:(sl + 1) * SEG],
                    start=first,
                    stop=(bi == nbr - 1 and h == H - 1 and sl == 2 * h + 1))
                first = False
        k.pe.inc(mm)
    k.w(nc.vector, k.pe)
    k.dve.inc(k.V.tensor_scalar(stats[0:24, 0:SEG], k.w2[0:24, 0:SEG],
                                      1.0 / DV, None, OP.mult))
    k.w(nc.vector, k.dve)
    k.dve.inc(k.V.tensor_copy(stats_b[0:24, 0:SEG], stats[0:24, 0:SEG]))
    k.w(nc.tensor, k.dve)

    pend = None

    def l0_gelu(p):
        mo_, bank_ = p
        k.w(nc.scalar, k.pe)
        k.act.inc(k.A.activation(
            h1[:, mo_ * SEG:(mo_ + 1) * SEG], bank_[:, 0:SEG], AF.Gelu,
            bias=wts["gb1"][:, mo_:mo_ + 1]))

    for mo in range(16):
        bank = k.w1 if mo % 2 == 0 else k.w3
        if mo + 1 < 16:
            k.w(nc.gpsimd, k.pe)
            ctr = k.str_o if (mo + 1) % 2 else k.str_e
            ctr.inc(nc.gpsimd.dma_start(
                out=gw1db[:, ((mo + 1) % 2) * 1024:((mo + 1) % 2) * 1024 + 1024],
                in_=gw1x_hbm[:, (mo + 1) * 1024:(mo + 2) * 1024]), 16)
        k.w(nc.tensor, k.dve, k.act)
        pc = k.str_o if mo % 2 else k.str_e
        nc.tensor.wait_ge(pc.sem, pc.v)
        for ki in range(NS):
            nc.tensor.matmul(
                bank[:, 0:SEG],
                gw1db[:, (mo % 2) * 1024 + ki * 128:
                      (mo % 2) * 1024 + (ki + 1) * 128],
                ln_seg[:, ki * SEG:(ki + 1) * SEG],
                start=(ki == 0), stop=False)
        mm = nc.tensor.matmul(
            bank[:, 0:SEG], wts["gw1s"][0:24, mo * 128:(mo + 1) * 128],
            stats_b[0:24, 0:SEG], start=False, stop=True)
        k.pe.inc(mm)
        if pend is not None:
            l0_gelu(pend)
        pend = (mo, bank)
    l0_gelu(pend)
    k.w(nc.tensor, k.act)
    for ki in range(16):
        mm = nc.tensor.matmul(k.w2[0:24, 0:SEG],
                              wts["gw2"][:, ki * 24:(ki + 1) * 24],
                              h1[:, ki * SEG:(ki + 1) * SEG],
                              start=(ki == 0), stop=(ki == 15))
    k.pe.inc(mm)
    k.w(nc.vector, k.pe)
    k.dve.inc(k.V.tensor_scalar(lg[:, 0:SEG], k.w2[0:24, 0:SEG],
                                      wts["gb2"][0:24, 0:1], None, OP.add))
    k.w(nc.vector, k.dve)
    k.dve.inc(k.V.scalar_tensor_tensor(
        lg[:, 0:SEG], stats[0:24, 0:SEG], wts["alpha"][0:24, 0:1],
        lg[:, 0:SEG], OP.mult, OP.add))
    k.w(nc.scalar, k.dve)
    k.act.inc(k.A.activation(k.scr2[0:24, 0:1], wts["gtemp"][0:24, 0:1],
                             AF.Exp))
    k.act.inc(k.A.activation(k.scr2[0:24, 0:1], k.scr2[0:24, 0:1],
                             AF.Ln, bias=k.cst(1.0)[0:24]))
    k.w(nc.vector, k.act)
    k.dve.inc(k.V.tensor_scalar(k.scr2[0:24, 0:1], k.scr2[0:24, 0:1],
                                      1e-4, None, OP.add))
    k.w(nc.vector, k.dve)
    k.dve.inc(k.V.reciprocal(k.scr2[0:24, 0:1], k.scr2[0:24, 0:1]))
    k.w(nc.vector, k.dve)
    k.dve.inc(k.V.tensor_scalar(lg[:, 0:SEG], lg[:, 0:SEG],
                                      k.scr2[0:24, 0:1], None, OP.mult))
    emit_softmax_groups(k, lg, ptm, probs, nrows=24, ngrp=H, grp=6,
                        floor=0.05, floor_mode="last")


def emit_gate_l1(k, wts, br, nbr, ln_seg, stats, stats_b, lgh, ptm, probs,
                 h1):
    nc = k.nc
    for h in range(H):
        first = True
        for bi in range(nbr):
            k.w(nc.scalar, k.pe, k.dve)
            for vh in range(2):
                bsl = br_ap(br, nbr, bi, vh, 0, SEG)[:, h, :]
                k.act.inc(k.A.activation(
                    k.sq[:, vh * SEG:(vh + 1) * SEG], bsl, AF.Square))
                k.act.inc(k.A.activation(
                    k.sq[:, (2 + vh) * SEG:(3 + vh) * SEG], bsl, AF.Abs))
            k.w(nc.tensor, k.act)
            for vh in range(2):
                bsl = br_ap(br, nbr, bi, vh, 0, SEG)[:, h, :]
                rS, rSq, rA, rL2 = 4 + bi, 8 + bi, 12 + bi, bi
                po = k.w2[32 * h:32 * h + 16, 0:SEG]
                nc.tensor.matmul(po, k.rowsel16[:, rS * 16:(rS + 1) * 16],
                                 bsl, start=first, stop=False,
                                 tile_position=(0, 32 * h))
                first = False
                nc.tensor.matmul(po,
                                 k.rowsel16[:, rSq * 16:(rSq + 1) * 16],
                                 k.sq[:, vh * SEG:(vh + 1) * SEG],
                                 start=False, stop=False,
                                 tile_position=(0, 32 * h))
                nc.tensor.matmul(po,
                                 k.rowsel16[:, rL2 * 16:(rL2 + 1) * 16],
                                 k.sq[:, vh * SEG:(vh + 1) * SEG],
                                 start=False, stop=False,
                                 tile_position=(0, 32 * h))
                mm = nc.tensor.matmul(
                    po, k.rowsel16[:, rA * 16:(rA + 1) * 16],
                    k.sq[:, (2 + vh) * SEG:(3 + vh) * SEG],
                    start=False, stop=(bi == nbr - 1 and vh == 1),
                    tile_position=(0, 32 * h))
            k.pe.inc(mm)
        k.w(nc.vector, k.pe)
        allv = stats[32 * h:32 * h + 16, 0:SEG]
        k.dve.inc(k.V.tensor_scalar(
            allv, k.w2[32 * h:32 * h + 16, 0:SEG], 1.0 / DV, None, OP.mult))
        k.w(nc.vector, k.dve)
        k.dve.inc(k.V.tensor_tensor(
            k.scr3[32 * h:32 * h + 16, 0:SEG], allv, allv, OP.mult))
        k.w(nc.tensor, k.dve)
        k.pe.inc(nc.tensor.matmul(
            k.w1[32 * h:32 * h + 16, 0:SEG],
            k.shv16[32 * h:32 * h + 16, 0:16],
            k.scr3[32 * h:32 * h + 16, 0:SEG], start=True, stop=True,
            tile_position=(32 * h, 32 * h)))
        k.w(nc.vector, k.pe)
        k.dve.inc(k.V.tensor_tensor(
            allv, allv, k.w1[32 * h:32 * h + 16, 0:SEG], OP.subtract))
        view = stats[32 * h:32 * h + 4, 0:SEG]
        k.w(nc.scalar, k.dve)
        k.act.inc(k.A.activation(view, view, AF.Ln,
                                 bias=k.cst(1e-37)[32 * h:32 * h + 4],
                                 scale=float(DV)))
        k.act.inc(k.A.activation(view, view, AF.Exp, scale=0.5))
        k.w(nc.vector, k.act)
        k.dve.inc(k.V.tensor_copy(stats_b[32 * h:32 * h + 16, 0:SEG],
                                        stats[32 * h:32 * h + 16, 0:SEG]))
    k.w(nc.tensor, k.dve)

    for h in range(H):
        pend = None

        def l1_gelu(p):
            mo_, bank_ = p
            k.w(nc.scalar, k.pe)
            k.act.inc(k.A.activation(
                h1[:, mo_ * SEG:(mo_ + 1) * SEG], bank_[:, 0:SEG],
                AF.Gelu_apprx_tanh, bias=wts["gb1"][:, mo_:mo_ + 1]))

        for mo in range(8):
            bank = k.w1 if mo % 2 == 0 else k.w3
            k.w(nc.tensor, k.dve, k.act)
            for ki in range(NS):
                nc.tensor.matmul(
                    bank[:, 0:SEG],
                    wts["gw1x"][:, (ki * 8 + mo) * 128:(ki * 8 + mo) * 128 + 128],
                    ln_seg[:, ki * SEG:(ki + 1) * SEG],
                    start=(ki == 0), stop=False)
            mm = nc.tensor.matmul(
                bank[:, 0:SEG],
                wts["gw1s"][32 * h:32 * h + 16, mo * 128:(mo + 1) * 128],
                stats_b[32 * h:32 * h + 16, 0:SEG], start=False, stop=True,
                tile_position=(32 * h, 0))
            k.pe.inc(mm)
            if pend is not None:
                l1_gelu(pend)
            pend = (mo, bank)
        l1_gelu(pend)
        k.w(nc.tensor, k.act)
        for ki in range(8):
            mm = nc.tensor.matmul(k.w2[32 * h:32 * h + 4, 0:SEG],
                                  wts["gw2"][:, ki * 4:(ki + 1) * 4],
                                  h1[:, ki * SEG:(ki + 1) * SEG],
                                  start=(ki == 0), stop=(ki == 7),
                                  tile_position=(0, 32 * h))
        k.pe.inc(mm)
        hs = slice(32 * h, 32 * h + 4)
        k.w(nc.vector, k.pe)
        k.dve.inc(k.V.tensor_scalar(
            lgh[hs, 0:SEG], k.w2[hs, 0:SEG], wts["gb2"][hs, 0:1], None,
            OP.add))
        k.w(nc.vector, k.dve)
        k.dve.inc(k.V.tensor_scalar(
            lgh[hs, 0:SEG], lgh[hs, 0:SEG],
            wts["gcbd"][hs, 0:1], None, OP.add))
        k.w(nc.vector, k.dve)
        k.dve.inc(k.V.tensor_scalar(
            lgh[hs, 0:SEG], lgh[hs, 0:SEG], wts["grt"][hs, 0:1], None,
            OP.mult))
        k.w(nc.vector, k.dve)
    k.dve.inc(k.V.memset(k.scr3[:, 0:SEG], 0.0))
    k.w(nc.vector, k.dve)
    for h in range(H):
        k.dve.inc(k.V.tensor_copy(k.scr3[32 * h:32 * h + 4, 0:SEG],
                                  lgh[32 * h:32 * h + 4, 0:SEG]))
    k.w(nc.tensor, k.dve)
    k.pe.inc(nc.tensor.matmul(k.w2[0:16, 0:SEG], wts["sel16"][:, 0:16],
                              k.scr3[:, 0:SEG], start=True, stop=True))
    k.w(nc.vector, k.pe)
    k.dve.inc(k.V.tensor_copy(lgh[0:16, 0:SEG], k.w2[0:16, 0:SEG]))
    k.w(nc.vector, k.dve)
    DECAY = 1.0 - 1.0 / 3000.0
    emit_softmax_groups(k, lgh, ptm, probs, nrows=16, ngrp=H, grp=4,
                        floor=0.08 * DECAY, floor_mode="all", lgh=None)


def emit_ffn0(k, xr_hbm, wgu_hbm, wd_hbm):
    nc = k.nc
    T = k.T
    FSEG = min(1024, T)
    NF = T // FSEG
    NTT = FSEG // 256
    sc = contextlib.ExitStack()
    with sc:
        xrf = k.sb("f0_xr", [128, NS * FSEG], F32, sc)
        lnf = k.sb("f0_ln", [128, NS * FSEG], BF16, sc)
        hs = k.sb("f0_hs", [128, 24 * FSEG], BF16, sc)
        gu_db = k.sb("f0_gu", [128, 2 * 2048], BF16, sc)
        wd_db = k.sb("f0_wd", [128, 2 * 3072], BF16, sc)
        gsb = k.sb("f0_gsb", [128, 1024], F32, sc)
        ao = k.sb("f0_ao", [128, NS * FSEG], F32, sc)
        SC = 1.0 / 127.0
        TWD = min(512, FSEG)
        NTW = FSEG // TWD
        k.hb_cfg = True

        for f in range(NF):
            t0 = f * FSEG
            k.w(nc.sync, k.dve, k.act, k.pe, k.dma)
            nc.sync.dma_start(
                out=xrf[:].rearrange("p (s t) -> p s t", s=NS),
                in_=xr_slab_ap(xr_hbm, t0, FSEG)).then_inc(k.dma.sem, 16)
            k.dma.v += 16
            k.wall(k.dma)
            xrf3 = xrf[:].rearrange("p (s t) -> p s t", s=NS)
            lnf3 = lnf[:].rearrange("p (s t) -> p s t", s=NS)
            for sub in range(NTT):
                emit_ln(k, None, 256, eps=1e-5,
                        src3=xrf3[:, :, sub * 256:sub * 256 + 256],
                        out3=lnf3[:, :, sub * 256:sub * 256 + 256])
            def f0_consume(p):
                mo_, tt_, gb_, ub_, i_ = p
                gs = gsb[:, i_ * TWD:(i_ + 1) * TWD]
                k.w(nc.scalar, k.pe)
                k.act.inc(k.A.activation(gs, gb_[:, 0:TWD], AF.Silu,
                                         scale=SC))
                k.w(nc.vector, k.pe, k.act)
                k.dve.inc(k.V.scalar_tensor_tensor(
                    hs[:, mo_ * FSEG + tt_ * TWD:
                       mo_ * FSEG + tt_ * TWD + TWD],
                    ub_[:, 0:TWD], SC, gs, OP.mult, OP.mult))

            k.w(nc.gpsimd, k.act, k.dve)
            k.str_e.inc(nc.gpsimd.dma_start(out=gu_db[:, 0:2048],
                                            in_=wgu_hbm[:, 0:2048]), 16)
            pend = None
            for mo in range(24):
                par = mo % 2
                if mo + 1 < 24:
                    k.w(nc.gpsimd, k.pe)
                    ctr = k.str_o if (mo + 1) % 2 else k.str_e
                    ctr.inc(nc.gpsimd.dma_start(
                        out=gu_db[:, ((mo + 1) % 2) * 2048:
                                  ((mo + 1) % 2) * 2048 + 2048],
                        in_=wgu_hbm[:, (mo + 1) * 2048:(mo + 2) * 2048]), 16)
                pc = k.str_o if par else k.str_e
                nc.tensor.wait_ge(pc.sem, pc.v)
                for tt in range(NTW):
                    i = (mo * NTW + tt) % 2
                    gb = k.w1 if i == 0 else k.w3
                    ub = k.w2 if i == 0 else k.S0
                    k.w(nc.tensor, k.dve, k.act)
                    for ki in range(NS):
                        nc.tensor.matmul(
                            gb[:, 0:TWD],
                            gu_db[:, par * 2048 + ki * 128:
                                  par * 2048 + (ki + 1) * 128],
                            lnf[:, ki * FSEG + tt * TWD:
                                ki * FSEG + tt * TWD + TWD],
                            start=(ki == 0), stop=(ki == NS - 1))
                        mm = nc.tensor.matmul(
                            ub[:, 0:TWD],
                            gu_db[:, par * 2048 + 1024 + ki * 128:
                                  par * 2048 + 1024 + (ki + 1) * 128],
                            lnf[:, ki * FSEG + tt * TWD:
                                ki * FSEG + tt * TWD + TWD],
                            start=(ki == 0), stop=(ki == NS - 1))
                    k.pe.inc(mm)
                    if pend is not None:
                        f0_consume(pend)
                    pend = (mo, tt, gb, ub, i)
            f0_consume(pend)
            def f0_dconsume(p):
                mo_, tt_, bank_ = p
                k.w(nc.scalar, k.pe)
                k.act.inc(k.A.activation(
                    ao[:, mo_ * FSEG + tt_ * TWD:
                       mo_ * FSEG + tt_ * TWD + TWD],
                    bank_[:, 0:TWD], AF.Copy, scale=SC))

            k.w(nc.gpsimd, k.dve, k.act)
            k.str_e.inc(nc.gpsimd.dma_start(out=wd_db[:, 0:3072],
                                            in_=wd_hbm[:, 0:3072]), 16)
            pend = None
            for mo in range(NS):
                par = mo % 2
                if mo + 1 < NS:
                    k.w(nc.gpsimd, k.pe)
                    ctr = k.str_o if (mo + 1) % 2 else k.str_e
                    ctr.inc(nc.gpsimd.dma_start(
                        out=wd_db[:, ((mo + 1) % 2) * 3072:
                                  ((mo + 1) % 2) * 3072 + 3072],
                        in_=wd_hbm[:, (mo + 1) * 3072:(mo + 2) * 3072]), 16)
                pc = k.str_o if par else k.str_e
                nc.tensor.wait_ge(pc.sem, pc.v)
                for tt in range(NTW):
                    bank = k.w1 if (mo * NTW + tt) % 2 == 0 else k.w3
                    k.w(nc.tensor, k.dve, k.act)
                    for ki in range(24):
                        mm = nc.tensor.matmul(
                            bank[:, 0:TWD],
                            wd_db[:, par * 3072 + ki * 128:
                                  par * 3072 + (ki + 1) * 128],
                            hs[:, ki * FSEG + tt * TWD:
                               ki * FSEG + tt * TWD + TWD],
                            start=(ki == 0), stop=(ki == 23))
                    k.pe.inc(mm)
                    if pend is not None:
                        f0_dconsume(pend)
                    pend = (mo, tt, bank)
            f0_dconsume(pend)
            k.w(nc.vector, k.act)
            ao3 = ao[:].rearrange("p (s t) -> p s t", s=NS)
            for sub in range(NTT):
                emit_ln(k, None, 256, eps=1e-5,
                        src3=ao3[:, :, sub * 256:sub * 256 + 256],
                        res3=xrf3[:, :, sub * 256:sub * 256 + 256])
            k.w(nc.sync, k.dve)
            nc.sync.dma_start(
                out=xr_slab_ap(xr_hbm, t0, FSEG),
                in_=xrf[:].rearrange("p (s t) -> p s t", s=NS)
            ).then_inc(k.dma.sem, 16)
            k.dma.v += 16
    k.hb_cfg = None
    k.wall(k.dma)



def emit_dummy_y(k, y_out):
    nc = k.nc
    T = k.T
    zz = k.sb("dummy_y", [128, D], F32)
    k.w(nc.vector, k.dma, k.dmag)
    k.dve.inc(k.V.memset(zz[:], 0.0))
    k.w(nc.sync, k.dve)
    for n in range(T // 128):
        nc.sync.dma_start(out=y_out[n * 128:(n + 1) * 128, :],
                          in_=zz[:]).then_inc(k.dma.sem, 16)
        k.dma.v += 16
    k.wall(k.dma)


def emit_moe(k, xr_hbm, win, y_out, lim=99, wbh=None):
    nc = k.nc
    T = k.T
    NC = k.NC
    N = NC * T
    E, TK, LGK = 32, 4, 4
    EPB = E // NC
    CAP = max(1, int(1.25 * N * TK / E))
    CAPF = -(-CAP // 128) * 128
    PT = T // 16
    FLOOR_BIG = -3.0e38
    assert T % 16 == 0 and PT <= 128

    xf_pay = nc.dram_tensor("xf_pay", [T, D], BF16).ap()
    wf_pay = nc.dram_tensor("wf_pay", [E, T], BF16).ap()
    xf_all = nc.dram_tensor("xf_all", [N, D], BF16).ap()
    wf_all = nc.dram_tensor("wf_all", [NC * E, T], BF16).ap()
    eo_pay = nc.dram_tensor("eo_pay", [EPB * CAPF, D], BF16).ap()
    eo_all = nc.dram_tensor("eo_all", [NC * EPB * CAPF, D], BF16).ap()
    xr_tm = nc.dram_tensor("xr_tm", [T, D], F32).ap()
    z_hbm = nc.dram_tensor("z_hbm", [T, D], F32).ap()
    slot_hbm = nc.dram_tensor("slot_hbm", [(E + 4) * CAPF, 1], F32).ap()
    slot_all = nc.dram_tensor("slot_all", [(E + 4) * CAPF, 1], F32).ap()
    pos_hbm = nc.dram_tensor("pos_hbm", [128, 4 * (N // 16)], F32).ap()

    sc = contextlib.ExitStack()
    with sc:
        idxa = k.sb("m_idx", [128, 16 * TK], F32, sc)
        wna = k.sb("m_wn", [128, 16 * TK], F32, sc)
        p1 = contextlib.ExitStack()
        p1.__enter__()
        xrm = k.sb("m_xr", [128, NS * 256], F32, p1)
        xff = k.sb("m_xff", [128, NS * T], BF16, p1)
        hshf = k.sb("m_hshf", [128, NS * T], BF16, p1)
        lgf = k.sb("m_lgf", [32, T], F32, p1)
        stm = k.sb("m_stm", [128, 32], F32, p1)
        wdt = k.sb("m_wd", [128, 32], BF16, p1)
        io32 = k.sb("m_io32", [128, 32], F32, p1)
        io32g = k.sb("m_io32g", [128, 32], F32, p1)
        m32 = k.sb("m_m32", [128, 32], F32, p1)
        v8 = k.sb("m_v8", [128, 8], F32, p1)
        i8 = k.sb("m_i8", [128, 8], U32, p1)
        i8f = k.sb("m_i8f", [128, 8], F32, p1)
        tmx = k.sb("m_tmx", [128, 32], F32, p1)
        ztm = k.sb("m_ztm", [128, D], F32, p1)

        k.pool.inc(k.G.iota(io32[:], pattern=[[1, 32]], base=0,
                            channel_multiplier=0,
                            allow_small_or_imprecise_dtypes=True))
        k.pool.inc(k.G.iota(io32g[:].rearrange("p (g j) -> p g j", g=8),
                            pattern=[[1, 8], [0, 4]], base=0,
                            channel_multiplier=0,
                            allow_small_or_imprecise_dtypes=True))

        wgt = k.sb("m_wg", [128, 8 * 32], BF16, p1)
        sw13 = k.sb("m_sw13", [128, 2 * 64 * 128], BF16, p1)
        sw2 = k.sb("m_sw2", [128, 64 * 128], BF16, p1)
        k.hload(wgt[:], win["moe_wg"])
        k.hload(sw13[:], win["moe_sw13"])
        k.hload(sw2[:], win["moe_sw2"])
        k.wall(k.dmag)
        k.wall(k.pool)

        for sub in range(T // 256):
            t0 = sub * 256
            k.w(nc.sync, k.dve, k.act, k.pe, k.dma)
            nc.sync.dma_start(
                out=xrm[:].rearrange("p (s t) -> p s t", s=NS),
                in_=xr_slab_ap(xr_hbm, t0, 256)).then_inc(k.dma.sem, 16)
            k.dma.v += 16
            k.wall(k.dma)
            for tt in range(2):
                k.w(nc.vector, k.dma)
                for s in range(NS):
                    k.w(nc.tensor, k.dve)
                    k.pe.inc(nc.tensor.transpose(
                        (k.w1 if s % 2 == 0 else k.w2)[:, 0:128],
                        xrm[:, s * 256 + tt * 128:s * 256 + tt * 128 + 128],
                        k.identf[:]))
                    k.evac(ztm[:, s * 128:(s + 1) * 128],
                           (k.w1 if s % 2 == 0 else k.w2)[:, 0:128])
                k.w(nc.sync, k.dve)
                nc.sync.dma_start(
                    out=xr_tm[t0 + tt * 128:t0 + tt * 128 + 128, :],
                    in_=ztm[:, 0:D]).then_inc(k.dma.sem, 16)
                k.dma.v += 16
            k.w(nc.vector, k.dma)
            emit_ln(k, None, 256, eps=1e-5,
                    src3=xrm[:].rearrange("p (s t) -> p s t", s=NS),
                    out3=xff[:].rearrange("p (s t) -> p s t", s=NS)
                    [:, :, t0:t0 + 256])
            k.w(nc.tensor, k.dve)
            for ki in range(NS):
                mm = nc.tensor.matmul(k.w1[0:32, 0:256],
                                      wgt[:, ki * 32:(ki + 1) * 32],
                                      xff[:, ki * T + t0:ki * T + t0 + 256],
                                      start=(ki == 0), stop=(ki == NS - 1))
            k.pe.inc(mm)
            k.w(nc.vector, k.pe)
            k.dve.inc(k.V.tensor_copy(lgf[0:32, t0:t0 + 256],
                                      k.w1[0:32, 0:256]))
            for mo in range(NS):
                k.w(nc.tensor, k.dve, k.act)
                for ki in range(NS):
                    nc.tensor.matmul(
                        k.w2[:, 0:256],
                        sw13[:, (ki * NS + mo) * 128:(ki * NS + mo) * 128 + 128],
                        xff[:, ki * T + t0:ki * T + t0 + 256],
                        start=(ki == 0), stop=(ki == NS - 1))
                    mm = nc.tensor.matmul(
                        k.w3[:, 0:256],
                        sw13[:, 8192 + (ki * NS + mo) * 128:
                              8192 + (ki * NS + mo) * 128 + 128],
                        xff[:, ki * T + t0:ki * T + t0 + 256],
                        start=(ki == 0), stop=(ki == NS - 1))
                k.pe.inc(mm)
                k.w(nc.scalar, k.pe)
                k.act.inc(k.A.activation(k.scr[:, 0:256], k.w2[:, 0:256],
                                         AF.Silu))
                k.w(nc.vector, k.pe, k.act)
                k.dve.inc(k.V.tensor_tensor(
                    hshf[:, mo * T + t0:mo * T + t0 + 256], k.w3[:, 0:256],
                    k.scr[:, 0:256], OP.mult))
        k.w(nc.tensor, k.dve)

        PTs = slice(0, PT)
        for a in range(16):
            sl = lambda base: slice(base + a, base + T, 16)
            k.w(nc.vector, k.dma)
            k.w(nc.tensor, k.dve, k.act)
            for half in range(2):
                for ki in range(NS):
                    mm = nc.tensor.matmul(
                        (k.w2 if half == 0 else k.w3)[0:PT, 0:512],
                        hshf[:, sl(ki * T)],
                        sw2[:, ki * 1024 + half * 512:
                            ki * 1024 + half * 512 + 512],
                        start=(ki == 0), stop=(ki == NS - 1))
            k.pe.inc(mm)
            k.w(nc.vector, k.pe)
            k.dve.inc(k.V.tensor_copy(ztm[PTs, 0:512], k.w2[0:PT, 0:512]))
            k.dve.inc(k.V.tensor_copy(ztm[PTs, 512:1024],
                                      k.w3[0:PT, 0:512]))
            k.w(nc.sync, k.dve)
            nc.sync.dma_start(out=z_hbm[a * PT:(a + 1) * PT, :],
                              in_=ztm[PTs, 0:D]).then_inc(k.dma.sem, 16)
            k.dma.v += 16
            k.w(nc.tensor, k.dve)
            k.pe.inc(nc.tensor.transpose(k.w1[0:PT, 0:32],
                                         lgf[0:32, sl(0)],
                                         k.identf[0:32, 0:32]))
            k.w(nc.vector, k.pe)
            k.dve.inc(k.V.tensor_copy(stm[PTs, :], k.w1[0:PT, 0:32]))
            k.dve.inc(k.V.tensor_reduce(k.scr[PTs, 0:1], stm[PTs, :], AX.X,
                                        OP.max))
            k.dve.inc(k.V.tensor_scalar(stm[PTs, :], stm[PTs, :],
                                        k.scr[PTs, 0:1], None, OP.subtract))
            k.w(nc.scalar, k.dve)
            k.act.inc(k.A.activation(stm[PTs, :], stm[PTs, :], AF.Exp))
            k.w(nc.vector, k.act)
            k.dve.inc(k.V.tensor_reduce(k.scr[PTs, 0:1], stm[PTs, :], AX.X,
                                        OP.add))
            k.dve.inc(k.V.reciprocal(k.scr[PTs, 1:2], k.scr[PTs, 0:1]))
            k.dve.inc(k.V.tensor_scalar(stm[PTs, :], stm[PTs, :],
                                        k.scr[PTs, 1:2], None, OP.mult))
            k.dve.inc(k.V.tensor_reduce(
                tmx[PTs, 0:8],
                stm[PTs, :].rearrange("p (g j) -> p g j", g=8), AX.X,
                OP.max))
            k.dve.inc(k.V.max(v8[PTs, :], tmx[PTs, 0:8]))
            k.dve.inc(k.V.max_index(i8[PTs, :], v8[PTs, :], tmx[PTs, 0:8]))
            k.dve.inc(k.V.tensor_copy(i8f[PTs, :], i8[PTs, :]))
            k.dve.inc(k.V.memset(m32[:], 0.0))
            for kk in range(LGK):
                k.dve.inc(k.V.tensor_scalar(tmx[PTs, :], io32g[PTs, :],
                                            i8f[PTs, kk:kk + 1], None,
                                            OP.is_equal))
                k.dve.inc(k.V.tensor_tensor(m32[PTs, :], m32[PTs, :],
                                            tmx[PTs, :], OP.max))
            k.dve.inc(k.V.tensor_tensor(tmx[PTs, :], stm[PTs, :],
                                        m32[PTs, :], OP.mult))
            k.dve.inc(k.V.tensor_scalar(m32[PTs, :], m32[PTs, :],
                                        -FLOOR_BIG, FLOOR_BIG, OP.mult,
                                        OP.add))
            k.dve.inc(k.V.tensor_tensor(tmx[PTs, :], tmx[PTs, :],
                                        m32[PTs, :], OP.add))
            k.dve.inc(k.V.max(v8[PTs, :], tmx[PTs, :]))
            k.dve.inc(k.V.max_index(i8[PTs, :], v8[PTs, :], tmx[PTs, :]))
            k.dve.inc(k.V.tensor_copy(i8f[PTs, :], i8[PTs, :]))
            k.dve.inc(k.V.tensor_reduce(k.scr[PTs, 2:3], v8[PTs, 0:TK],
                                        AX.X, OP.add))
            k.dve.inc(k.V.tensor_scalar(k.scr[PTs, 2:3], k.scr[PTs, 2:3],
                                        1e-9, None, OP.add))
            k.dve.inc(k.V.reciprocal(k.scr[PTs, 3:4], k.scr[PTs, 2:3]))
            k.dve.inc(k.V.tensor_scalar(wna[PTs, a * TK:(a + 1) * TK],
                                        v8[PTs, 0:TK], k.scr[PTs, 3:4],
                                        None, OP.mult))
            k.dve.inc(k.V.tensor_copy(idxa[PTs, a * TK:(a + 1) * TK],
                                      i8f[PTs, 0:TK]))
            k.dve.inc(k.V.memset(wdt[:], 0.0))
            for kk in range(TK):
                k.dve.inc(k.V.tensor_scalar(tmx[PTs, :], io32[PTs, :],
                                            i8f[PTs, kk:kk + 1], None,
                                            OP.is_equal))
                k.dve.inc(k.V.scalar_tensor_tensor(
                    wdt[PTs, :], tmx[PTs, :],
                    wna[PTs, a * TK + kk:a * TK + kk + 1], wdt[PTs, :],
                    OP.mult, OP.add))
            k.w(nc.tensor, k.dve)
            k.pe.inc(nc.tensor.transpose(k.w4b[0:32, 0:128], wdt[:],
                                         k.identb[:]))
            k.w(nc.vector, k.pe)
            k.dve.inc(k.V.tensor_copy(k.sq[0:32, 1024:1024 + PT],
                                      k.w4b[0:32, 0:PT]))
            k.w(nc.sync, k.dve, k.dma)
            nc.sync.dma_start(out=wf_pay[:, a * PT:(a + 1) * PT],
                              in_=k.sq[0:32, 1024:1024 + PT]).then_inc(
                k.dma.sem, 16)
            k.dma.v += 16
            for s in range(NS):
                k.w(nc.tensor, k.dve)
                k.pe.inc(nc.tensor.transpose(
                    k.w4b[0:PT, s * 128:(s + 1) * 128], xff[:, sl(s * T)],
                    k.identb[:]))
            k.w(nc.vector, k.pe)
            k.dve.inc(k.V.tensor_copy(k.sq[0:PT, 0:1024],
                                      k.w4b[0:PT, 0:1024]))
            k.w(nc.sync, k.dve)
            nc.sync.dma_start(out=xf_pay[a * PT:(a + 1) * PT, :],
                              in_=k.sq[0:PT, 0:1024]).then_inc(k.dma.sem,
                                                               16)
            k.dma.v += 16
        k.wall(k.dma)

        p1.close()
        if lim <= 4:
            return

        k.cc.inc(nc.gpsimd.collective_compute(
            "AllGather", OP.bypass, replica_groups=[list(range(NC))],
            ins=[xf_pay], outs=[xf_all]))
        k.cc.inc(nc.gpsimd.collective_compute(
            "AllGather", OP.bypass, replica_groups=[list(range(NC))],
            ins=[wf_pay], outs=[wf_all]))
        k.wall(k.cc)
        if lim <= 5:
            return

        emit_moe_scatter(k, win, wf_all, idxa, wna, slot_hbm, pos_hbm)
        k.w(nc.gpsimd, k.dma, k.dmag)
        k.cc.inc(nc.gpsimd.collective_compute(
            "AllReduce", OP.add, replica_groups=[list(range(NC))],
            ins=[slot_hbm], outs=[slot_all]))
        k.wall(k.cc)
        if lim <= 6:
            return

        emit_moe_expert(k, win, xf_all, slot_all, eo_pay, eo_all,
                        wbh)
        if lim <= 7:
            return
        k.wall(k.cc)
        if lim <= 8:
            return

        emit_moe_origin(k, win, eo_all, xr_tm, z_hbm, y_out, idxa, wna,
                        pos_hbm)


def emit_moe_scatter(k, win, wf_all, idxa, wna, slot_hbm, pos_hbm):
    nc = k.nc
    T = k.T
    NC = k.NC
    N = NC * T
    E, TK, NG = 32, 4, 4
    CAP = max(1, int(1.25 * N * TK / E))
    CAPF = -(-CAP // 128) * 128
    NF16 = N // 16
    PT = T // 16
    PTs = slice(0, PT)
    SLOTC = (E + 4) * CAPF // 128

    sc = contextlib.ExitStack()
    with sc:
        wgrp = k.sb("s_wgrp", [128, NF16], BF16, sc)
        mgrp = k.sb("s_mgrp", [128, NF16], F32, sc)
        csum = k.sb("s_csum", [8, NF16], F32, sc)
        cs2 = k.sb("s_cs2", [8, NF16], F32, sc)
        ones8 = k.sb("s_ones8", [8, NF16], F32, sc)
        pos_all = k.sb("s_pos", [128, NG * NF16], F32, sc)
        pos_own = k.sb("s_posown", [128, NG * PT], F32, sc)
        myf0 = k.sb("s_myf0", [1, 1], U32, sc)
        posTM = k.sb("s_posTM", [128, 32], F32, sc)
        iot32 = k.sb("s_iot32", [128, 32], F32, sc)
        iotp = k.sb("s_iotp", [128, 1], F32, sc)
        rowb = k.sb("s_rowb", [128, 16], F32, sc)
        eq = k.sb("s_eq", [128, TK * 32], F32, sc)
        pek = k.sb("s_pek", [128, TK], F32, sc)
        keep = k.sb("s_keep", [128, TK], F32, sc)
        sidx = k.sb("s_sidx", [128, TK], F32, sc)
        tb = k.sb("s_tb", [128, TK], F32, sc)
        sidxu = k.sb("s_sidxu", [128, TK], U32, sc)
        zs = k.sb("s_zs", [128, SLOTC], F32, sc)

        nc.sync.dma_start(out=myf0[:], in_=win["moe_myf0"]).then_inc(
            k.dma.sem, 16)
        k.dma.v += 16
        nc.sync.dma_start(out=rowb[:], in_=win["moe_rowb"]).then_inc(
            k.dma.sem, 16)
        k.dma.v += 16
        nc.sync.wait_ge(k.dma.sem, k.dma.v)
        k.w(nc.gpsimd, k.dma, k.cc)
        k.pool.inc(k.G.iota(iot32[:], pattern=[[1, 32]], base=0,
                            channel_multiplier=0,
                            allow_small_or_imprecise_dtypes=True))
        k.pool.inc(k.G.iota(iotp[:], pattern=[[0, 1]], base=0,
                            channel_multiplier=1,
                            allow_small_or_imprecise_dtypes=True))
        k.dve.inc(k.V.memset(ones8[:], 1.0))
        k.dve.inc(k.V.memset(zs[:], 0.0))
        for kk in range(TK):
            k.w(nc.vector, k.pool)
            k.dve.inc(k.V.tensor_scalar(tb[:, kk:kk + 1], iotp[:],
                                        float(E * CAPF + kk * 128), None,
                                        OP.add))
        k.w(nc.sync, k.dve)
        nc.sync.dma_start(
            out=slot_hbm.rearrange("(p r) one -> p (r one)", p=128),
            in_=zs[:]).then_inc(k.dma.sem, 16)
        k.dma.v += 16

        for g in range(NG):
            k.w(nc.sync, k.dve, k.pe, k.dma, k.cc)
            nc.sync.dma_start(
                out=wgrp[:],
                in_=wf_all.rearrange("(c e) t -> e c t", e=E)
                [g * 8:(g + 1) * 8, :, :]
                .rearrange("e c (pp p) -> (e pp) c p", pp=16),
            ).then_inc(k.dma.sem, 16)
            k.dma.v += 16
            k.wall(k.dma)
            k.dve.inc(k.V.tensor_scalar(mgrp[:], wgrp[:], 0.0, None,
                                        OP.is_gt))
            k.w(nc.tensor, k.dve, k.act)
            for nn in range(0, NF16, 512):
                w = min(512, NF16 - nn)
                k.w(nc.tensor, k.dve)
                nc.tensor.matmul(k.w1[:, 0:w], k.su16f[:],
                                 mgrp[:, nn:nn + w], start=True, stop=True)
                mm = nc.tensor.matmul(k.w2[0:8, 0:w], k.esum8f[:],
                                      mgrp[:, nn:nn + w], start=True,
                                      stop=True)
                k.pe.inc(mm)
                k.w(nc.vector, k.pe)
                k.dve.inc(k.V.tensor_copy(csum[0:8, nn:nn + w],
                                          k.w2[0:8, 0:w]))
                k.dve.inc(k.V.tensor_copy(
                    pos_all[:, g * NF16 + nn:g * NF16 + nn + w],
                    k.w1[:, 0:w]))
            k.w(nc.vector, k.dve)
            k.dve.inc(k.V.tensor_tensor_scan(cs2[0:8, :], ones8[0:8, :],
                                             csum[0:8, :], 0.0, OP.mult,
                                             OP.add))
            k.w(nc.vector, k.dve)
            k.dve.inc(k.V.tensor_tensor(cs2[0:8, :], cs2[0:8, :],
                                        csum[0:8, :], OP.subtract))
            for nn in range(0, NF16, 512):
                w = min(512, NF16 - nn)
                k.w(nc.tensor, k.dve)
                k.pe.inc(nc.tensor.matmul(k.w1[:, 0:w], k.exp8f[:],
                                          cs2[0:8, nn:nn + w], start=True,
                                          stop=True))
                k.w(nc.vector, k.pe)
                k.dve.inc(k.V.tensor_tensor(
                    pos_all[:, g * NF16 + nn:g * NF16 + nn + w],
                    pos_all[:, g * NF16 + nn:g * NF16 + nn + w],
                    k.w1[:, 0:w], OP.add))
        k.w(nc.sync, k.dve)
        nc.sync.dma_start(out=pos_hbm, in_=pos_all[:]).then_inc(k.dma.sem,
                                                                16)
        k.dma.v += 16
        nc.sync.wait_ge(k.dma.sem, k.dma.v)
        k.w(nc.gpsimd, k.dma)
        for g in range(NG):
            f0g = nc.gpsimd.value_load(myf0[0:1, 0:1])
            k.dmag.inc(nc.gpsimd.dma_start(
                out=pos_own[:, g * PT:(g + 1) * PT],
                in_=pos_hbm[:, bass.ds(f0g + g * NF16, PT)]), 16)
        k.wall(k.dma, k.dmag)

        for a in range(16):
            k.w(nc.tensor, k.dve, k.act)
            for g in range(NG):
                mm = nc.tensor.matmul(
                    k.w1[0:32, 0:PT],
                    k.collf[:, (a * NG + g) * 32:(a * NG + g) * 32 + 32],
                    pos_own[:, g * PT:(g + 1) * PT],
                    start=(g == 0), stop=(g == NG - 1))
            k.pe.inc(mm)
            k.w(nc.vector, k.pe)
            k.dve.inc(k.V.tensor_copy(k.scr2[0:32, 0:PT], k.w1[0:32, 0:PT]))
            k.w(nc.tensor, k.dve)
            k.pe.inc(nc.tensor.transpose(k.w1[0:PT, 0:32],
                                         k.scr2[0:32, 0:PT],
                                         k.identf[0:32, 0:32]))
            k.w(nc.vector, k.pe)
            k.dve.inc(k.V.tensor_copy(posTM[PTs, :], k.w1[0:PT, 0:32]))
            eq4 = eq[PTs, :].rearrange("p (kk e) -> p kk e", kk=TK)
            io4 = iot32[PTs, :].unsqueeze(1).broadcast_to([PT, TK, 32])
            idx4c = idxa[PTs, a * TK:(a + 1) * TK]
            idx4 = idx4c.unsqueeze(2).broadcast_to([PT, TK, 32])
            ptm4 = posTM[PTs, :].unsqueeze(1).broadcast_to([PT, TK, 32])
            k.w(nc.vector, k.dmag)
            k.dve.inc(k.V.tensor_tensor(eq4, io4, idx4, OP.is_equal))
            k.w(nc.vector, k.dve)
            k.dve.inc(k.V.tensor_tensor(eq4, eq4, ptm4, OP.mult))
            k.w(nc.vector, k.dve)
            k.dve.inc(k.V.tensor_reduce(pek[PTs, 0:TK], eq4, AX.X, OP.add))
            k.w(nc.vector, k.dve)
            k.dve.inc(k.V.tensor_scalar(keep[PTs, :], pek[PTs, :],
                                        float(CAP), None, OP.is_lt))
            k.w(nc.vector, k.dve)
            k.dve.inc(k.V.tensor_scalar(sidx[PTs, :], idx4c, float(CAPF),
                                        None, OP.mult))
            k.w(nc.vector, k.dve)
            k.dve.inc(k.V.tensor_tensor(sidx[PTs, :], sidx[PTs, :],
                                        pek[PTs, :], OP.add))
            k.w(nc.vector, k.dve)
            k.dve.inc(k.V.tensor_tensor(sidx[PTs, :], sidx[PTs, :],
                                        tb[PTs, :], OP.subtract))
            k.w(nc.vector, k.dve)
            k.dve.inc(k.V.tensor_tensor(sidx[PTs, :], sidx[PTs, :],
                                        keep[PTs, :], OP.mult))
            k.w(nc.vector, k.dve)
            k.dve.inc(k.V.tensor_tensor(sidx[PTs, :], sidx[PTs, :],
                                        tb[PTs, :], OP.add))
            k.w(nc.vector, k.dve)
            k.dve.inc(k.V.tensor_copy(sidxu[PTs, :], sidx[PTs, :]))
            k.w(nc.gpsimd, k.dve, k.dma)
            for kk in range(TK):
                k.dmag.inc(nc.gpsimd.indirect_dma_start(
                    out=slot_hbm, out_offset=bass.IndirectOffsetOnAxis(
                        sidxu[PTs, kk:kk + 1], axis=0),
                    in_=rowb[PTs, a:a + 1], in_offset=None), 16)
        k.wall(k.dmag)


def emit_moe_expert(k, win, xf_all, slot_all, eo_pay, eo_all, wbh):
    nc = k.nc
    T = k.T
    NC = k.NC
    N = NC * T
    E, TK = 32, 4
    EPB = E // NC
    CAP = max(1, int(1.25 * N * TK / E))
    CAPF = -(-CAP // 128) * 128
    CHKW = min(512, CAPF)
    NCHK = CAPF // CHKW
    NSUB = CHKW // 128

    NRE = CAPF // 128
    assert EPB * NRE <= 128

    sc = contextlib.ExitStack()
    with sc:
        myeb = k.sb("e_myeb", [1, 1], U32, sc)
        myebf = k.sb("e_myebf", [1, 1], F32, sc)
        iotc = k.sb("e_iotc", [128, 1], F32, sc)
        bcf = k.sb("e_bcf", [128, 1], F32, sc)
        idxg = k.sb("e_idxg", [128, 1], U32, sc)
        svr = k.sb("e_svr", [128, 128], F32, sc)
        sv = k.sb("e_sv", [128, 128], F32, sc)
        NQ = EPB * NRE
        tidf20 = k.sb("e_tidf", [128, NQ], F32, sc)
        tid20 = k.sb("e_tidu", [128, NQ], U32, sc)
        xg2 = k.sb("e_xg", [128, 4 * D], BF16, sc)
        we13 = k.sb("e_w13", [128, 2 * 2 * 64 * 128], BF16, sc)
        we2 = k.sb("e_w2", [128, 2 * 64 * 128], BF16, sc)
        xeT = k.sb("e_xeT", [128, NS * CHKW], BF16, sc)
        he = k.sb("e_he", [128, NS * CHKW], BF16, sc)
        osl = k.sb("e_osl", [128, 2 * (CHKW // 128) * D], BF16, sc)

        nc.sync.dma_start(out=myeb[:], in_=win["moe_myeb"]).then_inc(
            k.dma.sem, 16)
        k.dma.v += 16
        slotv = slot_all.rearrange("(r p) one -> r (p one)", p=128)
        xeT3 = xeT[:].rearrange("p (s q) -> p s q", s=NS)
        w4b3 = k.w4b[:].rearrange("p (s q) -> p s q", s=NS)

        k.w(nc.vector, k.dma, k.cc)
        k.dve.inc(k.V.tensor_copy(myebf[:], myeb[:]))
        k.pool.inc(k.G.iota(iotc[:], pattern=[[0, 1]], base=0,
                            channel_multiplier=1,
                            allow_small_or_imprecise_dtypes=True))
        k.w(nc.gpsimd, k.pool)
        k.pool.inc(k.G.affine_select(
            out=iotc[:], in_=iotc[:], compare_op=OP.is_ge,
            fill=float(EPB * NRE - 1), base=EPB * NRE - 1,
            pattern=[[0, 1]], channel_multiplier=-1))
        k.w(nc.tensor, k.dve, k.act)
        k.pe.inc(nc.tensor.matmul(k.w2[:, 0:1], k.ones128f[0:1, 0:128],
                                  myebf[0:1, 0:1], start=True, stop=True))
        k.w(nc.vector, k.pe, k.pool)
        k.dve.inc(k.V.tensor_scalar(bcf[:], k.w2[:, 0:1], float(NRE), None,
                                    OP.mult))
        k.w(nc.vector, k.dve)
        k.dve.inc(k.V.tensor_tensor(bcf[:], bcf[:], iotc[:], OP.add))
        k.w(nc.vector, k.dve)
        k.dve.inc(k.V.tensor_copy(idxg[:], bcf[:]))
        k.w(nc.gpsimd, k.dve, k.cc)
        k.dmag.inc(nc.gpsimd.indirect_dma_start(
            out=svr[:], out_offset=None, in_=slotv,
            in_offset=bass.IndirectOffsetOnAxis(idxg[:, 0:1], axis=0)), 16)
        nc.tensor.wait_ge(k.dmag.sem, k.dmag.v)
        k.w(nc.tensor, k.dve)
        k.pe.inc(nc.tensor.transpose(k.w1[:, 0:128], svr[:], k.identf[:]))
        k.w(nc.vector, k.pe)
        k.dve.inc(k.V.tensor_copy(sv[:], k.w1[:, 0:128]))

        k.w(nc.vector, k.dve)
        k.dve.inc(k.V.tensor_scalar(tidf20[:, 0:NQ], sv[:, 0:NQ], -1.0,
                                    None, OP.add))
        k.w(nc.vector, k.dve)
        k.dve.inc(k.V.tensor_scalar_max(tidf20[:, 0:NQ], tidf20[:, 0:NQ],
                                        0.0))
        k.w(nc.vector, k.dve)
        k.dve.inc(k.V.tensor_copy(tid20[:, 0:NQ], tidf20[:, 0:NQ]))

        gv = {}

        def issue_gather(q):
            if q >= NQ:
                return
            ctr = k.gat[q % 4]
            k.w(nc.gpsimd, k.dve, k.pe, k.cc)
            ctr.inc(nc.gpsimd.indirect_dma_start(
                out=xg2[:, (q % 4) * D:(q % 4) * D + D], out_offset=None,
                in_=xf_all,
                in_offset=bass.IndirectOffsetOnAxis(tid20[:, q:q + 1],
                                                    axis=0)), 16)
            gv[q] = (ctr, ctr.v)

        for _q0 in range(4):
            issue_gather(_q0)

        k.hb_cfg = True
        wb13_hbm, wb2_hbm = wbh
        nc.sync.wait_ge(k.wcast.sem, k.wcast.v)
        k.w(nc.sync, k.pe, k.dve)
        nc.sync.dma_start(out=we13[:, 0:16384],
                          in_=wb13_hbm[:, 0:16384]).then_inc(k.we_e.sem, 16)
        nc.sync.dma_start(out=we2[:, 0:8192],
                          in_=wb2_hbm[:, 0:8192]).then_inc(k.we_e.sem, 16)
        k.we_e.v += 32
        wl = {0: (k.we_e, k.we_e.v)}
        ov = {}

        for j in range(EPB):
            jb = j % 2
            wb13 = we13[:, jb * 16384:jb * 16384 + 16384]
            wb2 = we2[:, jb * 8192:jb * 8192 + 8192]
            for ch in range(NCHK):
                for sb_ in range(NSUB):
                    q = j * NRE + ch * NSUB + sb_
                    nc.tensor.wait_ge(gv[q][0].sem, gv[q][1])
                    k.w(nc.tensor, k.dve)
                    for s in range(NS):
                        mm = nc.tensor.transpose(
                            k.w4b[:, s * 128:(s + 1) * 128],
                            xg2[:, (q % 4) * D + s * 128:
                                (q % 4) * D + (s + 1) * 128], k.identb[:])
                    k.pe.inc(mm)
                    k.w(nc.vector, k.pe)
                    k.dve.inc(k.V.tensor_copy(
                        xeT3[:, :, sb_ * 128:(sb_ + 1) * 128], w4b3[:]))
                    issue_gather(q + 4)
                nc.tensor.wait_ge(k.dve.sem, k.dve.v)
                if ch == 0:
                    nc.tensor.wait_ge(wl[j][0].sem, wl[j][1])

                def e_consume(p):
                    mo_, gb_, ub_, scr_ = p
                    k.w(nc.scalar, k.pe)
                    k.act.inc(k.A.activation(scr_, gb_, AF.Silu))
                    k.w(nc.vector, k.pe, k.act)
                    k.dve.inc(k.V.tensor_tensor(
                        he[:, mo_ * CHKW:(mo_ + 1) * CHKW], ub_, scr_,
                        OP.mult))

                pend = None
                for mo in range(NS):
                    gb = (k.w1 if mo % 2 == 0 else k.w3)[:, 0:CHKW]
                    ub = (k.w2 if mo % 2 == 0 else k.S0)[:, 0:CHKW]
                    scrh = (k.scr if mo % 2 == 0 else k.scr2)[:, 0:CHKW]
                    k.w(nc.tensor, k.dve, k.act)
                    for ki in range(NS):
                        nc.tensor.matmul(
                            gb,
                            wb13[:, (ki * NS + mo) * 128:
                                 (ki * NS + mo) * 128 + 128],
                            xeT[:, ki * CHKW:(ki + 1) * CHKW],
                            start=(ki == 0), stop=(ki == NS - 1))
                        mm = nc.tensor.matmul(
                            ub,
                            wb13[:, 8192 + (ki * NS + mo) * 128:
                                 8192 + (ki * NS + mo) * 128 + 128],
                            xeT[:, ki * CHKW:(ki + 1) * CHKW],
                            start=(ki == 0), stop=(ki == NS - 1))
                    k.pe.inc(mm)
                    if pend is not None:
                        e_consume(pend)
                    pend = (mo, gb, ub, scrh)
                e_consume(pend)

                ci = j * NCHK + ch
                ob = (ci % 2) * (CHKW // 128) * D

                def o_consume(p):
                    st_, b0_, b1_ = p
                    if ci - 2 in ov:
                        nc.vector.wait_ge(ov[ci - 2][0].sem, ov[ci - 2][1])
                        nc.scalar.wait_ge(ov[ci - 2][0].sem, ov[ci - 2][1])
                    k.w(nc.vector, k.pe)
                    k.dve.inc(k.V.tensor_copy(
                        osl[:, ob + st_ * D:ob + st_ * D + 512],
                        b0_[:, 0:512]))
                    k.w(nc.scalar, k.pe)
                    k.act.inc(k.A.copy(
                        osl[:, ob + st_ * D + 512:ob + (st_ + 1) * D],
                        b1_[:, 0:512]))

                pend = None
                for st in range(CHKW // 128):
                    b0 = (k.w2 if st % 2 == 0 else k.S0[:, 0:512])
                    b1 = (k.w3 if st % 2 == 0 else k.S0[:, 512:1024])
                    k.w(nc.tensor, k.dve, k.act)
                    for half in range(2):
                        for ki in range(NS):
                            mm = nc.tensor.matmul(
                                (b0 if half == 0 else b1)[:, 0:512],
                                he[:, ki * CHKW + st * 128:
                                   ki * CHKW + st * 128 + 128],
                                wb2[:, ki * 1024 + half * 512:
                                    ki * 1024 + half * 512 + 512],
                                start=(ki == 0), stop=(ki == NS - 1))
                    k.pe.inc(mm)
                    if pend is not None:
                        o_consume(pend)
                    pend = (st, b0, b1)
                o_consume(pend)
                ectr = k.eo_e if ci % 2 == 0 else k.eo_o
                k.w(nc.sync, k.dve, k.act)
                nc.sync.dma_start(
                    out=eo_pay[j * CAPF + ch * CHKW:
                               j * CAPF + (ch + 1) * CHKW, :]
                    .rearrange("(st p) d -> p st d", p=128),
                    in_=osl[:, ob:ob + (CHKW // 128) * D]
                    .rearrange("p (st d) -> p st d", st=CHKW // 128),
                ).then_inc(ectr.sem, 16)
                ectr.v += 16
                ov[ci] = (ectr, ectr.v)
                if ch == 0 and j + 1 < EPB:
                    nj = (j + 1) % 2
                    wctr = k.we_e if (j + 1) % 2 == 0 else k.we_o
                    k.w(nc.sync, k.pe)
                    nc.sync.dma_start(
                        out=we13[:, nj * 16384:nj * 16384 + 16384],
                        in_=wb13_hbm[:, (j + 1) * 16384:
                                     (j + 2) * 16384]).then_inc(
                        wctr.sem, 16)
                    nc.sync.dma_start(
                        out=we2[:, nj * 8192:nj * 8192 + 8192],
                        in_=wb2_hbm[:, (j + 1) * 8192:
                                    (j + 2) * 8192]).then_inc(wctr.sem, 16)
                    wctr.v += 32
                    wl[j + 1] = (wctr, wctr.v)
            nc.gpsimd.wait_ge(k.eo_e.sem, k.eo_e.v)
            nc.gpsimd.wait_ge(k.eo_o.sem, k.eo_o.v)
            k.cc.inc(nc.gpsimd.collective_compute(
                "AllGather", OP.bypass, replica_groups=[list(range(NC))],
                ins=[eo_pay[j * CAPF:(j + 1) * CAPF, :]],
                outs=[eo_all[j * NC * CAPF:(j + 1) * NC * CAPF, :]]))
        k.hb_cfg = None
        k.wall(k.dma, k.eo_e, k.eo_o, k.we_e, k.we_o)


def emit_moe_origin(k, win, eo_all, xr_tm, z_hbm, y_out, idxa, wna,
                    pos_hbm):
    nc = k.nc
    T = k.T
    NC = k.NC
    N = NC * T
    E, TK = 32, 4
    EPB = E // NC
    CAP = max(1, int(1.25 * N * TK / E))
    CAPF = -(-CAP // 128) * 128
    NF16 = N // 16
    PT = T // 16
    NG = 4
    PTs = slice(0, PT)

    sc = contextlib.ExitStack()
    with sc:
        pos_own = k.sb("o_posown", [128, NG * PT], F32, sc)
        myf0 = k.sb("o_myf0", [1, 1], U32, sc)
        posTM = k.sb("o_posTM", [128, 32], F32, sc)
        iot32 = k.sb("o_iot32", [128, 32], F32, sc)
        eq = k.sb("o_eq", [128, TK * 32], F32, sc)
        pek = k.sb("o_pek", [128, TK], F32, sc)
        keep = k.sb("o_keep", [128, TK], F32, sc)
        wt4 = k.sb("o_wt4", [128, TK], F32, sc)
        gbi = k.sb("o_gbi", [128, TK], F32, sc)
        ii32 = k.sb("o_ii32", [128, TK], I32, sc)
        iqf = k.sb("o_iqf", [128, TK], F32, sc)
        gbu = k.sb("o_gbu", [128, TK], U32, sc)
        gbrow = k.sb("o_gbrow", [128, TK * D], BF16, sc)
        acc = k.sb("o_acc", [128, D], F32, sc)
        ztm = k.sb("o_ztm", [128, D], F32, sc)
        x3t = k.sb("o_x3t", [128, D], F32, sc)
        sq2 = k.sb("o_sq2", [128, D], F32, sc)

        k.hb_cfg = True
        nc.sync.dma_start(out=myf0[:], in_=win["moe_myf0"]).then_inc(
            k.dma.sem, 16)
        k.dma.v += 16
        nc.sync.wait_ge(k.dma.sem, k.dma.v)
        k.w(nc.gpsimd, k.dma)
        k.pool.inc(k.G.iota(iot32[:], pattern=[[1, 32]], base=0,
                            channel_multiplier=0,
                            allow_small_or_imprecise_dtypes=True))
        for g in range(NG):
            f0g = nc.gpsimd.value_load(myf0[0:1, 0:1])
            k.dmag.inc(nc.gpsimd.dma_start(
                out=pos_own[:, g * PT:(g + 1) * PT],
                in_=pos_hbm[:, bass.ds(f0g + g * NF16, PT)]), 16)
        k.wall(k.dma, k.dmag)

        for a in range(16):
            k.w(nc.tensor, k.dve, k.act)
            for g in range(NG):
                mm = nc.tensor.matmul(
                    k.w1[0:32, 0:PT],
                    k.collf[:, (a * NG + g) * 32:(a * NG + g) * 32 + 32],
                    pos_own[:, g * PT:(g + 1) * PT],
                    start=(g == 0), stop=(g == NG - 1))
            k.pe.inc(mm)
            k.w(nc.vector, k.pe)
            k.dve.inc(k.V.tensor_copy(k.scr2[0:32, 0:PT], k.w1[0:32, 0:PT]))
            k.w(nc.tensor, k.dve)
            k.pe.inc(nc.tensor.transpose(k.w1[0:PT, 0:32],
                                         k.scr2[0:32, 0:PT],
                                         k.identf[0:32, 0:32]))
            k.w(nc.vector, k.pe)
            k.dve.inc(k.V.tensor_copy(posTM[PTs, :], k.w1[0:PT, 0:32]))
            k.w(nc.sync, k.dve, k.dma)
            nc.sync.dma_start(out=ztm[PTs, 0:D],
                              in_=z_hbm[a * PT:(a + 1) * PT, :]).then_inc(
                k.dma.sem, 16)
            k.dma.v += 16
            nc.sync.dma_start(
                out=x3t[PTs, 0:D],
                in_=xr_tm.rearrange("(f q) d -> f q d", q=16)[:, a, :]
            ).then_inc(k.dma.sem, 16)
            k.dma.v += 16
            eq4 = eq[PTs, :].rearrange("p (kk e) -> p kk e", kk=TK)
            io4 = iot32[PTs, :].unsqueeze(1).broadcast_to([PT, TK, 32])
            idx4c = idxa[PTs, a * TK:(a + 1) * TK]
            idx4 = idx4c.unsqueeze(2).broadcast_to([PT, TK, 32])
            ptm4 = posTM[PTs, :].unsqueeze(1).broadcast_to([PT, TK, 32])
            k.dve.inc(k.V.tensor_tensor(eq4, io4, idx4, OP.is_equal))
            k.w(nc.vector, k.dve)
            k.dve.inc(k.V.tensor_tensor(eq4, eq4, ptm4, OP.mult))
            k.w(nc.vector, k.dve)
            k.dve.inc(k.V.tensor_reduce(pek[PTs, 0:TK], eq4, AX.X, OP.add))
            k.w(nc.vector, k.dve)
            k.dve.inc(k.V.tensor_scalar(keep[PTs, :], pek[PTs, :],
                                        float(CAP), None, OP.is_lt))
            k.w(nc.vector, k.dve)
            k.dve.inc(k.V.tensor_tensor(
                wt4[PTs, 0:TK], wna[PTs, a * TK:(a + 1) * TK],
                keep[PTs, :], OP.mult))
            EPBL = EPB.bit_length() - 1
            k.dve.inc(k.V.tensor_copy(ii32[PTs, :], idx4c))
            k.w(nc.vector, k.dve)
            k.dve.inc(k.V.tensor_scalar(ii32[PTs, :], ii32[PTs, :], EPBL,
                                        None, OP.arith_shift_right))
            k.w(nc.vector, k.dve)
            k.dve.inc(k.V.tensor_copy(iqf[PTs, :], ii32[PTs, :]))
            k.w(nc.vector, k.dve)
            k.dve.inc(k.V.scalar_tensor_tensor(
                gbi[PTs, :], iqf[PTs, :], float(-EPB), idx4c,
                OP.mult, OP.add))
            k.w(nc.vector, k.dve)
            k.dve.inc(k.V.tensor_scalar(gbi[PTs, :], gbi[PTs, :],
                                        float(NC * CAPF), None, OP.mult))
            k.w(nc.vector, k.dve)
            k.dve.inc(k.V.scalar_tensor_tensor(
                gbi[PTs, :], iqf[PTs, :], float(CAPF), gbi[PTs, :],
                OP.mult, OP.add))
            k.w(nc.vector, k.dve)
            k.dve.inc(k.V.tensor_tensor(gbi[PTs, :], gbi[PTs, :],
                                        pek[PTs, :], OP.add))
            k.w(nc.vector, k.dve)
            k.dve.inc(k.V.tensor_tensor(gbi[PTs, :], gbi[PTs, :],
                                        keep[PTs, :], OP.mult))
            k.w(nc.vector, k.dve)
            k.dve.inc(k.V.tensor_copy(gbu[PTs, 0:TK], gbi[PTs, :]))
            k.w(nc.vector, k.dve)
            k.w(nc.gpsimd, k.dve, k.cc)
            for kk in range(TK):
                k.dmag.inc(nc.gpsimd.indirect_dma_start(
                    out=gbrow[PTs, kk * D:(kk + 1) * D], out_offset=None,
                    in_=eo_all,
                    in_offset=bass.IndirectOffsetOnAxis(
                        gbu[PTs, kk:kk + 1], axis=0)), 16)
            nc.vector.wait_ge(k.dmag.sem, k.dmag.v)
            k.w(nc.vector, k.dma)
            k.dve.inc(k.V.tensor_scalar(acc[PTs, :],
                                        gbrow[PTs, 0:D],
                                        wt4[PTs, 0:1], None, OP.mult))
            for kk in range(1, TK):
                k.w(nc.vector, k.dve)
                k.dve.inc(k.V.scalar_tensor_tensor(
                    acc[PTs, :], gbrow[PTs, kk * D:(kk + 1) * D],
                    wt4[PTs, kk:kk + 1], acc[PTs, :], OP.mult, OP.add))
            k.w(nc.vector, k.dve)
            k.dve.inc(k.V.tensor_tensor(acc[PTs, :], acc[PTs, :],
                                        ztm[PTs, 0:D], OP.add))
            k.w(nc.vector, k.dve)
            k.dve.inc(k.V.tensor_reduce(k.scr[PTs, 0:1], acc[PTs, :], AX.X,
                                        OP.add))
            k.w(nc.scalar, k.dve)
            k.act.inc(k.A.activation(sq2[PTs, 0:D], acc[PTs, :],
                                     AF.Square))
            k.w(nc.vector, k.act)
            k.dve.inc(k.V.tensor_reduce(k.scr[PTs, 1:2], sq2[PTs, 0:D],
                                        AX.X, OP.add))
            k.w(nc.vector, k.dve)
            k.dve.inc(k.V.tensor_scalar(k.scr[PTs, 0:1], k.scr[PTs, 0:1],
                                        1.0 / D, None, OP.mult))
            k.dve.inc(k.V.tensor_scalar(k.scr[PTs, 1:2], k.scr[PTs, 1:2],
                                        1.0 / D, None, OP.mult))
            k.w(nc.vector, k.dve)
            k.dve.inc(k.V.tensor_tensor(k.scr[PTs, 2:3], k.scr[PTs, 0:1],
                                        k.scr[PTs, 0:1], OP.mult))
            k.w(nc.vector, k.dve)
            k.dve.inc(k.V.tensor_tensor(k.scr[PTs, 1:2], k.scr[PTs, 1:2],
                                        k.scr[PTs, 2:3], OP.subtract))
            k.rsqrt(k.scr[PTs, 1:2], 1e-5)
            k.w(nc.vector, k.act)
            k.dve.inc(k.V.tensor_scalar(acc[PTs, :], acc[PTs, :],
                                        k.scr[PTs, 0:1], None, OP.subtract))
            k.w(nc.vector, k.dve)
            k.dve.inc(k.V.tensor_scalar(acc[PTs, :], acc[PTs, :],
                                        k.scr[PTs, 1:2], None, OP.mult))
            k.w(nc.vector, k.dve)
            k.dve.inc(k.V.tensor_tensor(acc[PTs, :], acc[PTs, :],
                                        x3t[PTs, 0:D], OP.add))
            k.w(nc.sync, k.dve)
            nc.sync.dma_start(
                out=y_out.rearrange("(f q) d -> f q d", q=16)[:, a, :],
                in_=acc[PTs, 0:D]).then_inc(k.dma.sem, 16)
            k.dma.v += 16
        k.hb_cfg = None
        k.wall(k.dma)


def emit_softmax_groups(k, lg, ptm, probs, nrows, ngrp, grp, floor,
                        floor_mode, lgh=None):
    nc = k.nc
    ntile = SEG // 128
    k.w(nc.tensor, k.dve, k.act)
    for tt in range(ntile):
        if lgh is None:
            mm = nc.tensor.transpose(k.w1[:, tt * nrows:(tt + 1) * nrows],
                                     lg[0:nrows, tt * 128:(tt + 1) * 128],
                                     k.identf[0:nrows, 0:nrows])
        else:
            for h in range(ngrp):
                mm = nc.tensor.transpose(
                    k.w1[:, tt * nrows + h * grp:tt * nrows + (h + 1) * grp],
                    lgh[32 * h:32 * h + grp, tt * 128:(tt + 1) * 128],
                    k.identf[32 * h:32 * h + grp, 32 * h:32 * h + grp],
                    tile_position=(32 * h, 0))
    k.pe.inc(mm)
    k.w(nc.vector, k.pe)
    k.dve.inc(k.V.tensor_copy(ptm[:, 0:ntile * nrows],
                                    k.w1[:, 0:ntile * nrows]))
    k.w(nc.vector, k.dve)
    for tt in range(ntile):
        tp = ptm[:, tt * nrows:(tt + 1) * nrows]
        tp3 = tp.rearrange("p (g j) -> p g j", g=ngrp)
        mx = k.scr[:, 0:ngrp]
        k.dve.inc(k.V.tensor_reduce(mx, tp3, AX.X, OP.max))
        k.w(nc.vector, k.dve)
        for g in range(ngrp):
            k.dve.inc(k.V.tensor_scalar(
                tp[:, g * grp:(g + 1) * grp], tp[:, g * grp:(g + 1) * grp],
                mx[:, g:g + 1], None, OP.subtract))
        k.w(nc.scalar, k.dve)
        k.act.inc(k.A.activation(tp, tp, AF.Exp))
        k.w(nc.vector, k.act)
        sm = k.scr[:, 8:8 + ngrp]
        k.dve.inc(k.V.tensor_reduce(sm, tp3, AX.X, OP.add))
        k.w(nc.vector, k.dve)
        k.dve.inc(k.V.reciprocal(sm, sm))
        k.w(nc.vector, k.dve)
        for g in range(ngrp):
            k.dve.inc(k.V.tensor_scalar(
                tp[:, g * grp:(g + 1) * grp], tp[:, g * grp:(g + 1) * grp],
                sm[:, g:g + 1], None, OP.mult))
        k.w(nc.vector, k.dve)
        if floor_mode == "last":
            k.dve.inc(k.V.tensor_scalar(tp, tp, 1.0 - floor, None,
                                              OP.mult))
            k.w(nc.vector, k.dve)
            lastcols = tp.rearrange("p (g j) -> p g j", g=ngrp)[:, :, grp - 1]
            k.dve.inc(k.V.tensor_scalar(lastcols, lastcols, floor, None,
                                              OP.add))
        else:
            k.dve.inc(k.V.tensor_scalar(tp, tp, 1.0 - grp * floor,
                                              floor, OP.mult, OP.add))
        k.w(nc.vector, k.dve)
    k.w(nc.tensor, k.dve)
    first = True
    for tt in range(ntile):
        mm = nc.tensor.transpose(k.w2[0:nrows, tt * 128:(tt + 1) * 128],
                                 ptm[:, tt * nrows:(tt + 1) * nrows],
                                 k.identf[:, 0:128])
        first = False
    k.pe.inc(mm)
    k.w(nc.vector, k.pe)
    k.dve.inc(k.V.tensor_copy(probs[0:nrows, 0:SEG],
                                    k.w2[0:nrows, 0:SEG]))
    k.w(nc.vector, k.dve)
    k.w(nc.tensor, k.dve)



E_MOE = 32


def build(T, n_cores, dbg=(), upto=None, asub=99):
    lim = {"ln0": 0, "l0a": 1, "f0": 2, "l1a": 3, "moe1": 4, "ag1": 5,
           "scat": 6, "exp": 7, "ag2": 8, None: 99}[upto]
    nc = bass.Bass("TRN2", target_bir_lowering=False)
    stack = contextlib.ExitStack()
    k = K(nc, stack, T, n_cores)
    k.asub = asub

    x_in = nc.dram_tensor("x", [D, T], F32, kind="ExternalInput").ap()
    EPB_ = E_MOE // n_cores
    wb13_hbm = nc.dram_tensor("wb13_hbm", [128, EPB_ * 16384], BF16).ap()
    wb2_hbm = nc.dram_tensor("wb2_hbm", [128, EPB_ * 8192], BF16).ap()
    consts_in = nc.dram_tensor("consts", [128, 512], F32,
                               kind="ExternalInput").ap()
    rs24_in = nc.dram_tensor("rowsel24", [128, 24 * 24], F32,
                             kind="ExternalInput").ap()
    rs16_in = nc.dram_tensor("rowsel16", [128, 16 * 16], F32,
                             kind="ExternalInput").ap()
    rr24_in = nc.dram_tensor("rowrep24", [24, 24 * 128], F32,
                             kind="ExternalInput").ap()
    shv_in = nc.dram_tensor("shv16", [128, 16], F32,
                            kind="ExternalInput").ap()
    y_out = nc.dram_tensor("y", [T, D], F32, kind="ExternalOutput").ap()
    xr_hbm = nc.dram_tensor("xr_hbm", [D, T], F32).ap()

    dbg_outs = set(dbg)

    win = {}

    def wtin(nm, shape, dtype=F32):
        win[nm] = nc.dram_tensor("w_" + nm, shape, dtype,
                                 kind="ExternalInput").ap()

    wtin("l0a_wq", [128, 64 * 128])
    wtin("l0a_wk", [128, 64 * 128])
    wtin("l0a_wv", [128, 64 * 128])
    wtin("l0a_wb", [128, 8 * 4])
    wtin("l0a_cq", [128, 32])
    wtin("l0a_ck", [128, 32])
    wtin("l0a_cv", [128, 32])
    wtin("l0a_fir0", [128, NS * 3])
    wtin("l0a_fir1", [128, NS * 7])
    wtin("l0a_fir2", [128, NS * 15])
    wtin("l0a_fir3", [128, NS * 31])
    wtin("l0a_gw1x", [128, 128 * 128])
    wtin("l0a_gw1s", [24, 16 * 128])
    wtin("l0a_gw2", [128, 16 * 24])
    wtin("l0a_gb1", [128, 16])
    wtin("l0a_gb2", [24, 1])
    wtin("l0a_alpha", [24, 1])
    wtin("l0a_gtemp", [24, 1])
    wtin("l0a_onw", [128, 2])
    wtin("l0a_wo", [128, 64 * 128])
    wtin("f0_wgu", [128, 24 * 2048], I32)
    wtin("f0_wd", [128, 8 * 3072], I32)
    wtin("l1a_wq", [128, 64 * 128])
    wtin("l1a_wk", [128, 64 * 128])
    wtin("l1a_wv", [128, 64 * 128])
    wtin("l1a_wb", [128, 8 * 4])
    wtin("l1a_cq", [128, 32])
    wtin("l1a_ck", [128, 32])
    wtin("l1a_cv", [128, 32])
    wtin("l1a_fir0", [128, NS * 5])
    wtin("l1a_fir1", [128, NS * 64])
    wtin("l1a_gw1x", [128, 64 * 128])
    wtin("l1a_gw1s", [128, 8 * 128])
    wtin("l1a_gw2", [128, 8 * 4])
    wtin("l1a_sel16", [128, 16])
    wtin("l1a_gb1", [128, 8])
    wtin("l1a_gb2", [128, 1])
    wtin("l1a_gcb", [128, 1])
    wtin("l1a_gtemp", [128, 1])
    wtin("l1a_onw", [128, 2])
    wtin("l1a_wo", [128, 64 * 128])
    wtin("moe_wg", [128, 8 * 32])
    wtin("moe_sw13", [128, 2 * 64 * 128])
    wtin("moe_sw2", [128, 64 * 128])
    wtin("moe_w13", [128, (E_MOE // n_cores) * 2 * 64 * 128])
    wtin("moe_w2", [128, (E_MOE // n_cores) * 64 * 128])
    wtin("moe_myeb", [1, 1], U32)
    wtin("moe_myf0", [1, 1], U32)
    wtin("moe_rowb", [128, 16])
    wtin("moe_cst", [128, 128 + 8 + 16 * 4 * 32])
    wtin("moe_exp8", [8, 128])

    with stack:
        k.ones128 = k.sb("ones128", [128, 128], BF16)
        k.ones128f = k.sb("ones128f", [128, 128], F32)
        k._consts_tensor = k.sb("constcols", [128, 16], F32)
        k.sq = k.sb("gsq", [128, NS * SEG], BF16)
        k.scr = k.sb("gscr", [128, 512], F32)
        k.scr2 = k.sb("gscr2", [128, 512], F32)
        k.scr3 = k.sb("gscr3", [128, 512], F32)
        k.wide = k.sb("gwide", [128, 2048], F32)
        k.wide2 = k.sb("gwide2", [128, 2048], F32)
        k.w1 = stack.enter_context(nc.psum_tensor("pw1", [128, 512], F32))
        k.w2 = stack.enter_context(nc.psum_tensor("pw2", [128, 512], F32))
        k.w3 = stack.enter_context(nc.psum_tensor("pw3", [128, 512], F32))
        k.w4b = stack.enter_context(nc.psum_tensor("pw4b", [128, 1024], BF16))
        k.S0 = stack.enter_context(nc.psum_tensor("S0", [128, 1024], F32))
        k.S1 = stack.enter_context(nc.psum_tensor("S1", [128, 1024], F32))

        k.dve.inc(k.V.memset(k.ones128[:], 1.0))
        k.dve.inc(k.V.memset(k.ones128f[:], 1.0))
        for _c in (1e-5, 1e-6, 1e-12, 1e-9, 1e-4, 1e-30, 1e-37, 1.0):
            k.cst(_c)

        cst4 = k.sb("cst4", [128, 512], F32)
        k.identb = k.sb("identb", [128, 128], BF16)
        k.identf = k.sb("identf", [128, 128], F32)
        k.negslb = k.sb("negslb", [128, 128], BF16)
        k.triub = k.sb("triub", [128, 128], BF16)
        k.rowsel = k.sb("rowsel_sb", [128, 24 * 24], BF16)
        k.rowsel16 = k.sb("rowsel16_sb", [128, 16 * 16], BF16)
        k.rowrepb = k.sb("rowrep_sb", [24, 24 * 128], BF16)
        k.shv16 = k.sb("shv16_sb", [128, 16], F32)
        nc.sync.dma_start(out=cst4[:], in_=consts_in).then_inc(k.dma.sem, 16)
        nc.sync.dma_start(out=k.shv16[:], in_=shv_in).then_inc(k.dma.sem, 16)
        nc.sync.dma_start(out=xr_hbm, in_=x_in).then_inc(k.dma.sem, 16)
        k.dma.v += 48
        k.hload(k.rowsel[:], rs24_in)
        k.hload(k.rowsel16[:], rs16_in)
        k.hload(k.rowrepb[:], rr24_in)
        k.wall(k.dmag)
        k.wall(k.dma)
        k.dve.inc(k.V.tensor_copy(k.identb[:], cst4[:, 0:128]))
        k.dve.inc(k.V.tensor_copy(k.identf[:], cst4[:, 0:128]))
        k.dve.inc(k.V.tensor_copy(k.negslb[:], cst4[:, 128:256]))
        k.dve.inc(k.V.tensor_copy(k.triub[:], cst4[:, 256:384]))
        k.wall(k.dve)

        if lim > 3:
            k.wcast.inc(nc.gpsimd.dma_start(out=wb13_hbm,
                                            in_=win["moe_w13"]), 16)
            k.wcast.inc(nc.gpsimd.dma_start(out=wb2_hbm,
                                            in_=win["moe_w2"]), 16)

        wsc = contextlib.ExitStack()
        with wsc:
            wts = {}
            for nm, wd in (("wq", 64 * 128), ("wk", 64 * 128),
                           ("wv", 64 * 128), ("wo", 64 * 128)):
                wts[nm] = k.sb("w_" + nm, [128, wd], BF16, wsc)
                k.hload(wts[nm][:], win["l0a_" + nm])
            wts["wb"] = k.sb("w_wb", [128, 32], BF16, wsc)
            k.hload(wts["wb"][:], win["l0a_wb"])
            wts["gw1s"] = k.sb("w_gw1s", [24, 16 * 128], BF16, wsc)
            k.hload(wts["gw1s"][:], win["l0a_gw1s"])
            wts["gw2"] = k.sb("w_gw2", [128, 16 * 24], BF16, wsc)
            k.hload(wts["gw2"][:], win["l0a_gw2"])
            for nm, wd in (("cq", 32), ("ck", 32), ("cv", 32),
                           ("fir0", NS * 3), ("fir1", NS * 7),
                           ("fir2", NS * 15), ("fir3", NS * 31),
                           ("gb1", 16), ("onw", 2)):
                wts[nm] = k.sb("w_" + nm, [128, wd], F32, wsc)
                k.hload(wts[nm][:], win["l0a_" + nm])
            for nm, rows in (("gb2", 24), ("alpha", 24), ("gtemp", 24)):
                wts[nm] = k.sb("w_" + nm, [rows, 1], F32, wsc)
                k.hload(wts[nm][:], win["l0a_" + nm])
            k.wall(k.dmag)

            dbgw = make_dbgw(k, dbg_outs, 0) if dbg else None
            if lim > 0:
                emit_attn(k, 0, xr_hbm, wts, dbgw,
                          gw1x_hbm=win["l0a_gw1x"])

        if lim > 1:
            emit_ffn0(k, xr_hbm, win["f0_wgu"], win["f0_wd"])

        wsc = contextlib.ExitStack()
        with wsc:
            wts = {}
            for nm, wd in (("wq", 64 * 128), ("wk", 64 * 128),
                           ("wv", 64 * 128), ("wo", 64 * 128),
                           ("gw1x", 64 * 128), ("gw1s", 8 * 128)):
                wts[nm] = k.sb("w1_" + nm, [128, wd], BF16, wsc)
                k.hload(wts[nm][:], win["l1a_" + nm])
            wts["wb"] = k.sb("w1_wb", [128, 32], BF16, wsc)
            k.hload(wts["wb"][:], win["l1a_wb"])
            wts["gw2"] = k.sb("w1_gw2", [128, 8 * 4], BF16, wsc)
            k.hload(wts["gw2"][:], win["l1a_gw2"])
            wts["sel16"] = k.sb("w1_sel16", [128, 16], F32, wsc)
            k.hload(wts["sel16"][:], win["l1a_sel16"])
            for nm, wd in (("cq", 32), ("ck", 32), ("cv", 32),
                           ("fir0", NS * 5), ("fir1", NS * 64),
                           ("gb1", 8), ("onw", 2)):
                wts[nm] = k.sb("w1_" + nm, [128, wd], F32, wsc)
                k.hload(wts[nm][:], win["l1a_" + nm])
            for nm in ("gb2", "gcb", "gtemp"):
                wts[nm] = k.sb("w1_" + nm, [128, 1], F32, wsc)
                k.hload(wts[nm][:], win["l1a_" + nm])
            wts["gcbd"] = k.sb("w1_gcbd", [128, 1], F32, wsc)
            wts["grt"] = k.sb("w1_grt", [128, 1], F32, wsc)
            k.wall(k.dmag)
            DECAY = 1.0 - 1.0 / 3000.0
            k.dve.inc(k.V.tensor_scalar(wts["gcbd"][:], wts["gcb"][:],
                                        DECAY, None, OP.mult))
            k.act.inc(k.A.activation(wts["grt"][:], wts["gtemp"][:], AF.Exp,
                                     scale=-1.0))
            k.wall(k.dve, k.act)

            dbgw1 = make_dbgw(k, dbg_outs, 1) if dbg else None
            if lim > 2:
                emit_attn(k, 1, xr_hbm, wts, dbgw1, gw1x_hbm=None)

        mc = k.sb("moecst", [128, 128 + 8 + 16 * 4 * 32], F32)
        k.exp8f = k.sb("exp8f", [8, 128], F32)
        nc.sync.dma_start(out=mc[:], in_=win["moe_cst"]).then_inc(
            k.dma.sem, 16)
        nc.sync.dma_start(out=k.exp8f[:], in_=win["moe_exp8"]).then_inc(
            k.dma.sem, 16)
        k.dma.v += 32
        k.su16f = mc[:, 0:128]
        k.esum8f = mc[:, 128:136]
        k.collf = mc[:, 136:136 + 2048]
        k.wall(k.dma)

        if lim > 3:
            emit_moe(k, xr_hbm, {nm: win[nm] for nm in win}, y_out,
                     lim=lim, wbh=(wb13_hbm, wb2_hbm))
        if lim < 99:
            emit_dummy_y(k, y_out)


    return nc, dbg_outs


def make_dbgw(k, dbg_outs, var):
    nc = k.nc
    NSEG = k.T // SEG
    state = {}

    def w(sg, ls):
        keys = {"ln": "ln_seg", "qn": "qn", "kn": "kn", "br": "br",
                "probs": "probs", "oc": "oc", "ao": "ao", "beta": "beta",
                "stats": "stats", "u0": "u0a", "wfm": "wFM", "x0": "x0a",
                "ssb": "S_sb", "vc": "vc", "kTM": "kTM", "ocb": "ocb",
                "h1": "h1", "kb": "kb", "vb": "vb", "lg": "lg"}
        srcs = {kk: ls[vv] for kk, vv in keys.items() if vv in ls}
        for key, src in srcs.items():
            nm = "a%d_%s" % (var, key)
            if nm not in dbg_outs:
                continue
            if nm not in state:
                rows, wdt = src.shape
                hnd = nc.dram_tensor("dbg_" + nm, [rows, NSEG * wdt], F32,
                                     kind="ExternalOutput").ap()
                state[nm] = (hnd, rows, wdt)
            hnd, rows, wdt = state[nm]
            k.wall(k.dve, k.act, k.pe)
            k.dmag.inc(nc.gpsimd.dma_start(
                out=hnd[:, sg * wdt:(sg + 1) * wdt], in_=src[:]), 16)
            nc.gpsimd.wait_ge(k.dmag.sem, k.dmag.v)
            k.wall(k.dmag)
    return w


def emit_output(k, xr_hbm, y_out):
    nc = k.nc
    T = k.T
    sc = contextlib.ExitStack()
    with sc:
        out_tm = k.sb("out_tm", [128, D], F32, sc)
        xseg = k.sb("oxseg", [128, NS * 128], F32, sc)
        for tt in range(T // 128):
            k.w(nc.sync, k.dve)
            nc.sync.dma_start(
                out=xseg[:].rearrange("p (s t) -> p s t", s=NS),
                in_=xr_slab_ap(xr_hbm, tt * 128, 128)).then_inc(k.dma.sem, 16)
            k.dma.v += 16
            k.wall(k.dma)
            for s in range(NS):
                k.w(nc.tensor, k.dve)
                mm = nc.tensor.transpose(
                    k.w1[:, 0:128] if s % 2 == 0 else k.w2[:, 0:128],
                    xseg[:, s * 128:(s + 1) * 128], k.identf[:])
                k.pe.inc(mm)
                k.evac(out_tm[:, s * 128:(s + 1) * 128],
                       (k.w1 if s % 2 == 0 else k.w2)[:, 0:128])
            k.w(nc.sync, k.dve)
            nc.sync.dma_start(out=y_out[tt * 128:(tt + 1) * 128, :],
                              in_=out_tm[:]).then_inc(k.dma.sem, 16)
            k.dma.v += 16
        nc.sync.wait_ge(k.dma.sem, k.dma.v)



def _sel16():
    s = np.zeros((128, 16), np.float32)
    for h in range(4):
        for j in range(4):
            s[32 * h + j, 4 * h + j] = 1.0
    return s


def make_in_maps(inputs, T, n_cores):
    x = np.asarray(inputs["L0_attn_hidden_states"])
    shared = {
        "consts": host_consts(),
        "rowsel24": host_rowsel(24),
        "rowsel16": host_rowsel(16),
        "rowrep24": host_rowrep(24),
        "shv16": host_shv16(),
        "w_l0a_wq": pack_mm(inputs["L0_attn_q_proj_w"]),
        "w_l0a_wk": pack_mm(inputs["L0_attn_k_proj_w"]),
        "w_l0a_wv": pack_mm(inputs["L0_attn_v_proj_w"]),
        "w_l0a_wb": pack_thin(inputs["L0_attn_b_proj_w"]),
        "w_l0a_cq": pack_chan(inputs["L0_attn_q_conv_w"]),
        "w_l0a_ck": pack_chan(inputs["L0_attn_k_conv_w"]),
        "w_l0a_cv": pack_chan(inputs["L0_attn_v_conv_w"]),
        "w_l0a_fir0": pack_chan(inputs["L0_attn_fir_w3"]),
        "w_l0a_fir1": pack_chan(inputs["L0_attn_fir_w7"]),
        "w_l0a_fir2": pack_chan(inputs["L0_attn_fir_w15"]),
        "w_l0a_fir3": pack_chan(inputs["L0_attn_fir_w31"]),
        "w_l0a_gw1x": pack_mm(np.asarray(inputs["L0_attn_gate_w1"])[:, :1024], mo_major=True),
        "w_l0a_gw1s": np.ascontiguousarray(
            np.asarray(inputs["L0_attn_gate_w1"])[:, 1024:].T),
        "w_l0a_gw2": pack_thin(inputs["L0_attn_gate_w2"]),
        "w_l0a_gb1": np.ascontiguousarray(
            np.asarray(inputs["L0_attn_gate_b1"]).reshape(16, 128).T),
        "w_l0a_gb2": np.asarray(inputs["L0_attn_gate_b2"])
        .reshape(24, 1).astype(np.float32),
        "w_l0a_alpha": np.ascontiguousarray(
            np.tile(np.asarray(inputs["L0_attn_alpha_stat"]), 4)
            .reshape(24, 1)).astype(np.float32),
        "w_l0a_gtemp": np.ascontiguousarray(
            np.repeat(np.asarray(inputs["L0_attn_gate_log_temp"]), 6)
            .reshape(24, 1)).astype(np.float32),
        "w_l0a_onw": np.ascontiguousarray(
            np.asarray(inputs["L0_attn_o_norm_w"]).reshape(2, 128).T)
        .astype(np.float32),
        "w_l0a_wo": pack_mm(inputs["L0_attn_o_proj_w"]),
        "w_f0_wgu": np.concatenate(
            [pack_mm(np.asarray(inputs["L0_ffn_w_gate"]),
                     mo_major=True).reshape(128, 24, 1024),
             pack_mm(np.asarray(inputs["L0_ffn_w_up"]),
                     mo_major=True).reshape(128, 24, 1024)],
            axis=2).reshape(128, -1),
        "w_f0_wd": pack_mm(np.asarray(inputs["L0_ffn_w_down"]),
                           mo_major=True),
        "w_l1a_wq": pack_mm(inputs["L1_attn_Wq"]),
        "w_l1a_wk": pack_mm(inputs["L1_attn_Wk"]),
        "w_l1a_wv": pack_mm(inputs["L1_attn_Wv"]),
        "w_l1a_wb": pack_thin(inputs["L1_attn_Wb"]),
        "w_l1a_cq": pack_chan(inputs["L1_attn_conv_q_w"]),
        "w_l1a_ck": pack_chan(inputs["L1_attn_conv_k_w"]),
        "w_l1a_cv": pack_chan(inputs["L1_attn_conv_v_w"]),
        "w_l1a_fir0": pack_chan(
            np.asarray(inputs["L1_attn_fir_short_filt"]).reshape(1024, 5)),
        "w_l1a_fir1": pack_chan(
            np.asarray(inputs["L1_attn_fir_long_filt"]).reshape(1024, 64)),
        "w_l1a_gw1x": pack_mm(np.asarray(inputs["L1_attn_gate_W1"])[:, :1024]),
        "w_l1a_gw1s": _l1_gw1s(np.asarray(inputs["L1_attn_gate_W1"])),
        "w_l1a_gw2": pack_thin(inputs["L1_attn_gate_W2"]),
        "w_l1a_gb1": np.ascontiguousarray(
            np.asarray(inputs["L1_attn_gate_b1"]).reshape(8, 128).T),
        "w_l1a_gb2": np.ascontiguousarray(
            np.tile(np.asarray(inputs["L1_attn_gate_b2"]), 32)
            .reshape(128, 1)),
        "w_l1a_gcb": _l1_gcb(np.asarray(inputs["L1_attn_gate_copy_bias"])),
        "w_l1a_gtemp": np.ascontiguousarray(
            np.repeat(np.asarray(inputs["L1_attn_gate_log_temp"]), 32)
            .reshape(128, 1)),
        "w_l1a_onw": np.ascontiguousarray(
            np.asarray(inputs["L1_attn_o_norm_w"]).reshape(2, 128).T)
        .astype(np.float32),
        "w_l1a_wo": pack_mm(inputs["L1_attn_Wo"]),
        "w_l1a_sel16": _sel16(),
        "w_moe_wg": pack_thin(inputs["L1_ffn_wg"]),
        "w_moe_sw13": np.concatenate(
            [pack_mm(np.asarray(inputs["L1_ffn_sw1"]).T),
             pack_mm(np.asarray(inputs["L1_ffn_sw3"]).T)], axis=1),
        "w_moe_sw2": pack_rhs(inputs["L1_ffn_sw2"]),
        "w_moe_cst": host_moe_consts(),
        "w_moe_exp8": host_moe_exp8(),
    }
    shared = {kk: np.ascontiguousarray(
        vv if vv.dtype == np.int32 else vv.astype(np.float32))
        for kk, vv in shared.items()}
    w1e = np.asarray(inputs["L1_ffn_w1"])
    w2e = np.asarray(inputs["L1_ffn_w2"])
    w3e = np.asarray(inputs["L1_ffn_w3"])
    EPB = 32 // n_cores
    PT = T // 16
    maps = []
    for c in range(n_cores):
        m = dict(shared)
        m["x"] = np.ascontiguousarray(x[c, :T].T.astype(np.float32))
        m["w_moe_w13"] = np.ascontiguousarray(np.concatenate(
            [np.concatenate([pack_mm(w1e[c * EPB + j].T),
                             pack_mm(w3e[c * EPB + j].T)], axis=1)
             for j in range(EPB)], axis=1)).astype(np.float32)
        m["w_moe_w2"] = np.ascontiguousarray(np.concatenate(
            [pack_rhs(w2e[c * EPB + j]) for j in range(EPB)],
            axis=1)).astype(np.float32)
        m["w_moe_myeb"] = np.array([[c * EPB]], np.uint32)
        m["w_moe_myf0"] = np.array([[c * PT]], np.uint32)
        m["w_moe_rowb"] = np.ascontiguousarray(
            (c * T + np.arange(16)[None, :] * PT
             + np.arange(128)[:, None] + 1).astype(np.float32))
        maps.append(m)
    return maps


def kernel(**inputs):
    x = inputs["L0_attn_hidden_states"]
    B, T, _ = x.shape
    n_cores = 8
    nc, _ = build(T, n_cores)
    library_overlay.lower_extended_insts(nc)
    in_maps = make_in_maps(inputs, T, n_cores)
    res = run_bass_kernel_spmd(nc, in_maps, core_ids=list(range(n_cores)))
    return np.stack([r["y"] for r in res.results], axis=0)
